# version 07
# baseline (speedup 1.0000x reference)
import numpy as np
import concourse.bass as bass
import concourse.mybir as mybir
from concourse.bass_utils import run_bass_kernel_spmd
from concourse.alu_op_type import AluOpType as OP

F32 = mybir.dt.float32
BF16 = mybir.dt.bfloat16
AF = mybir.ActivationFunctionType
AX = mybir.AxisListType


class Cfg:
    def __init__(self, T=4096, DM=4096, DSL=1024, N=16, RK=128, E=16, R=64,
                 TB=256, eps=1e-5):
        self.T, self.DM, self.DSL, self.N, self.RK, self.E, self.R, self.TB = \
            T, DM, DSL, N, RK, E, R, TB
        self.eps = eps
        self.NCH = T // TB
        self.KCH = DM // 128
        self.DC = DSL // 128
        self.NTT = TB // 128
        self.AUX = 2 * N + R + E
        self.KH = (E * R) // 128


def build_program(cfg: Cfg):
    nc = bass.Bass("TRN2", target_bir_lowering=False)
    T, DM, DSL, N, RK, E, R, TB = (cfg.T, cfg.DM, cfg.DSL, cfg.N, cfg.RK,
                                   cfg.E, cfg.R, cfg.TB)
    NCH, KCH, DC, NTT, AUX, KH = (cfg.NCH, cfg.KCH, cfg.DC, cfg.NTT,
                                  cfg.AUX, cfg.KH)
    eps = cfg.eps
    TB1 = TB + 1
    NLH = 8 if DM >= 2048 else 2
    HDM = DM // NLH
    HKC = KCH // NLH
    NKQ = max(1, (DM // NLH) // 512)
    SBW = (DM // NLH) // NKQ
    NB = NLH * NKQ

    x_h = nc.dram_tensor("x", [T, DM], F32, kind="ExternalInput")
    bw_h = nc.dram_tensor("bw_t", [DM, DSL], F32, kind="ExternalInput")
    bm_h = nc.dram_tensor("bm_t", [E * R, DSL], F32, kind="ExternalInput")
    xpd_h = nc.dram_tensor("xpw_dt_t", [DM, RK], F32, kind="ExternalInput")
    aux_h = nc.dram_tensor("auxw_t", [DM, AUX], F32, kind="ExternalInput")
    dtw_h = nc.dram_tensor("dtw_t", [RK, DSL], F32, kind="ExternalInput")
    dtb_h = nc.dram_tensor("dtb", [DSL], F32, kind="ExternalInput")
    alog_h = nc.dram_tensor("alog", [DSL, N], F32, kind="ExternalInput")
    dvec_h = nc.dram_tensor("dvec", [DSL], F32, kind="ExternalInput")
    bb_h = nc.dram_tensor("bb", [DSL], F32, kind="ExternalInput")
    id_h = nc.dram_tensor("ident", [128, 128], F32, kind="ExternalInput")
    cst_h = nc.dram_tensor("consts", [128, 2], F32, kind="ExternalInput")
    out_h = nc.dram_tensor("out", [T, DSL], F32, kind="ExternalOutput")
    Q = (KCH + KH) * 128
    wscr_hi = nc.dram_tensor("wscr_hi", [DC, 128, Q], BF16)
    wscr_lo = nc.dram_tensor("wscr_lo", [DC, 128, Q], BF16)
    xscr_hi = nc.dram_tensor("xscr_hi", [KCH // 4, 128, 4 * RK], BF16)
    xscr_lo = nc.dram_tensor("xscr_lo", [KCH // 4, 128, 4 * RK], BF16)
    ascr_hi = nc.dram_tensor("ascr_hi", [KCH // 4, 128, 4 * AUX], BF16)
    ascr_lo = nc.dram_tensor("ascr_lo", [KCH // 4, 128, 4 * AUX], BF16)

    from contextlib import ExitStack
    with ExitStack() as _stk:
        bnbuf = _stk.enter_context(nc.sbuf_tensor("bnbuf", [128, 6 * NB], F32))
        mv = _stk.enter_context(nc.sbuf_tensor("mv", [128, 2], F32))
        lnv = _stk.enter_context(nc.sbuf_tensor("lnv", [128, 1], F32))
        mpak = _stk.enter_context(nc.sbuf_tensor("mpak", [128, 2], F32))
        sig_sv = _stk.enter_context(nc.sbuf_tensor("sig_sv", [128, 2], F32))
        xnh = _stk.enter_context(nc.sbuf_tensor("xnh", [128, HDM], F32))
        hih = _stk.enter_context(nc.sbuf_tensor("hih", [128, HDM], BF16))
        loh = _stk.enter_context(nc.sbuf_tensor("loh", [128, HDM], BF16))
        xnA_hi = _stk.enter_context(nc.sbuf_tensor("xnA_hi", [128, KCH * TB], BF16))
        xnA_lo = _stk.enter_context(nc.sbuf_tensor("xnA_lo", [128, KCH * TB], BF16))
        xnB_hi = _stk.enter_context(nc.sbuf_tensor("xnB_hi", [128, (KCH // 2) * TB], BF16))
        xnB_lo = _stk.enter_context(nc.sbuf_tensor("xnB_lo", [128, (KCH // 2) * TB], BF16))
        wbuf_hi = _stk.enter_context(nc.sbuf_tensor("wbuf_hi", [128, 2 * Q], BF16))
        wbuf_lo = _stk.enter_context(nc.sbuf_tensor("wbuf_lo", [128, 2 * Q], BF16))
        xbuf_hi = _stk.enter_context(nc.sbuf_tensor("xbuf_hi", [128, 4 * RK], BF16))
        xbuf_lo = _stk.enter_context(nc.sbuf_tensor("xbuf_lo", [128, 4 * RK], BF16))
        abuf_hi = _stk.enter_context(nc.sbuf_tensor("abuf_hi", [128, 4 * AUX], BF16))
        abuf_lo = _stk.enter_context(nc.sbuf_tensor("abuf_lo", [128, 4 * AUX], BF16))
        dtraw = _stk.enter_context(nc.sbuf_tensor("dtraw", [128, DC * TB], F32))
        dtw_sb = _stk.enter_context(nc.sbuf_tensor("dtw_sb", [128, DSL], F32))
        a_sb = _stk.enter_context(nc.sbuf_tensor("a_sb", [128, DC * N], F32))
        dtb_sb = _stk.enter_context(nc.sbuf_tensor("dtb_sb", [128, DC], F32))
        dvec_sb = _stk.enter_context(nc.sbuf_tensor("dvec_sb", [128, DC], F32))
        bb_sb = _stk.enter_context(nc.sbuf_tensor("bb_sb", [128, DC], F32))
        id_sb = _stk.enter_context(nc.sbuf_tensor("id_sb", [128, 128], F32))
        cst_sb = _stk.enter_context(nc.sbuf_tensor("cst_sb", [128, 2], F32))
        dtin_sb = _stk.enter_context(nc.sbuf_tensor("dtin_sb", [128, TB], F32))
        aux_sb = _stk.enter_context(nc.sbuf_tensor("aux_sb", [AUX, TB], F32))
        lt = _stk.enter_context(nc.sbuf_tensor("lt", [128, E], F32))
        r1 = _stk.enter_context(nc.sbuf_tensor("r1", [128, 1], F32))
        r2 = _stk.enter_context(nc.sbuf_tensor("r2", [128, 1], F32))
        r3 = _stk.enter_context(nc.sbuf_tensor("r3", [128, 1], F32))
        exb = _stk.enter_context(nc.sbuf_tensor("exb", [128, E], F32))
        pb = _stk.enter_context(nc.sbuf_tensor("pb", [128, E], F32))
        msk = _stk.enter_context(nc.sbuf_tensor("msk", [128, E], F32))
        cmb_tm = _stk.enter_context(nc.sbuf_tensor("cmb_tm", [128, E], F32))
        cmb_sb = _stk.enter_context(nc.sbuf_tensor("cmb_sb", [E, TB], F32))
        cmb_rep = _stk.enter_context(nc.sbuf_tensor("cmb_rep", [128, TB], F32))
        rh2 = _stk.enter_context(nc.sbuf_tensor("rh2", [128, TB], F32))
        h2_hi = _stk.enter_context(nc.sbuf_tensor("h2_hi", [128, KH * TB], BF16))
        h2_lo = _stk.enter_context(nc.sbuf_tensor("h2_lo", [128, KH * TB], BF16))
        g_sb = _stk.enter_context(nc.sbuf_tensor("g_sb", [128, DC * TB], F32))
        msf = _stk.enter_context(nc.sbuf_tensor("msf", [2, TB], F32))
        m_rep = _stk.enter_context(nc.sbuf_tensor("m_rep", [128, TB], F32))
        r_rep = _stk.enter_context(nc.sbuf_tensor("r_rep", [128, TB], F32))
        cor_t = _stk.enter_context(nc.sbuf_tensor("cor_t", [128, TB], F32))
        ws_b = _stk.enter_context(nc.sbuf_tensor("ws_b", [128, DC], F32))
        ws_d = _stk.enter_context(nc.sbuf_tensor("ws_d", [128, 1], F32))
        ws_a = _stk.enter_context(nc.sbuf_tensor("ws_a", [128, 1], F32))
        b_rep = _stk.enter_context(nc.sbuf_tensor("b_rep", [128, N * TB], F32))
        c_rep = _stk.enter_context(nc.sbuf_tensor("c_rep", [128, N * TB], F32))
        dt_dc = _stk.enter_context(nc.sbuf_tensor("dt_dc", [128, TB], F32))
        z_dc = _stk.enter_context(nc.sbuf_tensor("z_dc", [128, TB], F32))
        xnsl = _stk.enter_context(nc.sbuf_tensor("xnsl", [128, TB], F32))
        da = _stk.enter_context(nc.sbuf_tensor("da", [128, N * TB1], F32))
        dbx = _stk.enter_context(nc.sbuf_tensor("dbx", [128, N * TB1], F32))
        state = _stk.enter_context(nc.sbuf_tensor("state", [128, DC * N], F32))
        p_dtin = _stk.enter_context(nc.psum_tensor("p_dtin", [128, 512], F32))
        p_aux = _stk.enter_context(nc.psum_tensor("p_aux", [128, 512], F32))
        p_ffn0 = _stk.enter_context(nc.psum_tensor("p_ffn0", [128, 512], F32))
        p_ffn1 = _stk.enter_context(nc.psum_tensor("p_ffn1", [128, 512], F32))
        p_dt0 = _stk.enter_context(nc.psum_tensor("p_dt0", [128, 512], F32))
        p_dt1 = _stk.enter_context(nc.psum_tensor("p_dt1", [128, 512], F32))
        p_tr = _stk.enter_context(nc.psum_tensor("p_tr", [128, 512], F32))
        p_tr2 = _stk.enter_context(nc.psum_tensor("p_tr2", [128, 512], F32))
        h_ld = _stk.enter_context(nc.semaphore("s_ld"))
        h_tp = _stk.enter_context(nc.semaphore("s_tp"))
        h_rp = _stk.enter_context(nc.semaphore("s_rp"))
        h_st = _stk.enter_context(nc.semaphore("s_st"))
        h_pe = _stk.enter_context(nc.semaphore("s_pe"))
        h_ve = _stk.enter_context(nc.semaphore("s_ve"))
        h_ac = _stk.enter_context(nc.semaphore("s_ac"))
        h_gp = _stk.enter_context(nc.semaphore("s_gp"))
        h_ws = _stk.enter_context(nc.semaphore("s_ws"))
        h_x = _stk.enter_context(nc.semaphore("s_x"))
        h_w0 = _stk.enter_context(nc.semaphore("s_w0"))
        h_w1 = _stk.enter_context(nc.semaphore("s_w1"))
        h_rb = _stk.enter_context(nc.semaphore("s_rb"))
        h_rc = _stk.enter_context(nc.semaphore("s_rc"))
        h_rm = _stk.enter_context(nc.semaphore("s_rm"))
        h_rcb = _stk.enter_context(nc.semaphore("s_rcb"))
        B = _stk.enter_context(nc.Block())
        ld = ve = ac = pe = gp = tp = rp = st = ws = 0
        xc = w0 = w1 = rb = rc = rm = rcb = 0
        h_w = [h_w0, h_w1]
        p_ffn = [p_ffn0, p_ffn1]
        p_dt = [p_dt0, p_dt1]

        @B.sync
        def _(e):
            e.dma_start(out=dtw_sb[:, :], in_=dtw_h.ap()[:, :]).then_inc(h_ld, 16)
            for (sb, hb) in ((dtb_sb, dtb_h), (dvec_sb, dvec_h), (bb_sb, bb_h)):
                e.dma_start(out=bass.AP(sb, 0, [[DC, 128], [1, DC]]),
                            in_=bass.AP(hb, 0, [[1, 128], [128, DC]]),
                            allow_slow_non_contiguous=True
                            ).then_inc(h_ld, 16)
            e.dma_start(out=bass.AP(a_sb, 0, [[DC * N, 128], [N, DC], [1, N]]),
                        in_=bass.AP(alog_h, 0, [[N, 128], [128 * N, DC], [1, N]])
                        ).then_inc(h_ld, 16)
            e.dma_start(out=id_sb[:, :], in_=id_h.ap()[:, :]).then_inc(h_ld, 16)
            e.dma_start(out=cst_sb[:, :], in_=cst_h.ap()[:, :]).then_inc(h_ld, 16)
        ld += 7 * 16
        nc.const_aps.register(F32, eps, cst_sb[:, 0:1])
        nc.const_aps.register(F32, 1.0, cst_sb[:, 1:2])
        ld_consts = ld

        @B.scalar
        def _(e, w=ld):
            e.wait_ge(h_ld, w)
            e.activation(out=a_sb[:, :], in_=a_sb[:, :], func=AF.Exp
                         ).then_inc(h_ac, 1)
        ac += 1

        @B.vector
        def _(e, w=ac):
            e.wait_ge(h_ac, w)
            e.tensor_scalar(a_sb[:, :], a_sb[:, :], -1.0, None, op0=OP.mult)
            e.memset(state[:, :], 0.0)
            e.drain()
            e.nop().then_inc(h_ve, 1)
        ve += 1
        a_ready = ve

        prev_sinks = []
        prev_lo = 0

        o_tm = cmb_rep
        CW = DSL // 2
        wsplit, wsp_hi, wsp_lo = xnh, hih, loh

        ws_started = [False]

        def split_stream(src_h, n_tiles, width, sink, ws_map=None,
                         ws_last=False):
            nonlocal ld, ve, ac, pe, prev_sinks, prev_lo
            ncb = (width + CW - 1) // CW
            for kt in range(n_tiles):
                for cb in range(ncb):
                    cw = min(CW, width - cb * CW)

                    @B.sync
                    def _(e, kt=kt, cb=cb, cw=cw, g=list(prev_sinks), pl2=ld):
                        e.wait_ge(h_ld, pl2)
                        for (sh, vv) in g:
                            e.wait_ge(sh, vv)
                        e.dma_start(out=wsplit[:, :cw],
                                    in_=src_h.ap()[kt * 128:(kt + 1) * 128,
                                                   cb * CW:cb * CW + cw]
                                    ).then_inc(h_ld, 16)
                    ld += 16

                    @B.scalar
                    def _(e, w=ld, pl=prev_lo, cw=cw):
                        e.wait_ge(h_ld, w)
                        e.wait_ge(h_ve, pl)
                        e.drain()
                        e.copy(out=wsp_hi[:, :cw], in_=wsplit[:, :cw]
                               ).then_inc(h_ac, 1)
                    ac += 1
                    hi_mark = ac

                    @B.vector
                    def _(e, w=ld, hm=hi_mark, cw=cw):
                        e.wait_ge(h_ld, w)
                        e.wait_ge(h_ac, hm)
                        e.drain()
                        e.tensor_tensor(wsp_lo[:, :cw], wsplit[:, :cw],
                                        wsp_hi[:, :cw], op=OP.subtract
                                        ).then_inc(h_ve, 1)
                    ve += 1
                    lo_mark = ve
                    prev_sinks = sink(kt, cb, cw, hi_mark, lo_mark)
                    if ws_map is not None:
                        nsub = (cw + 127) // 128

                        @B.tensor
                        def _(e, kt=kt, cb=cb, cw=cw, nsub=nsub, w=ld,
                              ar=a_ready):
                            e.wait_ge(h_ld, w)
                            e.wait_ge(h_ve, ar)
                            for sub in range(nsub):
                                sw = 128
                                col = ws_map(cb, sub)
                                st_ = not ws_started[0]
                                ws_started[0] = True
                                sp_ = (ws_last and kt == n_tiles - 1
                                       and sub == nsub - 1)
                                mm = e.matmul(p_tr2[0:sw, col:col + 1],
                                              wsplit[:, sub * 128:sub * 128 + sw],
                                              cst_sb[:, 1:2],
                                              start=st_, stop=sp_)
                                if sub == nsub - 1:
                                    mm.then_inc(h_pe, 1)
                        pe += 1
                        prev_sinks = list(prev_sinks) + [(h_pe, pe)]
                    prev_lo = lo_mark

        def scr_sink(qoff):
            def sink(kt, cb, cw, hm, lm):
                nonlocal ws
                marks = []
                ndg = cw // 128
                for (sb, scr, sh, sv) in ((wsp_hi, wscr_hi, h_ac, hm),
                                          (wsp_lo, wscr_lo, h_ve, lm)):
                    @B.sync
                    def _(e, sb=sb, scr=scr, sh=sh, sv=sv, kt=kt, cb=cb,
                          ndg=ndg, pw=ws):
                        e.wait_ge(h_ws, pw)
                        e.wait_ge(sh, sv)
                        e.dma_start(
                            out=bass.AP(scr, (qoff + kt) * 128
                                        + cb * (CW // 128) * 128 * Q,
                                        [[Q, 128], [128 * Q, ndg], [1, 128]]),
                            in_=bass.AP(sb, 0, [[HDM, 128], [128, ndg], [1, 128]])
                        ).then_inc(h_ws, 16)
                    ws += 16
                    marks.append((h_ws, ws))
                return marks
            return sink

        def gscr_sink(scr_hi, scr_lo, W):
            def sink(kt, cb, cw, hm, lm):
                nonlocal ws
                marks = []
                g, s = kt // 4, kt % 4
                for (sb, scr, sh, sv) in ((wsp_hi, scr_hi, h_ac, hm),
                                          (wsp_lo, scr_lo, h_ve, lm)):
                    @B.sync
                    def _(e, sb=sb, scr=scr, sh=sh, sv=sv, g=g, s=s, pw=ws):
                        e.wait_ge(h_ws, pw)
                        e.wait_ge(sh, sv)
                        e.dma_start(
                            out=bass.AP(scr, g * 128 * 4 * W + s * W,
                                        [[4 * W, 128], [1, W]]),
                            in_=bass.AP(sb, 0, [[HDM, 128], [1, W]])
                        ).then_inc(h_ws, 16)
                    ws += 16
                    marks.append((h_ws, ws))
                return marks
            return sink

        def resident_sink(hi_t, lo_t, width):
            def sink(kt, cb, cw, hm, lm):
                nonlocal ve, ac
                marks = []
                off = kt * width + cb * CW

                @B.scalar
                def _(e, off=off, cw=cw):
                    e.drain()
                    e.copy(out=hi_t[:, off:off + cw],
                           in_=wsp_hi[:, :cw]).then_inc(h_ac, 1)
                ac += 1
                marks.append((h_ac, ac))

                @B.vector
                def _(e, off=off, cw=cw):
                    e.drain()
                    e.tensor_copy(lo_t[:, off:off + cw],
                                  wsp_lo[:, :cw]).then_inc(h_ve, 1)
                ve += 1
                marks.append((h_ve, ve))
                return marks
            return sink

        split_stream(bw_h, KCH, DSL, scr_sink(0),
                     ws_map=lambda cb, sub: cb * (CW // 128) + sub)
        split_stream(bm_h, KH, DSL, scr_sink(KCH))
        split_stream(xpd_h, KCH, RK, gscr_sink(xscr_hi, xscr_lo, RK),
                     ws_map=lambda cb, sub: DC)
        split_stream(aux_h, KCH, AUX, gscr_sink(ascr_hi, ascr_lo, AUX),
                     ws_map=lambda cb, sub: DC + 1, ws_last=True)

        @B.vector
        def _(e, w=pe):
            e.wait_ge(h_pe, w)
            e.drain()
            e.tensor_copy(ws_b[:, :], p_tr2[0:128, 0:DC])
            e.tensor_copy(ws_d[:, :], p_tr2[0:128, DC:DC + 1])
            e.tensor_copy(ws_a[:, :], p_tr2[0:128, DC + 1:DC + 2])
            e.drain()
            e.nop().then_inc(h_ve, 1)
        ve += 1
        ws_evac = ve

        @B.vector
        def _(e):
            e.drain()

        @B.scalar
        def _(e):
            e.drain()

        mk_xn_free_pe = 0
        prologue_ws = ws
        mk_wbuf_free = [0, 0]
        mk_ffn_evac = [0, 0]
        mk_dt_exp = [0, 0]
        mk_dtin_evac = 0
        mk_xnh_free = ve
        mk_half_free = 0
        mk_prev_hi = ac
        mk_rstd_prev = 0
        mk_mpak_pe = 0
        mk_msf_free = 0
        mk_brep_free = 0
        mk_crep_free = 0
        mk_rh2_free = 0
        mk_cmbsb_free = 0
        mk_aux_pe_free = 0
        mk_aux_rp_free = 0
        mk_aux_rb_free = 0
        mk_gsb_free = 0
        mk_mrep_free = 0
        mk_h2_free = 0
        mk_h2f_free = 0
        mk_odc_pe = 0
        mk_ptr2_evac = ws_evac
        mk_otm_st = 0
        mk_ptr_evac = 0
        mk_lt_free = 0
        mk_scan_prev = 0
        mk_hc_prev = 0
        mk_tree_prev = 0
        mk_z_free = 0
        mk_dt_z = 0
        mk_da_free = 0
        mk_aux_evac = 0
        dt_prev_ev = 0
        h2hi_prev = 0

        out_q = []
        mk_xnA_free = [0, 0]
        mk_xnA_scan = [0, 0]
        mk_xnB_free = 0
        mk_gout = [0] * DC
        mk_xabuf_free = 0
        dtev = [0] * DC
        dtsp = [0] * DC
        da_mark = [0] * DC
        z_mark = [0] * DC
        NG = KCH // 4
        HALF = KCH // 2

        def xn_hi_sl(kch, par):
            if kch < HALF:
                return xnA_hi[:, (par * HALF + kch) * TB:(par * HALF + kch + 1) * TB]
            return xnB_hi[:, (kch - HALF) * TB:(kch - HALF + 1) * TB]

        def xn_lo_sl(kch, par):
            if kch < HALF:
                return xnA_lo[:, (par * HALF + kch) * TB:(par * HALF + kch + 1) * TB]
            return xnB_lo[:, (kch - HALF) * TB:(kch - HALF + 1) * TB]

        for c in range(NCH):
            t0 = c * TB
            par = c % 2

            msf_marks = []
            for tt in range(NTT):
                for q in range(NLH):
                    @B.sync
                    def _(e, tt=tt, q=q, w=mk_xnh_free, wh=mk_prev_hi,
                          px=xc, c_=c, pws=prologue_ws):
                        e.wait_ge(h_x, px)
                        e.wait_ge(h_ve, w)
                        e.wait_ge(h_ac, wh)
                        if c_ == 0 and tt == 0 and q == 0:
                            e.wait_ge(h_ws, pws)
                        e.dma_start(
                            out=xnh[:, :],
                            in_=x_h.ap()[t0 + tt * 128:t0 + tt * 128 + 128,
                                         q * HDM:(q + 1) * HDM]
                            ).then_inc(h_x, 16)
                    xc += 16
                    xld = xc

                    @B.vector
                    def _(e, q=q, w=xld, pr=mk_rstd_prev, pm=mk_mpak_pe):
                        e.wait_ge(h_x, w)
                        if q == 0:
                            e.wait_ge(h_ac, pr)
                            e.wait_ge(h_pe, pm)
                        e.drain()
                        for k in range(NKQ):
                            e.bn_stats(
                                bnbuf[:, 6 * (q * NKQ + k):
                                      6 * (q * NKQ + k) + 6],
                                xnh[:, SBW * k:SBW * (k + 1)])
                        e.drain()
                        e.nop().then_inc(h_ve, 1)
                    ve += 1

                    @B.scalar
                    def _(e, q=q, w=xld, fr=mk_half_free):
                        e.wait_ge(h_x, w)
                        e.wait_ge(h_tp, fr)
                        e.drain()
                        e.copy(out=hih[:, :], in_=xnh[:, :]).then_inc(h_ac, 1)
                    ac += 1
                    hi_done = ac
                    mk_prev_hi = hi_done

                    @B.vector
                    def _(e, w=hi_done, fr=mk_half_free):
                        e.wait_ge(h_ac, w)
                        e.wait_ge(h_tp, fr)
                        e.drain()
                        e.tensor_tensor(loh[:, :], xnh[:, :], hih[:, :],
                                        op=OP.subtract).then_inc(h_ve, 1)
                    ve += 1
                    lo_done = ve
                    mk_xnh_free = ve

                    to_A = q < NLH // 2
                    if to_A:
                        dsts = (xnA_hi, xnA_lo)
                        doff = (par * HALF + q * HKC) * TB + tt * 128
                        dpitch = KCH * TB
                        pgate = mk_xnA_free[par]
                        vgate = mk_xnA_scan[par]
                    else:
                        dsts = (xnB_hi, xnB_lo)
                        doff = ((q - NLH // 2) * HKC) * TB + tt * 128
                        dpitch = HALF * TB
                        pgate = mk_xnB_free
                        vgate = 0

                    @B.sync
                    def _(e, tt=tt, q=q, hd=hi_done, ld_=lo_done, pt=tp,
                          dsts=dsts, doff=doff, dpi=dpitch, pg=pgate,
                          vg=vgate):
                        e.wait_ge(h_tp, pt)
                        e.wait_ge(h_ac, hd)
                        e.wait_ge(h_ve, ld_)
                        e.wait_ge(h_pe, pg)
                        if vg:
                            e.wait_ge(h_ve, vg)
                        e.dma_start_transpose(
                            out=bass.AP(dsts[0], doff,
                                        [[dpi, 128], [TB, HKC], [1, 128]]),
                            in_=hih[:, :]).then_inc(h_tp, 16)
                        e.dma_start_transpose(
                            out=bass.AP(dsts[1], doff,
                                        [[dpi, 128], [TB, HKC], [1, 128]]),
                            in_=loh[:, :]).then_inc(h_tp, 16)
                    tp += 32
                    mk_half_free = tp

                @B.vector
                def _(e, pm=mk_mpak_pe):
                    e.wait_ge(h_pe, pm)
                    e.drain()
                    e.bn_aggr(mv[:, :], bnbuf[:, :])
                    e.drain()
                    e.tensor_copy(mpak[:, 0:1], mv[:, 0:1])
                    e.drain()
                    e.nop().then_inc(h_ve, 1)
                ve += 1
                bn_done = ve

                @B.scalar
                def _(e, tt=tt, w=bn_done):
                    e.wait_ge(h_ve, w)
                    e.wait_ge(h_ld, ld_consts)
                    e.drain()
                    e.activation(out=lnv[:, :], in_=mv[:, 1:2], func=AF.Ln,
                                 bias=eps)
                    e.drain()
                    e.activation(out=mpak[:, 1:2], in_=lnv[:, :], func=AF.Exp,
                                 scale=-0.5)
                    e.activation(out=sig_sv[:, tt:tt + 1], in_=lnv[:, :],
                                 func=AF.Exp, scale=0.5)
                    e.drain()
                    e.nop().then_inc(h_ac, 1)
                ac += 1
                rstd_done = ac
                mk_rstd_prev = rstd_done

                @B.tensor
                def _(e, tt=tt, w=rstd_done, pf=mk_ptr_evac):
                    e.wait_ge(h_ac, w)
                    e.wait_ge(h_ve, pf)
                    e.transpose(p_tr[0:2, 0:128], mpak[:, :], id_sb[:, :]
                                ).then_inc(h_pe, 1)
                pe += 1
                mk_mpak_pe = pe

                @B.vector
                def _(e, tt=tt, w=pe, mf=mk_msf_free):
                    e.wait_ge(h_pe, w)
                    e.wait_ge(h_rm, mf)
                    e.drain()
                    e.tensor_copy(msf[:, tt * 128:(tt + 1) * 128],
                                  p_tr[0:2, 0:128]).then_inc(h_ve, 1)
                ve += 1
                mk_ptr_evac = ve
                msf_marks.append(ve)

            tp_done = tp

            @B.scalar
            def _(e, w=msf_marks[-1], fr=mk_mrep_free, pm2=rm):
                e.wait_ge(h_rm, pm2)
                e.wait_ge(h_ve, w)
                e.wait_ge(h_ve, fr)
                e.dma_start(out=m_rep[:, :],
                            in_=bass.AP(msf, 0, [[TB, 1], [0, 128], [1, TB]])
                            ).then_inc(h_rm, 16)
                e.dma_start(out=r_rep[:, :],
                            in_=bass.AP(msf, TB, [[TB, 1], [0, 128], [1, TB]])
                            ).then_inc(h_rm, 16)
            rm += 32
            mrep_done = rm
            mk_msf_free = rm

            for (odc, ovm) in out_q:
                for th in range(NTT):
                    @B.tensor
                    def _(e, dc=odc, th=th, w=ovm, pf=mk_ptr2_evac):
                        e.wait_ge(h_ve, max(w, pf))
                        e.transpose(p_tr2[0:128, 0:128],
                                    g_sb[:, dc * TB + th * 128:
                                         dc * TB + (th + 1) * 128],
                                    id_sb[:, :]).then_inc(h_pe, 1)
                    pe += 1
                    otr = pe

                    @B.vector
                    def _(e, dc=odc, th=th, w=otr, sf=mk_otm_st,
                          hh=h2hi_prev):
                        e.wait_ge(h_pe, w)
                        e.wait_ge(h_st, sf)
                        e.wait_ge(h_ac, hh)
                        e.drain()
                        e.tensor_copy(o_tm[:, 0:128],
                                      p_tr2[0:128, 0:128]).then_inc(h_ve, 1)
                    ve += 1
                    mk_ptr2_evac = ve
                    otm_done = ve

                    @B.scalar
                    def _(e, dc=odc, th=th, w=otm_done, ps=st, tprev=t0 - TB):
                        e.wait_ge(h_st, ps)
                        e.wait_ge(h_ve, w)
                        e.dma_start(
                            out=bass.AP(out_h,
                                        (tprev + th * 128) * DSL + dc * 128,
                                        [[DSL, 128], [1, 128]]),
                            in_=o_tm[:, 0:128]).then_inc(h_st, 16)
                    st += 16
                    mk_otm_st = st
                mk_gout[odc] = pe
            out_q = []

            xp_first = True
            for gidx in range(NG):
                @B.sync
                def _(e, gidx=gidx, pl=ld, fr=mk_xabuf_free):
                    e.wait_ge(h_ld, pl)
                    e.wait_ge(h_pe, fr)
                    e.dma_start(out=xbuf_hi[:, :], in_=xscr_hi.ap()[gidx, :, :]
                                ).then_inc(h_ld, 16)
                    e.dma_start(out=xbuf_lo[:, :], in_=xscr_lo.ap()[gidx, :, :]
                                ).then_inc(h_ld, 16)
                    e.dma_start(out=abuf_hi[:, :], in_=ascr_hi.ap()[gidx, :, :]
                                ).then_inc(h_ld, 16)
                    e.dma_start(out=abuf_lo[:, :], in_=ascr_lo.ap()[gidx, :, :]
                                ).then_inc(h_ld, 16)
                ld += 64
                gld = ld

                @B.tensor
                def _(e, gidx=gidx, w=gld, tp_=tp_done, dtf=mk_dtin_evac,
                      axf=mk_aux_evac, par=par):
                    e.wait_ge(h_ld, w)
                    if gidx == 0:
                        e.wait_ge(h_tp, tp_)
                        e.wait_ge(h_ve, max(dtf, axf))
                    for i in range(4):
                        kch = gidx * 4 + i
                        first = kch == 0
                        last = kch == KCH - 1
                        rh_ = xn_hi_sl(kch, par)
                        rl_ = xn_lo_sl(kch, par)
                        wh = xbuf_hi[:, i * RK:(i + 1) * RK]
                        wl = xbuf_lo[:, i * RK:(i + 1) * RK]
                        e.matmul(p_dtin[0:RK, 0:TB], wh, rh_, start=first,
                                 stop=False)
                        e.matmul(p_dtin[0:RK, 0:TB], wh, rl_, start=False,
                                 stop=False)
                        e.matmul(p_dtin[0:RK, 0:TB], wl, rh_, start=False,
                                 stop=last)
                        wh = abuf_hi[:, i * AUX:(i + 1) * AUX]
                        wl = abuf_lo[:, i * AUX:(i + 1) * AUX]
                        e.matmul(p_aux[0:AUX, 0:TB], wh, rh_, start=first,
                                 stop=False)
                        e.matmul(p_aux[0:AUX, 0:TB], wh, rl_, start=False,
                                 stop=False)
                        mm = e.matmul(p_aux[0:AUX, 0:TB], wl, rh_, start=False,
                                      stop=last)
                    mm.then_inc(h_pe, 1)
                pe += 1
                mk_xabuf_free = pe
            xp1_done = pe
            aux_done = pe

            @B.vector
            def _(e, w=xp1_done, mr=mrep_done):
                e.wait_ge(h_pe, w)
                e.wait_ge(h_rm, mr)
                e.drain()
                e.tensor_scalar(cor_t[:, :], m_rep[:, :], ws_d[:, 0:1], None,
                                op0=OP.mult)
                e.drain()
                e.tensor_tensor(dtin_sb[:, :], p_dtin[0:RK, 0:TB],
                                cor_t[:, :], op=OP.subtract)
                e.drain()
                e.tensor_tensor(dtin_sb[:, :], dtin_sb[:, :], r_rep[:, :],
                                op=OP.mult).then_inc(h_ve, 1)
            ve += 1
            mk_dtin_evac = ve

            @B.vector
            def _(e, w=aux_done, f1=mk_aux_pe_free, f2=mk_aux_rp_free,
                  f3=mk_aux_rb_free):
                e.wait_ge(h_pe, w)
                e.wait_ge(h_pe, f1)
                e.wait_ge(h_rc, f2)
                e.wait_ge(h_rb, f3)
                e.wait_ge(h_rm, mrep_done)
                e.drain()
                e.tensor_scalar(cor_t[0:AUX, :], m_rep[0:AUX, :],
                                ws_a[0:AUX, 0:1], None, op0=OP.mult)
                e.drain()
                e.tensor_tensor(aux_sb[:, :], p_aux[0:AUX, 0:TB],
                                cor_t[0:AUX, :], op=OP.subtract)
                e.drain()
                e.tensor_tensor(aux_sb[:, :], aux_sb[:, :], r_rep[0:AUX, :],
                                op=OP.mult).then_inc(h_ve, 1)
            ve += 1
            mk_aux_evac = ve

            for dc in range(DC):
                dpar = dc % 2

                @B.tensor
                def _(e, dc=dc, dpar=dpar, w=mk_dtin_evac,
                      fe=dtev[dc - 2] if dc >= 2 else dt_prev_ev):
                    e.wait_ge(h_ve, max(w, fe))
                    e.matmul(p_dt[dpar][0:128, 0:TB],
                             dtw_sb[:, dc * 128:(dc + 1) * 128],
                             dtin_sb[:, :], start=True, stop=True
                             ).then_inc(h_pe, 1)
                pe += 1
                dtmm = pe

                @B.vector
                def _(e, dc=dc, dpar=dpar, w=dtmm, fa=da_mark[dc],
                      fz=z_mark[dc]):
                    e.wait_ge(h_pe, w)
                    e.wait_ge(h_ac, fa)
                    e.wait_ge(h_ve, fz)
                    e.drain()
                    e.tensor_copy(dtraw[:, dc * TB:(dc + 1) * TB],
                                  p_dt[dpar][0:128, 0:TB]).then_inc(h_ve, 1)
                ve += 1
                dtev[dc] = ve

                @B.scalar
                def _(e, dc=dc, w=dtev[dc]):
                    e.wait_ge(h_ve, w)
                    e.drain()
                    e.activation(out=dtraw[:, dc * TB:(dc + 1) * TB],
                                 in_=dtraw[:, dc * TB:(dc + 1) * TB],
                                 func=AF.Exp, bias=dtb_sb[:, dc:dc + 1])
                    e.drain()
                    e.activation(out=dtraw[:, dc * TB:(dc + 1) * TB],
                                 in_=dtraw[:, dc * TB:(dc + 1) * TB],
                                 func=AF.Ln, bias=1.0).then_inc(h_ac, 1)
                ac += 1
                dtsp[dc] = ac
            dt_prev_ev = dtev[DC - 2]

            @B.sync
            def _(e, w=mk_aux_evac, fb=mk_brep_free, fr_=mk_rh2_free, pb_=rb):
                e.wait_ge(h_rb, pb_)
                e.wait_ge(h_ve, w)
                e.wait_ge(h_gp, fb)
                e.wait_ge(h_ve, fr_)
                for n in range(N):
                    e.dma_start(out=b_rep[:, n * TB:(n + 1) * TB],
                                in_=bass.AP(aux_sb, (E + n) * TB,
                                            [[TB, 1], [0, 128], [1, TB]])
                                ).then_inc(h_rb, 16)
                for half in range(2):
                    e.dma_start(out=rh2[half * R:(half + 1) * R, :],
                                in_=bass.AP(aux_sb, (E + 2 * N) * TB,
                                            [[TB, R], [1, TB]])
                                ).then_inc(h_rb, 16)
            rb += 16 * (N + 2)
            rh2_done = rb

            @B.scalar
            def _(e, w=mk_aux_evac, fc=mk_crep_free, pc=rc):
                e.wait_ge(h_rc, pc)
                e.wait_ge(h_ve, w)
                e.wait_ge(h_gp, fc)
                for n in range(N):
                    e.dma_start(out=c_rep[:, n * TB:(n + 1) * TB],
                                in_=bass.AP(aux_sb, (E + N + n) * TB,
                                            [[TB, 1], [0, 128], [1, TB]])
                                ).then_inc(h_rc, 16)
            rc += 16 * N
            crep_done = rc
            brep_done = rh2_done
            mk_aux_rp_free = rc
            mk_aux_rb_free = rb

            cmb_done_mark = 0
            for tt in range(NTT):
                @B.tensor
                def _(e, tt=tt, w=mk_aux_evac, pf=mk_ptr_evac):
                    e.wait_ge(h_ve, max(w, pf))
                    e.transpose(p_tr[0:128, 0:E],
                                aux_sb[0:E, tt * 128:(tt + 1) * 128],
                                id_sb[0:E, 0:E]).then_inc(h_pe, 1)
                pe += 1
                ltr = pe
                mk_aux_pe_free = pe

                @B.vector
                def _(e, w=ltr, lf=mk_lt_free):
                    e.wait_ge(h_pe, w)
                    e.wait_ge(h_ac, lf)
                    e.drain()
                    e.tensor_copy(lt[:, :], p_tr[0:128, 0:E]).then_inc(h_ve, 1)
                ve += 1
                mk_ptr_evac = ve

                @B.vector
                def _(e):
                    e.drain()
                    e.tensor_reduce(r1[:, :], lt[:, :], axis=AX.X, op=OP.max,
                                    negate=True)
                    e.drain()
                    e.nop().then_inc(h_ve, 1)
                ve += 1
                nmx_done = ve

                @B.scalar
                def _(e, w=nmx_done):
                    e.wait_ge(h_ve, w)
                    e.drain()
                    e.activation(out=exb[:, :], in_=lt[:, :], func=AF.Exp,
                                 bias=r1[:, 0:1]).then_inc(h_ac, 1)
                ac += 1
                exp_done = ac
                mk_lt_free = ac

                @B.vector
                def _(e, tt=tt, w=exp_done):
                    e.wait_ge(h_ac, w)
                    e.drain()
                    e.tensor_reduce(r2[:, :], exb[:, :], axis=AX.X, op=OP.add)
                    e.drain()
                    e.reciprocal(r2[:, :], r2[:, :])
                    e.drain()
                    e.tensor_scalar(pb[:, :], exb[:, :], r2[:, 0:1], None,
                                    op0=OP.mult)
                    e.drain()
                    e.tensor_reduce(r1[:, :], pb[:, :], axis=AX.X, op=OP.max)
                    e.drain()
                    e.tensor_scalar(msk[:, :], pb[:, :], r1[:, 0:1], None,
                                    op0=OP.is_ge)
                    e.drain()
                    e.scalar_tensor_tensor(exb[:, :], msk[:, :], -2.0, pb[:, :],
                                           op0=OP.mult, op1=OP.add)
                    e.drain()
                    e.tensor_reduce(r2[:, :], exb[:, :], axis=AX.X, op=OP.max)
                    e.drain()
                    e.tensor_scalar(msk[:, :], pb[:, :], r2[:, 0:1], None,
                                    op0=OP.is_ge)
                    e.tensor_tensor(r3[:, :], r1[:, :], r2[:, :], op=OP.add)
                    e.drain()
                    e.tensor_scalar(r3[:, :], r3[:, :], 2.0, 2e-6,
                                    op0=OP.mult, op1=OP.add)
                    e.drain()
                    e.reciprocal(r3[:, :], r3[:, :])
                    e.drain()
                    e.tensor_tensor(cmb_tm[:, :], pb[:, :], msk[:, :], op=OP.mult)
                    e.drain()
                    e.tensor_scalar(cmb_tm[:, :], cmb_tm[:, :], r3[:, 0:1],
                                    sig_sv[:, tt:tt + 1],
                                    op0=OP.mult, op1=OP.mult)
                    e.drain()
                    e.nop().then_inc(h_ve, 1)
                ve += 1
                cmbv = ve

                @B.tensor
                def _(e, w=cmbv, pf=mk_ptr_evac):
                    e.wait_ge(h_ve, max(w, pf))
                    e.transpose(p_tr[0:E, 0:128], cmb_tm[:, :], id_sb[:, :]
                                ).then_inc(h_pe, 1)
                pe += 1
                ctr = pe

                @B.vector
                def _(e, tt=tt, w=ctr, cf=mk_cmbsb_free):
                    e.wait_ge(h_pe, w)
                    e.wait_ge(h_rcb, cf)
                    e.drain()
                    e.tensor_copy(cmb_sb[:, tt * 128:(tt + 1) * 128],
                                  p_tr[0:E, 0:128]).then_inc(h_ve, 1)
                ve += 1
                mk_ptr_evac = ve
                cmb_done_mark = ve

            h2hi_last = h2lo_last = 0
            crp_last = 0
            for eg in range(KH):
                @B.scalar
                def _(e, eg=eg, w=cmb_done_mark, fr=mk_h2f_free, pcb=rcb,
                      so=mk_otm_st):
                    e.wait_ge(h_rcb, pcb)
                    e.wait_ge(h_st, so)
                    e.wait_ge(h_ve, max(w, fr))
                    e.dma_start(out=cmb_rep[:, :],
                                in_=bass.AP(cmb_sb, (2 * eg) * TB,
                                            [[TB, 2], [0, R], [1, TB]])
                                ).then_inc(h_rcb, 16)
                rcb += 16
                crp = rcb
                crp_last = rcb

                @B.vector
                def _(e, eg=eg, w=crp, r=rh2_done, hf=mk_h2f_free):
                    e.wait_ge(h_rcb, w)
                    e.wait_ge(h_rb, r)
                    e.wait_ge(h_ve, hf)
                    e.drain()
                    e.tensor_tensor(cmb_rep[:, :], cmb_rep[:, :], rh2[:, :],
                                    op=OP.mult).then_inc(h_ve, 1)
                ve += 1
                h2f_done = ve
                if eg == KH - 1:
                    mk_rh2_free = ve

                @B.scalar
                def _(e, eg=eg, w=h2f_done, pf=mk_h2_free):
                    e.wait_ge(h_ve, w)
                    e.wait_ge(h_pe, pf)
                    e.drain()
                    e.copy(out=h2_hi[:, eg * TB:(eg + 1) * TB],
                           in_=cmb_rep[:, :]).then_inc(h_ac, 1)
                ac += 1
                h2hi_last = ac

                @B.vector
                def _(e, eg=eg, w=h2hi_last, pf=mk_h2_free):
                    e.wait_ge(h_ac, w)
                    e.wait_ge(h_pe, pf)
                    e.drain()
                    e.tensor_tensor(h2_lo[:, eg * TB:(eg + 1) * TB],
                                    cmb_rep[:, :],
                                    h2_hi[:, eg * TB:(eg + 1) * TB],
                                    op=OP.subtract).then_inc(h_ve, 1)
                ve += 1
                h2lo_last = ve
                mk_h2f_free = ve
            mk_cmbsb_free = crp_last
            h2hi_prev = h2hi_last

            g_marks = []
            for dg in range(DC):
                fpar = dg % 2
                wcnt = [w0, w1]
                for (scr, wb) in ((wscr_hi, wbuf_hi), (wscr_lo, wbuf_lo)):
                    @B.sync
                    def _(e, scr=scr, wb=wb, dg=dg, wpar=fpar,
                          fr=mk_wbuf_free[fpar], pw=wcnt[fpar], c_=c, wsv=ws):
                        e.wait_ge(h_w[wpar], pw)
                        if c_ == 0:
                            e.wait_ge(h_ws, wsv)
                        e.wait_ge(h_pe, fr)
                        e.dma_start(
                            out=wb[:, wpar * Q:(wpar + 1) * Q],
                            in_=scr.ap()[dg, :, :]).then_inc(h_w[wpar], 16)
                    if fpar == 0:
                        w0 += 16
                    else:
                        w1 += 16
                wld = (fpar, w0 if fpar == 0 else w1)

                @B.tensor
                def _(e, dg=dg, fpar=fpar, wm=wld, par=par,
                      fe=mk_ffn_evac[fpar], hh=h2hi_last, hl=h2lo_last):
                    e.wait_ge(h_w[wm[0]], wm[1])
                    e.wait_ge(h_ac, hh)
                    e.wait_ge(h_ve, max(fe, hl))
                    pf = p_ffn[fpar]
                    wo = fpar * Q
                    for kch in range(KCH):
                        wh = wbuf_hi[:, wo + kch * 128:wo + (kch + 1) * 128]
                        wl = wbuf_lo[:, wo + kch * 128:wo + (kch + 1) * 128]
                        rh_ = xn_hi_sl(kch, par)
                        rl_ = xn_lo_sl(kch, par)
                        e.matmul(pf[0:128, 0:TB], wh, rh_, start=(kch == 0),
                                 stop=False)
                        e.matmul(pf[0:128, 0:TB], wh, rl_, start=False,
                                 stop=False)
                        e.matmul(pf[0:128, 0:TB], wl, rh_, start=False,
                                 stop=False)
                    for kh in range(KH):
                        wh = wbuf_hi[:, wo + (KCH + kh) * 128:wo + (KCH + kh + 1) * 128]
                        wl = wbuf_lo[:, wo + (KCH + kh) * 128:wo + (KCH + kh + 1) * 128]
                        rh_ = h2_hi[:, kh * TB:(kh + 1) * TB]
                        rl_ = h2_lo[:, kh * TB:(kh + 1) * TB]
                        e.matmul(pf[0:128, 0:TB], wh, rh_, start=False,
                                 stop=False)
                        e.matmul(pf[0:128, 0:TB], wh, rl_, start=False,
                                 stop=False)
                        mm = e.matmul(pf[0:128, 0:TB], wl, rh_, start=False,
                                      stop=(kh == KH - 1))
                        if kh == KH - 1:
                            mm.then_inc(h_pe, 1)
                pe += 1
                ffn_done = pe
                mk_wbuf_free[fpar] = ffn_done
                if dg == DC - 1:
                    mk_xnB_free = ffn_done
                    mk_xnA_free[par] = ffn_done
                    mk_h2_free = ffn_done

                @B.vector
                def _(e, dg=dg, fpar=fpar, w=ffn_done, gf=mk_gout[dg],
                      mr=mrep_done):
                    e.wait_ge(h_pe, w)
                    e.wait_ge(h_pe, gf)
                    e.wait_ge(h_rm, mr)
                    e.drain()
                    e.tensor_scalar(cor_t[:, :], m_rep[:, :],
                                    ws_b[:, dg:dg + 1], None, op0=OP.mult)
                    e.drain()
                    e.tensor_tensor(g_sb[:, dg * TB:(dg + 1) * TB],
                                    p_ffn[fpar][0:128, 0:TB], cor_t[:, :],
                                    op=OP.subtract)
                    e.drain()
                    e.tensor_tensor(g_sb[:, dg * TB:(dg + 1) * TB],
                                    g_sb[:, dg * TB:(dg + 1) * TB],
                                    r_rep[:, :], op=OP.mult).then_inc(h_ve, 1)
                ve += 1
                mk_ffn_evac[fpar] = ve

                @B.scalar
                def _(e, dg=dg, w=ve):
                    e.wait_ge(h_ve, w)
                    e.drain()
                    e.activation(out=g_sb[:, dg * TB:(dg + 1) * TB],
                                 in_=g_sb[:, dg * TB:(dg + 1) * TB],
                                 func=AF.Silu,
                                 bias=bb_sb[:, dg:dg + 1]).then_inc(h_ac, 1)
                ac += 1
                g_marks.append(ac)

            for dc in range(DC):
                @B.vector
                def _(e, dc=dc, w=dtsp[dc], tp_=tp_done, zf=mk_z_free,
                      op_=mk_gout[dc] if False else 0, mr=mrep_done, par=par):
                    e.wait_ge(h_ac, w)
                    e.wait_ge(h_tp, tp_)
                    e.wait_ge(h_gp, zf)
                    e.wait_ge(h_rm, mr)
                    e.drain()
                    e.tensor_tensor(cor_t[:, :], xn_hi_sl(dc, par),
                                    xn_lo_sl(dc, par), op=OP.add)
                    e.drain()
                    e.tensor_tensor(xnsl[:, :], cor_t[:, :], m_rep[:, :],
                                    op=OP.subtract)
                    e.drain()
                    e.tensor_tensor(xnsl[:, :], xnsl[:, :], r_rep[:, :],
                                    op=OP.mult)
                    e.drain()
                    e.tensor_tensor(z_dc[:, :], dtraw[:, dc * TB:(dc + 1) * TB],
                                    xnsl[:, :], op=OP.mult)
                    e.drain()
                    e.nop().then_inc(h_ve, 1)
                ve += 1
                z_done = ve
                z_mark[dc] = ve
                if dc == DC - 1:
                    mk_xnA_scan[par] = ve

                @B.vector
                def _(e, dc=dc, hcp=mk_hc_prev):
                    e.wait_ge(h_gp, hcp)
                    e.drain()
                    e.memset(bass.AP(da, 0, [[N * TB1, 128], [TB1, N], [1, 1]]),
                             0.0)
                    e.drain()
                    e.nop().then_inc(h_ve, 1)
                ve += 1
                slots_done = ve

                @B.scalar
                def _(e, dc=dc, sd=slots_done, ar=a_ready):
                    e.wait_ge(h_ve, max(sd, ar))
                    e.drain()
                    for n in range(N):
                        e.activation(out=da[:, n * TB1 + 1:(n + 1) * TB1],
                                     in_=dtraw[:, dc * TB:(dc + 1) * TB],
                                     func=AF.Exp,
                                     scale=a_sb[:, dc * N + n:dc * N + n + 1])
                    e.drain()
                    e.nop().then_inc(h_ac, 1)
                ac += 1
                da_done = ac
                da_mark[dc] = ac

                @B.gpsimd
                def _(e, dc=dc, zm=z_done, bm_=brep_done, sm=mk_scan_prev,
                      tv=mk_tree_prev):
                    e.wait_ge(h_ve, max(zm, sm, tv))
                    e.wait_ge(h_rb, bm_)
                    e.drain()
                    for n in range(N):
                        e.tensor_tensor(dbx[:, n * TB1 + 1:(n + 1) * TB1],
                                        z_dc[:, :], b_rep[:, n * TB:(n + 1) * TB],
                                        op=OP.mult)
                    e.tensor_copy(
                        bass.AP(dbx, 0, [[N * TB1, 128], [TB1, N], [1, 1]]),
                        bass.AP(state, dc * N, [[DC * N, 128], [1, N], [1, 1]]))
                    e.drain()
                    e.nop().then_inc(h_gp, 1)
                gp += 1
                dbx_done = gp
                mk_z_free = gp

                @B.vector
                def _(e, dc=dc, dm=da_done, bm_=dbx_done):
                    e.wait_ge(h_ac, dm)
                    e.wait_ge(h_gp, bm_)
                    e.drain()
                    e.tensor_tensor_scan(dbx[:, :], da[:, :], dbx[:, :], 0.0,
                                         op0=OP.mult, op1=OP.add)
                    e.drain()
                    e.tensor_copy(
                        bass.AP(state, dc * N, [[DC * N, 128], [1, N], [1, 1]]),
                        bass.AP(dbx, TB, [[N * TB1, 128], [TB1, N], [1, 1]]))
                    e.drain()
                    e.nop().then_inc(h_ve, 1)
                ve += 1
                scan_done = ve
                mk_scan_prev = ve

                @B.gpsimd
                def _(e, dc=dc, w=scan_done, cm=crep_done):
                    e.wait_ge(h_ve, w)
                    e.wait_ge(h_rc, cm)
                    e.drain()
                    for n in range(N):
                        e.tensor_tensor(da[:, n * TB:(n + 1) * TB],
                                        dbx[:, n * TB1 + 1:(n + 1) * TB1],
                                        c_rep[:, n * TB:(n + 1) * TB],
                                        op=OP.mult)
                    e.drain()
                    e.tensor_tensor(dbx[:, 0:(N // 2) * TB],
                                    da[:, 0:(N // 2) * TB],
                                    da[:, (N // 2) * TB:N * TB], op=OP.add)
                    e.drain()
                    e.nop().then_inc(h_gp, 1)
                gp += 1
                hc_done = gp
                mk_crep_free = gp
                mk_hc_prev = gp

                @B.vector
                def _(e, dc=dc, w=hc_done, gm=g_marks[dc]):
                    e.wait_ge(h_gp, w)
                    e.wait_ge(h_ac, gm)
                    e.drain()
                    w4 = (N // 4) * TB
                    e.tensor_tensor(dbx[:, 0:w4], dbx[:, 0:w4],
                                    dbx[:, w4:2 * w4], op=OP.add)
                    e.drain()
                    w8 = (N // 8) * TB
                    e.tensor_tensor(dbx[:, 0:w8], dbx[:, 0:w8],
                                    dbx[:, w8:2 * w8], op=OP.add)
                    e.drain()
                    w16 = (N // 16) * TB
                    e.tensor_tensor(z_dc[:, 0:w16], dbx[:, 0:w16],
                                    dbx[:, w16:2 * w16], op=OP.add)
                    e.drain()
                    e.scalar_tensor_tensor(dt_dc[:, :], xnsl[:, :],
                                           dvec_sb[:, dc:dc + 1], z_dc[:, :],
                                           op0=OP.mult, op1=OP.add)
                    e.drain()
                    e.tensor_tensor(g_sb[:, dc * TB:(dc + 1) * TB],
                                    dt_dc[:, :],
                                    g_sb[:, dc * TB:(dc + 1) * TB], op=OP.mult)
                    e.drain()
                    e.tensor_tensor(g_sb[:, dc * TB:(dc + 1) * TB],
                                    g_sb[:, dc * TB:(dc + 1) * TB],
                                    cor_t[:, :], op=OP.add)
                    e.drain()
                    e.nop().then_inc(h_ve, 1)
                ve += 1
                mk_tree_prev = ve
                out_q.append((dc, ve))

        t0 = (NCH - 1) * TB
        for (odc, ovm) in out_q:
            for th in range(NTT):
                @B.tensor
                def _(e, dc=odc, th=th, w=ovm, pf=mk_ptr2_evac):
                    e.wait_ge(h_ve, max(w, pf))
                    e.transpose(p_tr2[0:128, 0:128],
                                g_sb[:, dc * TB + th * 128:
                                     dc * TB + (th + 1) * 128],
                                id_sb[:, :]).then_inc(h_pe, 1)
                pe += 1
                otr = pe

                @B.vector
                def _(e, dc=odc, th=th, w=otr, sf=mk_otm_st, hh=h2hi_prev):
                    e.wait_ge(h_pe, w)
                    e.wait_ge(h_st, sf)
                    e.wait_ge(h_ac, hh)
                    e.drain()
                    e.tensor_copy(o_tm[:, 0:128],
                                  p_tr2[0:128, 0:128]).then_inc(h_ve, 1)
                ve += 1
                mk_ptr2_evac = ve
                otm_done = ve

                @B.scalar
                def _(e, dc=odc, th=th, w=otm_done, ps=st):
                    e.wait_ge(h_st, ps)
                    e.wait_ge(h_ve, w)
                    e.dma_start(
                        out=bass.AP(out_h, (t0 + th * 128) * DSL + dc * 128,
                                    [[DSL, 128], [1, 128]]),
                        in_=o_tm[:, 0:128]).then_inc(h_st, 16)
                st += 16
                mk_otm_st = st

        @B.scalar
        def _(e, w=st):
            e.wait_ge(h_st, w)

        @B.sync
        def _(e, w=ws):
            e.wait_ge(h_ws, w)

    return nc



def _shard_inputs(inputs, cfg: Cfg, n_cores=8):
    x_full = np.asarray(inputs["L0_attn_x"], dtype=np.float32)
    alog = np.asarray(inputs["L0_attn_A_log"], dtype=np.float32)
    dvec = np.asarray(inputs["L0_attn_D"], dtype=np.float32)
    xpw = np.asarray(inputs["L0_attn_x_proj_w"], dtype=np.float32)
    dtw = np.asarray(inputs["L0_attn_dt_proj_w"], dtype=np.float32)
    dtb = np.asarray(inputs["L0_attn_dt_proj_b"], dtype=np.float32)
    bw = np.asarray(inputs["L0_ffn_base_w"], dtype=np.float32)
    bb = np.asarray(inputs["L0_ffn_base_b"], dtype=np.float32)
    amat = np.asarray(inputs["L0_ffn_A"], dtype=np.float32)
    bmat = np.asarray(inputs["L0_ffn_B"], dtype=np.float32)
    rw = np.asarray(inputs["L0_ffn_router_w"], dtype=np.float32)

    DM, DSL, N, RK, E, R = cfg.DM, cfg.DSL, cfg.N, cfg.RK, cfg.E, cfg.R
    ident = np.eye(128, dtype=np.float32)
    in_maps = []
    n_j = DM // DSL
    n_s = n_cores // n_j
    for c in range(n_cores):
        s = c // n_j
        j = c % n_j
        dsl = np.arange(j * DSL, (j + 1) * DSL)
        rest = np.concatenate([np.arange(0, j * DSL),
                               np.arange((j + 1) * DSL, DM)])
        perm = np.concatenate([dsl, rest])
        auxw = np.concatenate([rw, xpw[RK:RK + 2 * N, :], amat], axis=0)
        im = {
            "x": np.ascontiguousarray(x_full[s][:, perm]),
            "bw_t": np.ascontiguousarray(bw[dsl, :][:, perm].T),
            "bm_t": np.ascontiguousarray(
                bmat[:, dsl, :].transpose(0, 2, 1).reshape(E * R, DSL)),
            "xpw_dt_t": np.ascontiguousarray(xpw[:RK, :][:, perm].T),
            "auxw_t": np.ascontiguousarray(auxw[:, perm].T),
            "dtw_t": np.ascontiguousarray(dtw[dsl, :].T),
            "dtb": np.ascontiguousarray(dtb[dsl]),
            "alog": np.ascontiguousarray(alog[dsl, :]),
            "dvec": np.ascontiguousarray(dvec[dsl]),
            "bb": np.ascontiguousarray(bb[dsl]),
            "ident": ident,
            "consts": np.broadcast_to(
                np.array([1e-5, 1.0], dtype=np.float32), (128, 2)).copy(),
        }
        in_maps.append(im)
    return in_maps


def kernel(**inputs):
    cfg = Cfg()
    nc = build_program(cfg)
    in_maps = _shard_inputs(inputs, cfg, n_cores=8)
    res = run_bass_kernel_spmd(nc, in_maps, core_ids=list(range(8)))
    T, DM, DSL = cfg.T, cfg.DM, cfg.DSL
    n_j = DM // DSL
    out = np.empty((2, T, DM), dtype=np.float32)
    for c in range(8):
        s, j = c // n_j, c % n_j
        out[s][:, j * DSL:(j + 1) * DSL] = res.results[c]["out"]
    return out


# version 08
# speedup vs baseline: 1.0037x; 1.0037x over previous
import numpy as np
import concourse.bass as bass
import concourse.mybir as mybir
from concourse.bass_utils import run_bass_kernel_spmd
from concourse.alu_op_type import AluOpType as OP

F32 = mybir.dt.float32
BF16 = mybir.dt.bfloat16
AF = mybir.ActivationFunctionType
AX = mybir.AxisListType


class Cfg:
    def __init__(self, T=4096, DM=4096, DSL=1024, N=16, RK=128, E=16, R=64,
                 TB=256, eps=1e-5):
        self.T, self.DM, self.DSL, self.N, self.RK, self.E, self.R, self.TB = \
            T, DM, DSL, N, RK, E, R, TB
        self.eps = eps
        self.NCH = T // TB
        self.KCH = DM // 128
        self.DC = DSL // 128
        self.NTT = TB // 128
        self.AUX = 2 * N + R + E
        self.KH = (E * R) // 128


def build_program(cfg: Cfg):
    nc = bass.Bass("TRN2", target_bir_lowering=False)
    T, DM, DSL, N, RK, E, R, TB = (cfg.T, cfg.DM, cfg.DSL, cfg.N, cfg.RK,
                                   cfg.E, cfg.R, cfg.TB)
    NCH, KCH, DC, NTT, AUX, KH = (cfg.NCH, cfg.KCH, cfg.DC, cfg.NTT,
                                  cfg.AUX, cfg.KH)
    eps = cfg.eps
    TB1 = TB + 1
    NLH = 8 if DM >= 2048 else 2
    HDM = DM // NLH
    HKC = KCH // NLH
    NKQ = max(1, (DM // NLH) // 512)
    SBW = (DM // NLH) // NKQ
    NB = NLH * NKQ

    x_h = nc.dram_tensor("x", [T, DM], F32, kind="ExternalInput")
    bw_h = nc.dram_tensor("bw_t", [DM, DSL], F32, kind="ExternalInput")
    bm_h = nc.dram_tensor("bm_t", [E * R, DSL], F32, kind="ExternalInput")
    xpd_h = nc.dram_tensor("xpw_dt_t", [DM, RK], F32, kind="ExternalInput")
    aux_h = nc.dram_tensor("auxw_t", [DM, AUX], F32, kind="ExternalInput")
    dtw_h = nc.dram_tensor("dtw_t", [RK, DSL], F32, kind="ExternalInput")
    dtb_h = nc.dram_tensor("dtb", [DSL], F32, kind="ExternalInput")
    alog_h = nc.dram_tensor("alog", [DSL, N], F32, kind="ExternalInput")
    dvec_h = nc.dram_tensor("dvec", [DSL], F32, kind="ExternalInput")
    bb_h = nc.dram_tensor("bb", [DSL], F32, kind="ExternalInput")
    id_h = nc.dram_tensor("ident", [128, 128], F32, kind="ExternalInput")
    cst_h = nc.dram_tensor("consts", [128, 2], F32, kind="ExternalInput")
    out_h = nc.dram_tensor("out", [T, DSL], F32, kind="ExternalOutput")
    Q = (KCH + KH) * 128
    wscr_hi = nc.dram_tensor("wscr_hi", [DC, 128, Q], BF16)
    wscr_lo = nc.dram_tensor("wscr_lo", [DC, 128, Q], BF16)
    xscr_hi = nc.dram_tensor("xscr_hi", [KCH // 4, 128, 4 * RK], BF16)
    xscr_lo = nc.dram_tensor("xscr_lo", [KCH // 4, 128, 4 * RK], BF16)
    ascr_hi = nc.dram_tensor("ascr_hi", [KCH // 4, 128, 4 * AUX], BF16)
    ascr_lo = nc.dram_tensor("ascr_lo", [KCH // 4, 128, 4 * AUX], BF16)

    from contextlib import ExitStack
    with ExitStack() as _stk:
        bnbuf = _stk.enter_context(nc.sbuf_tensor("bnbuf", [128, 6 * NB], F32))
        mv = _stk.enter_context(nc.sbuf_tensor("mv", [128, 2], F32))
        lnv = _stk.enter_context(nc.sbuf_tensor("lnv", [128, 1], F32))
        mpak = _stk.enter_context(nc.sbuf_tensor("mpak", [128, 2], F32))
        sig_sv = _stk.enter_context(nc.sbuf_tensor("sig_sv", [128, 2], F32))
        xnh = _stk.enter_context(nc.sbuf_tensor("xnh", [128, HDM], F32))
        hih = _stk.enter_context(nc.sbuf_tensor("hih", [128, HDM], BF16))
        loh = _stk.enter_context(nc.sbuf_tensor("loh", [128, HDM], BF16))
        xnA_hi = _stk.enter_context(nc.sbuf_tensor("xnA_hi", [128, KCH * TB], BF16))
        xnA_lo = _stk.enter_context(nc.sbuf_tensor("xnA_lo", [128, KCH * TB], BF16))
        xnB_hi = _stk.enter_context(nc.sbuf_tensor("xnB_hi", [128, (KCH // 2) * TB], BF16))
        xnB_lo = _stk.enter_context(nc.sbuf_tensor("xnB_lo", [128, (KCH // 2) * TB], BF16))
        wbuf_hi = _stk.enter_context(nc.sbuf_tensor("wbuf_hi", [128, 2 * Q], BF16))
        wbuf_lo = _stk.enter_context(nc.sbuf_tensor("wbuf_lo", [128, 2 * Q], BF16))
        xbuf_hi = _stk.enter_context(nc.sbuf_tensor("xbuf_hi", [128, 4 * RK], BF16))
        xbuf_lo = _stk.enter_context(nc.sbuf_tensor("xbuf_lo", [128, 4 * RK], BF16))
        abuf_hi = _stk.enter_context(nc.sbuf_tensor("abuf_hi", [128, 4 * AUX], BF16))
        abuf_lo = _stk.enter_context(nc.sbuf_tensor("abuf_lo", [128, 4 * AUX], BF16))
        dtraw = _stk.enter_context(nc.sbuf_tensor("dtraw", [128, DC * TB], F32))
        dtw_sb = _stk.enter_context(nc.sbuf_tensor("dtw_sb", [128, DSL], F32))
        a_sb = _stk.enter_context(nc.sbuf_tensor("a_sb", [128, DC * N], F32))
        dtb_sb = _stk.enter_context(nc.sbuf_tensor("dtb_sb", [128, DC], F32))
        dvec_sb = _stk.enter_context(nc.sbuf_tensor("dvec_sb", [128, DC], F32))
        bb_sb = _stk.enter_context(nc.sbuf_tensor("bb_sb", [128, DC], F32))
        id_sb = _stk.enter_context(nc.sbuf_tensor("id_sb", [128, 128], F32))
        cst_sb = _stk.enter_context(nc.sbuf_tensor("cst_sb", [128, 2], F32))
        dtin_sb = _stk.enter_context(nc.sbuf_tensor("dtin_sb", [128, TB], F32))
        aux_sb = _stk.enter_context(nc.sbuf_tensor("aux_sb", [AUX, TB], F32))
        lt = _stk.enter_context(nc.sbuf_tensor("lt", [128, E], F32))
        r1 = _stk.enter_context(nc.sbuf_tensor("r1", [128, 1], F32))
        r2 = _stk.enter_context(nc.sbuf_tensor("r2", [128, 1], F32))
        r3 = _stk.enter_context(nc.sbuf_tensor("r3", [128, 1], F32))
        exb = _stk.enter_context(nc.sbuf_tensor("exb", [128, E], F32))
        pb = _stk.enter_context(nc.sbuf_tensor("pb", [128, E], F32))
        msk = _stk.enter_context(nc.sbuf_tensor("msk", [128, E], F32))
        cmb_tm = _stk.enter_context(nc.sbuf_tensor("cmb_tm", [128, E], F32))
        cmb_sb = _stk.enter_context(nc.sbuf_tensor("cmb_sb", [E, TB], F32))
        cmb_rep = _stk.enter_context(nc.sbuf_tensor("cmb_rep", [128, TB], F32))
        rh2 = _stk.enter_context(nc.sbuf_tensor("rh2", [128, TB], F32))
        h2_hi = _stk.enter_context(nc.sbuf_tensor("h2_hi", [128, KH * TB], BF16))
        h2_lo = _stk.enter_context(nc.sbuf_tensor("h2_lo", [128, KH * TB], BF16))
        g_sb = _stk.enter_context(nc.sbuf_tensor("g_sb", [128, DC * TB], F32))
        msf = _stk.enter_context(nc.sbuf_tensor("msf", [2, TB], F32))
        m_rep = _stk.enter_context(nc.sbuf_tensor("m_rep", [128, TB], F32))
        r_rep = _stk.enter_context(nc.sbuf_tensor("r_rep", [128, TB], F32))
        cor_t = _stk.enter_context(nc.sbuf_tensor("cor_t", [128, TB], F32))
        ws_b = _stk.enter_context(nc.sbuf_tensor("ws_b", [128, DC], F32))
        ws_d = _stk.enter_context(nc.sbuf_tensor("ws_d", [128, 1], F32))
        ws_a = _stk.enter_context(nc.sbuf_tensor("ws_a", [128, 1], F32))
        b_rep = _stk.enter_context(nc.sbuf_tensor("b_rep", [128, N * TB], F32))
        c_rep = _stk.enter_context(nc.sbuf_tensor("c_rep", [128, N * TB], F32))
        dt_dc = _stk.enter_context(nc.sbuf_tensor("dt_dc", [128, TB], F32))
        z_dc = _stk.enter_context(nc.sbuf_tensor("z_dc", [128, TB], F32))
        xnsl = _stk.enter_context(nc.sbuf_tensor("xnsl", [128, TB], F32))
        da = _stk.enter_context(nc.sbuf_tensor("da", [128, N * TB1], F32))
        dbx = _stk.enter_context(nc.sbuf_tensor("dbx", [128, N * TB1], F32))
        state = _stk.enter_context(nc.sbuf_tensor("state", [128, DC * N], F32))
        p_dtin = _stk.enter_context(nc.psum_tensor("p_dtin", [128, 512], F32))
        p_aux = _stk.enter_context(nc.psum_tensor("p_aux", [128, 512], F32))
        p_ffn0 = _stk.enter_context(nc.psum_tensor("p_ffn0", [128, 512], F32))
        p_ffn1 = _stk.enter_context(nc.psum_tensor("p_ffn1", [128, 512], F32))
        p_dt0 = _stk.enter_context(nc.psum_tensor("p_dt0", [128, 512], F32))
        p_dt1 = _stk.enter_context(nc.psum_tensor("p_dt1", [128, 512], F32))
        p_tr = _stk.enter_context(nc.psum_tensor("p_tr", [128, 512], F32))
        p_tr2 = _stk.enter_context(nc.psum_tensor("p_tr2", [128, 512], F32))
        h_ld = _stk.enter_context(nc.semaphore("s_ld"))
        h_tp = _stk.enter_context(nc.semaphore("s_tp"))
        h_rp = _stk.enter_context(nc.semaphore("s_rp"))
        h_st = _stk.enter_context(nc.semaphore("s_st"))
        h_pe = _stk.enter_context(nc.semaphore("s_pe"))
        h_ve = _stk.enter_context(nc.semaphore("s_ve"))
        h_ac = _stk.enter_context(nc.semaphore("s_ac"))
        h_gp = _stk.enter_context(nc.semaphore("s_gp"))
        h_ws = _stk.enter_context(nc.semaphore("s_ws"))
        h_x = _stk.enter_context(nc.semaphore("s_x"))
        h_w0 = _stk.enter_context(nc.semaphore("s_w0"))
        h_w1 = _stk.enter_context(nc.semaphore("s_w1"))
        h_rb = _stk.enter_context(nc.semaphore("s_rb"))
        h_rc = _stk.enter_context(nc.semaphore("s_rc"))
        h_rm = _stk.enter_context(nc.semaphore("s_rm"))
        h_rcb = _stk.enter_context(nc.semaphore("s_rcb"))
        B = _stk.enter_context(nc.Block())
        ld = ve = ac = pe = gp = tp = rp = st = ws = 0
        xc = w0 = w1 = rb = rc = rm = rcb = 0
        h_w = [h_w0, h_w1]
        p_ffn = [p_ffn0, p_ffn1]
        p_dt = [p_dt0, p_dt1]

        @B.sync
        def _(e):
            e.dma_start(out=dtw_sb[:, :], in_=dtw_h.ap()[:, :]).then_inc(h_ld, 16)
            for (sb, hb) in ((dtb_sb, dtb_h), (dvec_sb, dvec_h), (bb_sb, bb_h)):
                e.dma_start(out=bass.AP(sb, 0, [[DC, 128], [1, DC]]),
                            in_=bass.AP(hb, 0, [[1, 128], [128, DC]]),
                            allow_slow_non_contiguous=True
                            ).then_inc(h_ld, 16)
            e.dma_start(out=bass.AP(a_sb, 0, [[DC * N, 128], [N, DC], [1, N]]),
                        in_=bass.AP(alog_h, 0, [[N, 128], [128 * N, DC], [1, N]])
                        ).then_inc(h_ld, 16)
            e.dma_start(out=id_sb[:, :], in_=id_h.ap()[:, :]).then_inc(h_ld, 16)
            e.dma_start(out=cst_sb[:, :], in_=cst_h.ap()[:, :]).then_inc(h_ld, 16)
        ld += 7 * 16
        nc.const_aps.register(F32, eps, cst_sb[:, 0:1])
        nc.const_aps.register(F32, 1.0, cst_sb[:, 1:2])
        ld_consts = ld

        @B.scalar
        def _(e, w=ld):
            e.wait_ge(h_ld, w)
            e.activation(out=a_sb[:, :], in_=a_sb[:, :], func=AF.Exp
                         ).then_inc(h_ac, 1)
        ac += 1

        @B.vector
        def _(e, w=ac):
            e.wait_ge(h_ac, w)
            e.tensor_scalar(a_sb[:, :], a_sb[:, :], -1.0, None, op0=OP.mult)
            e.memset(state[:, :], 0.0)
            e.drain()
            e.nop().then_inc(h_ve, 1)
        ve += 1
        a_ready = ve

        prev_sinks = []
        prev_lo = 0

        o_tm = cmb_rep
        CW = DSL // 2
        wsplit, wsp_hi, wsp_lo = xnh, hih, loh

        ws_started = [False]

        def split_stream(src_h, n_tiles, width, sink, ws_map=None,
                         ws_last=False):
            nonlocal ld, ve, ac, pe, prev_sinks, prev_lo
            ncb = (width + CW - 1) // CW
            for kt in range(n_tiles):
                for cb in range(ncb):
                    cw = min(CW, width - cb * CW)

                    @B.sync
                    def _(e, kt=kt, cb=cb, cw=cw, g=list(prev_sinks), pl2=ld):
                        e.wait_ge(h_ld, pl2)
                        for (sh, vv) in g:
                            e.wait_ge(sh, vv)
                        e.dma_start(out=wsplit[:, :cw],
                                    in_=src_h.ap()[kt * 128:(kt + 1) * 128,
                                                   cb * CW:cb * CW + cw]
                                    ).then_inc(h_ld, 16)
                    ld += 16

                    @B.scalar
                    def _(e, w=ld, pl=prev_lo, cw=cw):
                        e.wait_ge(h_ld, w)
                        e.wait_ge(h_ve, pl)
                        e.drain()
                        e.copy(out=wsp_hi[:, :cw], in_=wsplit[:, :cw]
                               ).then_inc(h_ac, 1)
                    ac += 1
                    hi_mark = ac

                    @B.vector
                    def _(e, w=ld, hm=hi_mark, cw=cw):
                        e.wait_ge(h_ld, w)
                        e.wait_ge(h_ac, hm)
                        e.drain()
                        e.tensor_tensor(wsp_lo[:, :cw], wsplit[:, :cw],
                                        wsp_hi[:, :cw], op=OP.subtract
                                        ).then_inc(h_ve, 1)
                    ve += 1
                    lo_mark = ve
                    prev_sinks = sink(kt, cb, cw, hi_mark, lo_mark)
                    if ws_map is not None:
                        nsub = (cw + 127) // 128

                        @B.tensor
                        def _(e, kt=kt, cb=cb, cw=cw, nsub=nsub, w=ld,
                              ar=a_ready):
                            e.wait_ge(h_ld, w)
                            e.wait_ge(h_ve, ar)
                            for sub in range(nsub):
                                sw = 128
                                col = ws_map(cb, sub)
                                st_ = not ws_started[0]
                                ws_started[0] = True
                                sp_ = (ws_last and kt == n_tiles - 1
                                       and sub == nsub - 1)
                                mm = e.matmul(p_tr2[0:sw, col:col + 1],
                                              wsplit[:, sub * 128:sub * 128 + sw],
                                              cst_sb[:, 1:2],
                                              start=st_, stop=sp_)
                                if sub == nsub - 1:
                                    mm.then_inc(h_pe, 1)
                        pe += 1
                        prev_sinks = list(prev_sinks) + [(h_pe, pe)]
                    prev_lo = lo_mark

        def scr_sink(qoff):
            def sink(kt, cb, cw, hm, lm):
                nonlocal ws
                marks = []
                ndg = cw // 128
                for (sb, scr, sh, sv) in ((wsp_hi, wscr_hi, h_ac, hm),
                                          (wsp_lo, wscr_lo, h_ve, lm)):
                    @B.sync
                    def _(e, sb=sb, scr=scr, sh=sh, sv=sv, kt=kt, cb=cb,
                          ndg=ndg, pw=ws):
                        e.wait_ge(h_ws, pw)
                        e.wait_ge(sh, sv)
                        e.dma_start(
                            out=bass.AP(scr, (qoff + kt) * 128
                                        + cb * (CW // 128) * 128 * Q,
                                        [[Q, 128], [128 * Q, ndg], [1, 128]]),
                            in_=bass.AP(sb, 0, [[HDM, 128], [128, ndg], [1, 128]])
                        ).then_inc(h_ws, 16)
                    ws += 16
                    marks.append((h_ws, ws))
                return marks
            return sink

        def gscr_sink(scr_hi, scr_lo, W):
            def sink(kt, cb, cw, hm, lm):
                nonlocal ws
                marks = []
                g, s = kt // 4, kt % 4
                for (sb, scr, sh, sv) in ((wsp_hi, scr_hi, h_ac, hm),
                                          (wsp_lo, scr_lo, h_ve, lm)):
                    @B.sync
                    def _(e, sb=sb, scr=scr, sh=sh, sv=sv, g=g, s=s, pw=ws):
                        e.wait_ge(h_ws, pw)
                        e.wait_ge(sh, sv)
                        e.dma_start(
                            out=bass.AP(scr, g * 128 * 4 * W + s * W,
                                        [[4 * W, 128], [1, W]]),
                            in_=bass.AP(sb, 0, [[HDM, 128], [1, W]])
                        ).then_inc(h_ws, 16)
                    ws += 16
                    marks.append((h_ws, ws))
                return marks
            return sink

        def resident_sink(hi_t, lo_t, width):
            def sink(kt, cb, cw, hm, lm):
                nonlocal ve, ac
                marks = []
                off = kt * width + cb * CW

                @B.scalar
                def _(e, off=off, cw=cw):
                    e.drain()
                    e.copy(out=hi_t[:, off:off + cw],
                           in_=wsp_hi[:, :cw]).then_inc(h_ac, 1)
                ac += 1
                marks.append((h_ac, ac))

                @B.vector
                def _(e, off=off, cw=cw):
                    e.drain()
                    e.tensor_copy(lo_t[:, off:off + cw],
                                  wsp_lo[:, :cw]).then_inc(h_ve, 1)
                ve += 1
                marks.append((h_ve, ve))
                return marks
            return sink

        split_stream(bw_h, KCH, DSL, scr_sink(0),
                     ws_map=lambda cb, sub: cb * (CW // 128) + sub)
        split_stream(bm_h, KH, DSL, scr_sink(KCH))
        split_stream(xpd_h, KCH, RK, gscr_sink(xscr_hi, xscr_lo, RK),
                     ws_map=lambda cb, sub: DC)
        split_stream(aux_h, KCH, AUX, gscr_sink(ascr_hi, ascr_lo, AUX),
                     ws_map=lambda cb, sub: DC + 1, ws_last=True)

        @B.vector
        def _(e, w=pe):
            e.wait_ge(h_pe, w)
            e.drain()
            e.tensor_copy(ws_b[:, :], p_tr2[0:128, 0:DC])
            e.tensor_copy(ws_d[:, :], p_tr2[0:128, DC:DC + 1])
            e.tensor_copy(ws_a[:, :], p_tr2[0:128, DC + 1:DC + 2])
            e.drain()
            e.nop().then_inc(h_ve, 1)
        ve += 1
        ws_evac = ve

        @B.vector
        def _(e):
            e.drain()

        @B.scalar
        def _(e):
            e.drain()

        mk_xn_free_pe = 0
        prologue_ws = ws
        mk_wbuf_free = [0, 0]
        mk_ffn_evac = [0, 0]
        mk_dt_exp = [0, 0]
        mk_dtin_evac = 0
        mk_xnh_free = ve
        mk_half_free = 0
        mk_prev_hi = ac
        mk_rstd_prev = 0
        mk_mpak_pe = 0
        mk_msf_free = 0
        mk_brep_free = 0
        mk_crep_free = 0
        mk_rh2_free = 0
        mk_cmbsb_free = 0
        mk_aux_pe_free = 0
        mk_aux_rp_free = 0
        mk_aux_rb_free = 0
        mk_gsb_free = 0
        mk_mrep_free = 0
        mk_h2_free = 0
        mk_h2f_free = 0
        mk_odc_pe = 0
        mk_ptr2_evac = ws_evac
        mk_otm_st = 0
        mk_ptr_evac = 0
        mk_lt_free = 0
        mk_scan_prev = 0
        mk_hc_prev = 0
        mk_tree_prev = 0
        mk_z_free = 0
        mk_dt_z = 0
        mk_da_free = 0
        mk_aux_evac = 0
        dt_prev_ev = 0
        h2hi_prev = 0
        mk_sig_free = 0

        out_q = []
        mk_xnA_free = [0, 0]
        mk_xnA_scan = [0, 0]
        mk_xnB_free = 0
        mk_gout = [0] * DC
        mk_xabuf_free = 0
        dtev = [0] * DC
        dtsp = [0] * DC
        da_mark = [0] * DC
        z_mark = [0] * DC
        NG = KCH // 4
        HALF = KCH // 2

        def xn_hi_sl(kch, par):
            if kch < HALF:
                return xnA_hi[:, (par * HALF + kch) * TB:(par * HALF + kch + 1) * TB]
            return xnB_hi[:, (kch - HALF) * TB:(kch - HALF + 1) * TB]

        def xn_lo_sl(kch, par):
            if kch < HALF:
                return xnA_lo[:, (par * HALF + kch) * TB:(par * HALF + kch + 1) * TB]
            return xnB_lo[:, (kch - HALF) * TB:(kch - HALF + 1) * TB]

        def emit_ln(ci):
            nonlocal xc, ve, ac, pe, tp
            nonlocal mk_xnh_free, mk_prev_hi, mk_rstd_prev, mk_mpak_pe
            nonlocal mk_half_free, mk_ptr_evac
            t0 = ci * TB
            par = ci % 2
            c = ci
            msf_marks = []
            for tt in range(NTT):
                for q in range(NLH):
                    @B.sync
                    def _(e, tt=tt, q=q, w=mk_xnh_free, wh=mk_prev_hi,
                          px=xc, c_=c, pws=prologue_ws):
                        e.wait_ge(h_x, px)
                        e.wait_ge(h_ve, w)
                        e.wait_ge(h_ac, wh)
                        if c_ == 0 and tt == 0 and q == 0:
                            e.wait_ge(h_ws, pws)
                        e.dma_start(
                            out=xnh[:, :],
                            in_=x_h.ap()[t0 + tt * 128:t0 + tt * 128 + 128,
                                         q * HDM:(q + 1) * HDM]
                            ).then_inc(h_x, 16)
                    xc += 16
                    xld = xc

                    @B.vector
                    def _(e, q=q, w=xld, pr=mk_rstd_prev, pm=mk_mpak_pe):
                        e.wait_ge(h_x, w)
                        if q == 0:
                            e.wait_ge(h_ac, pr)
                            e.wait_ge(h_pe, pm)
                        e.drain()
                        for k in range(NKQ):
                            e.bn_stats(
                                bnbuf[:, 6 * (q * NKQ + k):
                                      6 * (q * NKQ + k) + 6],
                                xnh[:, SBW * k:SBW * (k + 1)])
                        e.drain()
                        e.nop().then_inc(h_ve, 1)
                    ve += 1

                    @B.scalar
                    def _(e, q=q, w=xld, fr=mk_half_free):
                        e.wait_ge(h_x, w)
                        e.wait_ge(h_tp, fr)
                        e.drain()
                        e.copy(out=hih[:, :], in_=xnh[:, :]).then_inc(h_ac, 1)
                    ac += 1
                    hi_done = ac
                    mk_prev_hi = hi_done

                    @B.vector
                    def _(e, w=hi_done, fr=mk_half_free):
                        e.wait_ge(h_ac, w)
                        e.wait_ge(h_tp, fr)
                        e.drain()
                        e.tensor_tensor(loh[:, :], xnh[:, :], hih[:, :],
                                        op=OP.subtract).then_inc(h_ve, 1)
                    ve += 1
                    lo_done = ve
                    mk_xnh_free = ve

                    to_A = q < NLH // 2
                    if to_A:
                        dsts = (xnA_hi, xnA_lo)
                        doff = (par * HALF + q * HKC) * TB + tt * 128
                        dpitch = KCH * TB
                        pgate = mk_xnA_free[par]
                        vgate = mk_xnA_scan[par]
                    else:
                        dsts = (xnB_hi, xnB_lo)
                        doff = ((q - NLH // 2) * HKC) * TB + tt * 128
                        dpitch = HALF * TB
                        pgate = mk_xnB_free
                        vgate = 0

                    @B.sync
                    def _(e, tt=tt, q=q, hd=hi_done, ld_=lo_done, pt=tp,
                          dsts=dsts, doff=doff, dpi=dpitch, pg=pgate,
                          vg=vgate):
                        e.wait_ge(h_tp, pt)
                        e.wait_ge(h_ac, hd)
                        e.wait_ge(h_ve, ld_)
                        e.wait_ge(h_pe, pg)
                        if vg:
                            e.wait_ge(h_ve, vg)
                        e.dma_start_transpose(
                            out=bass.AP(dsts[0], doff,
                                        [[dpi, 128], [TB, HKC], [1, 128]]),
                            in_=hih[:, :]).then_inc(h_tp, 16)
                        e.dma_start_transpose(
                            out=bass.AP(dsts[1], doff,
                                        [[dpi, 128], [TB, HKC], [1, 128]]),
                            in_=loh[:, :]).then_inc(h_tp, 16)
                    tp += 32
                    mk_half_free = tp

                @B.vector
                def _(e, pm=mk_mpak_pe):
                    e.wait_ge(h_pe, pm)
                    e.drain()
                    e.bn_aggr(mv[:, :], bnbuf[:, :])
                    e.drain()
                    e.tensor_copy(mpak[:, 0:1], mv[:, 0:1])
                    e.drain()
                    e.nop().then_inc(h_ve, 1)
                ve += 1
                bn_done = ve

                @B.scalar
                def _(e, tt=tt, w=bn_done, sg=mk_sig_free):
                    e.wait_ge(h_ve, w)
                    e.wait_ge(h_ve, sg)
                    e.wait_ge(h_ld, ld_consts)
                    e.drain()
                    e.activation(out=lnv[:, :], in_=mv[:, 1:2], func=AF.Ln,
                                 bias=eps)
                    e.drain()
                    e.activation(out=mpak[:, 1:2], in_=lnv[:, :], func=AF.Exp,
                                 scale=-0.5)
                    e.activation(out=sig_sv[:, tt:tt + 1], in_=lnv[:, :],
                                 func=AF.Exp, scale=0.5)
                    e.drain()
                    e.nop().then_inc(h_ac, 1)
                ac += 1
                rstd_done = ac
                mk_rstd_prev = rstd_done

                @B.tensor
                def _(e, tt=tt, w=rstd_done, pf=mk_ptr_evac):
                    e.wait_ge(h_ac, w)
                    e.wait_ge(h_ve, pf)
                    e.transpose(p_tr[0:2, 0:128], mpak[:, :], id_sb[:, :]
                                ).then_inc(h_pe, 1)
                pe += 1
                mk_mpak_pe = pe

                @B.vector
                def _(e, tt=tt, w=pe, mf=mk_msf_free):
                    e.wait_ge(h_pe, w)
                    e.wait_ge(h_rm, mf)
                    e.drain()
                    e.tensor_copy(msf[:, tt * 128:(tt + 1) * 128],
                                  p_tr[0:2, 0:128]).then_inc(h_ve, 1)
                ve += 1
                mk_ptr_evac = ve
                msf_marks.append(ve)

            return tp, msf_marks[-1]

        def emit_mrep(msf_last):
            nonlocal rm, mk_msf_free
            @B.scalar
            def _(e, w=msf_last, fr=mk_mrep_free, pm2=rm):
                e.wait_ge(h_rm, pm2)
                e.wait_ge(h_ve, w)
                e.wait_ge(h_ve, fr)
                e.dma_start(out=m_rep[:, :],
                            in_=bass.AP(msf, 0, [[TB, 1], [0, 128], [1, TB]])
                            ).then_inc(h_rm, 16)
                e.dma_start(out=r_rep[:, :],
                            in_=bass.AP(msf, TB, [[TB, 1], [0, 128], [1, TB]])
                            ).then_inc(h_rm, 16)
            rm += 32
            mrep_done = rm
            mk_msf_free = rm

            return mrep_done

        ln_tp, ln_msf = emit_ln(0)
        mrep_done = emit_mrep(ln_msf)

        for c in range(NCH):
            t0 = c * TB
            par = c % 2
            tp_done = ln_tp

            for (odc, ovm) in out_q:
                for th in range(NTT):
                    @B.tensor
                    def _(e, dc=odc, th=th, w=ovm, pf=mk_ptr2_evac):
                        e.wait_ge(h_ve, max(w, pf))
                        e.transpose(p_tr2[0:128, 0:128],
                                    g_sb[:, dc * TB + th * 128:
                                         dc * TB + (th + 1) * 128],
                                    id_sb[:, :]).then_inc(h_pe, 1)
                    pe += 1
                    otr = pe

                    @B.vector
                    def _(e, dc=odc, th=th, w=otr, sf=mk_otm_st,
                          hh=h2hi_prev):
                        e.wait_ge(h_pe, w)
                        e.wait_ge(h_st, sf)
                        e.wait_ge(h_ac, hh)
                        e.drain()
                        e.tensor_copy(o_tm[:, 0:128],
                                      p_tr2[0:128, 0:128]).then_inc(h_ve, 1)
                    ve += 1
                    mk_ptr2_evac = ve
                    otm_done = ve

                    @B.scalar
                    def _(e, dc=odc, th=th, w=otm_done, ps=st, tprev=t0 - TB):
                        e.wait_ge(h_st, ps)
                        e.wait_ge(h_ve, w)
                        e.dma_start(
                            out=bass.AP(out_h,
                                        (tprev + th * 128) * DSL + dc * 128,
                                        [[DSL, 128], [1, 128]]),
                            in_=o_tm[:, 0:128]).then_inc(h_st, 16)
                    st += 16
                    mk_otm_st = st
                mk_gout[odc] = pe
            out_q = []

            xp_first = True
            for gidx in range(NG):
                @B.sync
                def _(e, gidx=gidx, pl=ld, fr=mk_xabuf_free):
                    e.wait_ge(h_ld, pl)
                    e.wait_ge(h_pe, fr)
                    e.dma_start(out=xbuf_hi[:, :], in_=xscr_hi.ap()[gidx, :, :]
                                ).then_inc(h_ld, 16)
                    e.dma_start(out=xbuf_lo[:, :], in_=xscr_lo.ap()[gidx, :, :]
                                ).then_inc(h_ld, 16)
                    e.dma_start(out=abuf_hi[:, :], in_=ascr_hi.ap()[gidx, :, :]
                                ).then_inc(h_ld, 16)
                    e.dma_start(out=abuf_lo[:, :], in_=ascr_lo.ap()[gidx, :, :]
                                ).then_inc(h_ld, 16)
                ld += 64
                gld = ld

                @B.tensor
                def _(e, gidx=gidx, w=gld, tp_=tp_done, dtf=mk_dtin_evac,
                      axf=mk_aux_evac, par=par):
                    e.wait_ge(h_ld, w)
                    if gidx == 0:
                        e.wait_ge(h_tp, tp_)
                        e.wait_ge(h_ve, max(dtf, axf))
                    for i in range(4):
                        kch = gidx * 4 + i
                        first = kch == 0
                        last = kch == KCH - 1
                        rh_ = xn_hi_sl(kch, par)
                        rl_ = xn_lo_sl(kch, par)
                        wh = xbuf_hi[:, i * RK:(i + 1) * RK]
                        wl = xbuf_lo[:, i * RK:(i + 1) * RK]
                        e.matmul(p_dtin[0:RK, 0:TB], wh, rh_, start=first,
                                 stop=False)
                        e.matmul(p_dtin[0:RK, 0:TB], wh, rl_, start=False,
                                 stop=False)
                        e.matmul(p_dtin[0:RK, 0:TB], wl, rh_, start=False,
                                 stop=last)
                        wh = abuf_hi[:, i * AUX:(i + 1) * AUX]
                        wl = abuf_lo[:, i * AUX:(i + 1) * AUX]
                        e.matmul(p_aux[0:AUX, 0:TB], wh, rh_, start=first,
                                 stop=False)
                        e.matmul(p_aux[0:AUX, 0:TB], wh, rl_, start=False,
                                 stop=False)
                        mm = e.matmul(p_aux[0:AUX, 0:TB], wl, rh_, start=False,
                                      stop=last)
                    mm.then_inc(h_pe, 1)
                pe += 1
                mk_xabuf_free = pe
            xp1_done = pe
            aux_done = pe

            @B.vector
            def _(e, w=xp1_done, mr=mrep_done):
                e.wait_ge(h_pe, w)
                e.wait_ge(h_rm, mr)
                e.drain()
                e.tensor_scalar(cor_t[:, :], m_rep[:, :], ws_d[:, 0:1], None,
                                op0=OP.mult)
                e.drain()
                e.tensor_tensor(dtin_sb[:, :], p_dtin[0:RK, 0:TB],
                                cor_t[:, :], op=OP.subtract)
                e.drain()
                e.tensor_tensor(dtin_sb[:, :], dtin_sb[:, :], r_rep[:, :],
                                op=OP.mult).then_inc(h_ve, 1)
            ve += 1
            mk_dtin_evac = ve

            @B.vector
            def _(e, w=aux_done, f1=mk_aux_pe_free, f2=mk_aux_rp_free,
                  f3=mk_aux_rb_free):
                e.wait_ge(h_pe, w)
                e.wait_ge(h_pe, f1)
                e.wait_ge(h_rc, f2)
                e.wait_ge(h_rb, f3)
                e.wait_ge(h_rm, mrep_done)
                e.drain()
                e.tensor_scalar(cor_t[0:AUX, :], m_rep[0:AUX, :],
                                ws_a[0:AUX, 0:1], None, op0=OP.mult)
                e.drain()
                e.tensor_tensor(aux_sb[:, :], p_aux[0:AUX, 0:TB],
                                cor_t[0:AUX, :], op=OP.subtract)
                e.drain()
                e.tensor_tensor(aux_sb[:, :], aux_sb[:, :], r_rep[0:AUX, :],
                                op=OP.mult).then_inc(h_ve, 1)
            ve += 1
            mk_aux_evac = ve

            for dc in range(DC):
                dpar = dc % 2

                @B.tensor
                def _(e, dc=dc, dpar=dpar, w=mk_dtin_evac,
                      fe=dtev[dc - 2] if dc >= 2 else dt_prev_ev):
                    e.wait_ge(h_ve, max(w, fe))
                    e.matmul(p_dt[dpar][0:128, 0:TB],
                             dtw_sb[:, dc * 128:(dc + 1) * 128],
                             dtin_sb[:, :], start=True, stop=True
                             ).then_inc(h_pe, 1)
                pe += 1
                dtmm = pe

                @B.vector
                def _(e, dc=dc, dpar=dpar, w=dtmm, fa=da_mark[dc],
                      fz=z_mark[dc]):
                    e.wait_ge(h_pe, w)
                    e.wait_ge(h_ac, fa)
                    e.wait_ge(h_ve, fz)
                    e.drain()
                    e.tensor_copy(dtraw[:, dc * TB:(dc + 1) * TB],
                                  p_dt[dpar][0:128, 0:TB]).then_inc(h_ve, 1)
                ve += 1
                dtev[dc] = ve

                @B.scalar
                def _(e, dc=dc, w=dtev[dc]):
                    e.wait_ge(h_ve, w)
                    e.drain()
                    e.activation(out=dtraw[:, dc * TB:(dc + 1) * TB],
                                 in_=dtraw[:, dc * TB:(dc + 1) * TB],
                                 func=AF.Exp, bias=dtb_sb[:, dc:dc + 1])
                    e.drain()
                    e.activation(out=dtraw[:, dc * TB:(dc + 1) * TB],
                                 in_=dtraw[:, dc * TB:(dc + 1) * TB],
                                 func=AF.Ln, bias=1.0).then_inc(h_ac, 1)
                ac += 1
                dtsp[dc] = ac
            dt_prev_ev = dtev[DC - 2]

            @B.sync
            def _(e, w=mk_aux_evac, fb=mk_brep_free, fr_=mk_rh2_free, pb_=rb):
                e.wait_ge(h_rb, pb_)
                e.wait_ge(h_ve, w)
                e.wait_ge(h_gp, fb)
                e.wait_ge(h_ve, fr_)
                for n in range(N):
                    e.dma_start(out=b_rep[:, n * TB:(n + 1) * TB],
                                in_=bass.AP(aux_sb, (E + n) * TB,
                                            [[TB, 1], [0, 128], [1, TB]])
                                ).then_inc(h_rb, 16)
                for half in range(2):
                    e.dma_start(out=rh2[half * R:(half + 1) * R, :],
                                in_=bass.AP(aux_sb, (E + 2 * N) * TB,
                                            [[TB, R], [1, TB]])
                                ).then_inc(h_rb, 16)
            rb += 16 * (N + 2)
            rh2_done = rb

            @B.scalar
            def _(e, w=mk_aux_evac, fc=mk_crep_free, pc=rc):
                e.wait_ge(h_rc, pc)
                e.wait_ge(h_ve, w)
                e.wait_ge(h_gp, fc)
                for n in range(N):
                    e.dma_start(out=c_rep[:, n * TB:(n + 1) * TB],
                                in_=bass.AP(aux_sb, (E + N + n) * TB,
                                            [[TB, 1], [0, 128], [1, TB]])
                                ).then_inc(h_rc, 16)
            rc += 16 * N
            crep_done = rc
            brep_done = rh2_done
            mk_aux_rp_free = rc
            mk_aux_rb_free = rb

            cmb_done_mark = 0
            for tt in range(NTT):
                @B.tensor
                def _(e, tt=tt, w=mk_aux_evac, pf=mk_ptr_evac):
                    e.wait_ge(h_ve, max(w, pf))
                    e.transpose(p_tr[0:128, 0:E],
                                aux_sb[0:E, tt * 128:(tt + 1) * 128],
                                id_sb[0:E, 0:E]).then_inc(h_pe, 1)
                pe += 1
                ltr = pe
                mk_aux_pe_free = pe

                @B.vector
                def _(e, w=ltr, lf=mk_lt_free):
                    e.wait_ge(h_pe, w)
                    e.wait_ge(h_ac, lf)
                    e.drain()
                    e.tensor_copy(lt[:, :], p_tr[0:128, 0:E]).then_inc(h_ve, 1)
                ve += 1
                mk_ptr_evac = ve

                @B.vector
                def _(e):
                    e.drain()
                    e.tensor_reduce(r1[:, :], lt[:, :], axis=AX.X, op=OP.max,
                                    negate=True)
                    e.drain()
                    e.nop().then_inc(h_ve, 1)
                ve += 1
                nmx_done = ve

                @B.scalar
                def _(e, w=nmx_done):
                    e.wait_ge(h_ve, w)
                    e.drain()
                    e.activation(out=exb[:, :], in_=lt[:, :], func=AF.Exp,
                                 bias=r1[:, 0:1]).then_inc(h_ac, 1)
                ac += 1
                exp_done = ac
                mk_lt_free = ac

                @B.vector
                def _(e, tt=tt, w=exp_done):
                    e.wait_ge(h_ac, w)
                    e.drain()
                    e.tensor_reduce(r2[:, :], exb[:, :], axis=AX.X, op=OP.add)
                    e.drain()
                    e.reciprocal(r2[:, :], r2[:, :])
                    e.drain()
                    e.tensor_scalar(pb[:, :], exb[:, :], r2[:, 0:1], None,
                                    op0=OP.mult)
                    e.drain()
                    e.tensor_reduce(r1[:, :], pb[:, :], axis=AX.X, op=OP.max)
                    e.drain()
                    e.tensor_scalar(msk[:, :], pb[:, :], r1[:, 0:1], None,
                                    op0=OP.is_ge)
                    e.drain()
                    e.scalar_tensor_tensor(exb[:, :], msk[:, :], -2.0, pb[:, :],
                                           op0=OP.mult, op1=OP.add)
                    e.drain()
                    e.tensor_reduce(r2[:, :], exb[:, :], axis=AX.X, op=OP.max)
                    e.drain()
                    e.tensor_scalar(msk[:, :], pb[:, :], r2[:, 0:1], None,
                                    op0=OP.is_ge)
                    e.tensor_tensor(r3[:, :], r1[:, :], r2[:, :], op=OP.add)
                    e.drain()
                    e.tensor_scalar(r3[:, :], r3[:, :], 2.0, 2e-6,
                                    op0=OP.mult, op1=OP.add)
                    e.drain()
                    e.reciprocal(r3[:, :], r3[:, :])
                    e.drain()
                    e.tensor_tensor(cmb_tm[:, :], pb[:, :], msk[:, :], op=OP.mult)
                    e.drain()
                    e.tensor_scalar(cmb_tm[:, :], cmb_tm[:, :], r3[:, 0:1],
                                    sig_sv[:, tt:tt + 1],
                                    op0=OP.mult, op1=OP.mult)
                    e.drain()
                    e.nop().then_inc(h_ve, 1)
                ve += 1
                cmbv = ve

                @B.tensor
                def _(e, w=cmbv, pf=mk_ptr_evac):
                    e.wait_ge(h_ve, max(w, pf))
                    e.transpose(p_tr[0:E, 0:128], cmb_tm[:, :], id_sb[:, :]
                                ).then_inc(h_pe, 1)
                pe += 1
                ctr = pe

                @B.vector
                def _(e, tt=tt, w=ctr, cf=mk_cmbsb_free):
                    e.wait_ge(h_pe, w)
                    e.wait_ge(h_rcb, cf)
                    e.drain()
                    e.tensor_copy(cmb_sb[:, tt * 128:(tt + 1) * 128],
                                  p_tr[0:E, 0:128]).then_inc(h_ve, 1)
                ve += 1
                mk_ptr_evac = ve
                cmb_done_mark = ve

            h2hi_last = h2lo_last = 0
            crp_last = 0
            for eg in range(KH):
                @B.scalar
                def _(e, eg=eg, w=cmb_done_mark, fr=mk_h2f_free, pcb=rcb,
                      so=mk_otm_st):
                    e.wait_ge(h_rcb, pcb)
                    e.wait_ge(h_st, so)
                    e.wait_ge(h_ve, max(w, fr))
                    e.dma_start(out=cmb_rep[:, :],
                                in_=bass.AP(cmb_sb, (2 * eg) * TB,
                                            [[TB, 2], [0, R], [1, TB]])
                                ).then_inc(h_rcb, 16)
                rcb += 16
                crp = rcb
                crp_last = rcb

                @B.vector
                def _(e, eg=eg, w=crp, r=rh2_done, hf=mk_h2f_free):
                    e.wait_ge(h_rcb, w)
                    e.wait_ge(h_rb, r)
                    e.wait_ge(h_ve, hf)
                    e.drain()
                    e.tensor_tensor(cmb_rep[:, :], cmb_rep[:, :], rh2[:, :],
                                    op=OP.mult).then_inc(h_ve, 1)
                ve += 1
                h2f_done = ve
                if eg == KH - 1:
                    mk_rh2_free = ve

                @B.scalar
                def _(e, eg=eg, w=h2f_done, pf=mk_h2_free):
                    e.wait_ge(h_ve, w)
                    e.wait_ge(h_pe, pf)
                    e.drain()
                    e.copy(out=h2_hi[:, eg * TB:(eg + 1) * TB],
                           in_=cmb_rep[:, :]).then_inc(h_ac, 1)
                ac += 1
                h2hi_last = ac

                @B.vector
                def _(e, eg=eg, w=h2hi_last, pf=mk_h2_free):
                    e.wait_ge(h_ac, w)
                    e.wait_ge(h_pe, pf)
                    e.drain()
                    e.tensor_tensor(h2_lo[:, eg * TB:(eg + 1) * TB],
                                    cmb_rep[:, :],
                                    h2_hi[:, eg * TB:(eg + 1) * TB],
                                    op=OP.subtract).then_inc(h_ve, 1)
                ve += 1
                h2lo_last = ve
                mk_h2f_free = ve
            mk_cmbsb_free = crp_last
            h2hi_prev = h2hi_last

            g_marks = []
            for dg in range(DC):
                fpar = dg % 2
                wcnt = [w0, w1]
                for (scr, wb) in ((wscr_hi, wbuf_hi), (wscr_lo, wbuf_lo)):
                    @B.sync
                    def _(e, scr=scr, wb=wb, dg=dg, wpar=fpar,
                          fr=mk_wbuf_free[fpar], pw=wcnt[fpar], c_=c, wsv=ws):
                        e.wait_ge(h_w[wpar], pw)
                        if c_ == 0:
                            e.wait_ge(h_ws, wsv)
                        e.wait_ge(h_pe, fr)
                        e.dma_start(
                            out=wb[:, wpar * Q:(wpar + 1) * Q],
                            in_=scr.ap()[dg, :, :]).then_inc(h_w[wpar], 16)
                    if fpar == 0:
                        w0 += 16
                    else:
                        w1 += 16
                wld = (fpar, w0 if fpar == 0 else w1)

                @B.tensor
                def _(e, dg=dg, fpar=fpar, wm=wld, par=par,
                      fe=mk_ffn_evac[fpar], hh=h2hi_last, hl=h2lo_last):
                    e.wait_ge(h_w[wm[0]], wm[1])
                    e.wait_ge(h_ac, hh)
                    e.wait_ge(h_ve, max(fe, hl))
                    pf = p_ffn[fpar]
                    wo = fpar * Q
                    for kch in range(KCH):
                        wh = wbuf_hi[:, wo + kch * 128:wo + (kch + 1) * 128]
                        wl = wbuf_lo[:, wo + kch * 128:wo + (kch + 1) * 128]
                        rh_ = xn_hi_sl(kch, par)
                        rl_ = xn_lo_sl(kch, par)
                        e.matmul(pf[0:128, 0:TB], wh, rh_, start=(kch == 0),
                                 stop=False)
                        e.matmul(pf[0:128, 0:TB], wh, rl_, start=False,
                                 stop=False)
                        e.matmul(pf[0:128, 0:TB], wl, rh_, start=False,
                                 stop=False)
                    for kh in range(KH):
                        wh = wbuf_hi[:, wo + (KCH + kh) * 128:wo + (KCH + kh + 1) * 128]
                        wl = wbuf_lo[:, wo + (KCH + kh) * 128:wo + (KCH + kh + 1) * 128]
                        rh_ = h2_hi[:, kh * TB:(kh + 1) * TB]
                        rl_ = h2_lo[:, kh * TB:(kh + 1) * TB]
                        e.matmul(pf[0:128, 0:TB], wh, rh_, start=False,
                                 stop=False)
                        e.matmul(pf[0:128, 0:TB], wh, rl_, start=False,
                                 stop=False)
                        mm = e.matmul(pf[0:128, 0:TB], wl, rh_, start=False,
                                      stop=(kh == KH - 1))
                        if kh == KH - 1:
                            mm.then_inc(h_pe, 1)
                pe += 1
                ffn_done = pe
                mk_wbuf_free[fpar] = ffn_done
                if dg == DC - 1:
                    mk_xnB_free = ffn_done
                    mk_xnA_free[par] = ffn_done
                    mk_h2_free = ffn_done

                @B.vector
                def _(e, dg=dg, fpar=fpar, w=ffn_done, gf=mk_gout[dg],
                      mr=mrep_done):
                    e.wait_ge(h_pe, w)
                    e.wait_ge(h_pe, gf)
                    e.wait_ge(h_rm, mr)
                    e.drain()
                    e.tensor_scalar(cor_t[:, :], m_rep[:, :],
                                    ws_b[:, dg:dg + 1], None, op0=OP.mult)
                    e.drain()
                    e.tensor_tensor(g_sb[:, dg * TB:(dg + 1) * TB],
                                    p_ffn[fpar][0:128, 0:TB], cor_t[:, :],
                                    op=OP.subtract)
                    e.drain()
                    e.tensor_tensor(g_sb[:, dg * TB:(dg + 1) * TB],
                                    g_sb[:, dg * TB:(dg + 1) * TB],
                                    r_rep[:, :], op=OP.mult).then_inc(h_ve, 1)
                ve += 1
                mk_ffn_evac[fpar] = ve

                @B.scalar
                def _(e, dg=dg, w=ve):
                    e.wait_ge(h_ve, w)
                    e.drain()
                    e.activation(out=g_sb[:, dg * TB:(dg + 1) * TB],
                                 in_=g_sb[:, dg * TB:(dg + 1) * TB],
                                 func=AF.Silu,
                                 bias=bb_sb[:, dg:dg + 1]).then_inc(h_ac, 1)
                ac += 1
                g_marks.append(ac)

            if c + 1 < NCH:
                mk_sig_free = cmb_done_mark
                ln_tp, ln_msf = emit_ln(c + 1)

            for dc in range(DC):
                @B.vector
                def _(e, dc=dc, w=dtsp[dc], tp_=tp_done, zf=mk_z_free,
                      op_=mk_gout[dc] if False else 0, mr=mrep_done, par=par):
                    e.wait_ge(h_ac, w)
                    e.wait_ge(h_tp, tp_)
                    e.wait_ge(h_gp, zf)
                    e.wait_ge(h_rm, mr)
                    e.drain()
                    e.tensor_tensor(cor_t[:, :], xn_hi_sl(dc, par),
                                    xn_lo_sl(dc, par), op=OP.add)
                    e.drain()
                    e.tensor_tensor(xnsl[:, :], cor_t[:, :], m_rep[:, :],
                                    op=OP.subtract)
                    e.drain()
                    e.tensor_tensor(xnsl[:, :], xnsl[:, :], r_rep[:, :],
                                    op=OP.mult)
                    e.drain()
                    e.tensor_tensor(z_dc[:, :], dtraw[:, dc * TB:(dc + 1) * TB],
                                    xnsl[:, :], op=OP.mult)
                    e.drain()
                    e.nop().then_inc(h_ve, 1)
                ve += 1
                z_done = ve
                z_mark[dc] = ve
                if dc == DC - 1:
                    mk_xnA_scan[par] = ve

                @B.vector
                def _(e, dc=dc, hcp=mk_hc_prev):
                    e.wait_ge(h_gp, hcp)
                    e.drain()
                    e.memset(bass.AP(da, 0, [[N * TB1, 128], [TB1, N], [1, 1]]),
                             0.0)
                    e.drain()
                    e.nop().then_inc(h_ve, 1)
                ve += 1
                slots_done = ve

                @B.scalar
                def _(e, dc=dc, sd=slots_done, ar=a_ready):
                    e.wait_ge(h_ve, max(sd, ar))
                    e.drain()
                    for n in range(N):
                        e.activation(out=da[:, n * TB1 + 1:(n + 1) * TB1],
                                     in_=dtraw[:, dc * TB:(dc + 1) * TB],
                                     func=AF.Exp,
                                     scale=a_sb[:, dc * N + n:dc * N + n + 1])
                    e.drain()
                    e.nop().then_inc(h_ac, 1)
                ac += 1
                da_done = ac
                da_mark[dc] = ac

                @B.gpsimd
                def _(e, dc=dc, zm=z_done, bm_=brep_done, sm=mk_scan_prev,
                      tv=mk_tree_prev):
                    e.wait_ge(h_ve, max(zm, sm, tv))
                    e.wait_ge(h_rb, bm_)
                    e.drain()
                    for n in range(N):
                        e.tensor_tensor(dbx[:, n * TB1 + 1:(n + 1) * TB1],
                                        z_dc[:, :], b_rep[:, n * TB:(n + 1) * TB],
                                        op=OP.mult)
                    e.tensor_copy(
                        bass.AP(dbx, 0, [[N * TB1, 128], [TB1, N], [1, 1]]),
                        bass.AP(state, dc * N, [[DC * N, 128], [1, N], [1, 1]]))
                    e.drain()
                    e.nop().then_inc(h_gp, 1)
                gp += 1
                dbx_done = gp
                mk_z_free = gp

                @B.vector
                def _(e, dc=dc, dm=da_done, bm_=dbx_done):
                    e.wait_ge(h_ac, dm)
                    e.wait_ge(h_gp, bm_)
                    e.drain()
                    e.tensor_tensor_scan(dbx[:, :], da[:, :], dbx[:, :], 0.0,
                                         op0=OP.mult, op1=OP.add)
                    e.drain()
                    e.tensor_copy(
                        bass.AP(state, dc * N, [[DC * N, 128], [1, N], [1, 1]]),
                        bass.AP(dbx, TB, [[N * TB1, 128], [TB1, N], [1, 1]]))
                    e.drain()
                    e.nop().then_inc(h_ve, 1)
                ve += 1
                scan_done = ve
                mk_scan_prev = ve

                @B.gpsimd
                def _(e, dc=dc, w=scan_done, cm=crep_done):
                    e.wait_ge(h_ve, w)
                    e.wait_ge(h_rc, cm)
                    e.drain()
                    for n in range(N):
                        e.tensor_tensor(da[:, n * TB:(n + 1) * TB],
                                        dbx[:, n * TB1 + 1:(n + 1) * TB1],
                                        c_rep[:, n * TB:(n + 1) * TB],
                                        op=OP.mult)
                    e.drain()
                    e.tensor_tensor(dbx[:, 0:(N // 2) * TB],
                                    da[:, 0:(N // 2) * TB],
                                    da[:, (N // 2) * TB:N * TB], op=OP.add)
                    e.drain()
                    e.nop().then_inc(h_gp, 1)
                gp += 1
                hc_done = gp
                mk_crep_free = gp
                mk_hc_prev = gp

                @B.vector
                def _(e, dc=dc, w=hc_done, gm=g_marks[dc]):
                    e.wait_ge(h_gp, w)
                    e.wait_ge(h_ac, gm)
                    e.drain()
                    w4 = (N // 4) * TB
                    e.tensor_tensor(dbx[:, 0:w4], dbx[:, 0:w4],
                                    dbx[:, w4:2 * w4], op=OP.add)
                    e.drain()
                    w8 = (N // 8) * TB
                    e.tensor_tensor(dbx[:, 0:w8], dbx[:, 0:w8],
                                    dbx[:, w8:2 * w8], op=OP.add)
                    e.drain()
                    w16 = (N // 16) * TB
                    e.tensor_tensor(z_dc[:, 0:w16], dbx[:, 0:w16],
                                    dbx[:, w16:2 * w16], op=OP.add)
                    e.drain()
                    e.scalar_tensor_tensor(dt_dc[:, :], xnsl[:, :],
                                           dvec_sb[:, dc:dc + 1], z_dc[:, :],
                                           op0=OP.mult, op1=OP.add)
                    e.drain()
                    e.tensor_tensor(g_sb[:, dc * TB:(dc + 1) * TB],
                                    dt_dc[:, :],
                                    g_sb[:, dc * TB:(dc + 1) * TB], op=OP.mult)
                    e.drain()
                    e.tensor_tensor(g_sb[:, dc * TB:(dc + 1) * TB],
                                    g_sb[:, dc * TB:(dc + 1) * TB],
                                    cor_t[:, :], op=OP.add)
                    e.drain()
                    e.nop().then_inc(h_ve, 1)
                ve += 1
                mk_tree_prev = ve
                out_q.append((dc, ve))

            mk_mrep_free = z_mark[DC - 1]
            if c + 1 < NCH:
                mrep_done = emit_mrep(ln_msf)

        t0 = (NCH - 1) * TB
        for (odc, ovm) in out_q:
            for th in range(NTT):
                @B.tensor
                def _(e, dc=odc, th=th, w=ovm, pf=mk_ptr2_evac):
                    e.wait_ge(h_ve, max(w, pf))
                    e.transpose(p_tr2[0:128, 0:128],
                                g_sb[:, dc * TB + th * 128:
                                     dc * TB + (th + 1) * 128],
                                id_sb[:, :]).then_inc(h_pe, 1)
                pe += 1
                otr = pe

                @B.vector
                def _(e, dc=odc, th=th, w=otr, sf=mk_otm_st, hh=h2hi_prev):
                    e.wait_ge(h_pe, w)
                    e.wait_ge(h_st, sf)
                    e.wait_ge(h_ac, hh)
                    e.drain()
                    e.tensor_copy(o_tm[:, 0:128],
                                  p_tr2[0:128, 0:128]).then_inc(h_ve, 1)
                ve += 1
                mk_ptr2_evac = ve
                otm_done = ve

                @B.scalar
                def _(e, dc=odc, th=th, w=otm_done, ps=st):
                    e.wait_ge(h_st, ps)
                    e.wait_ge(h_ve, w)
                    e.dma_start(
                        out=bass.AP(out_h, (t0 + th * 128) * DSL + dc * 128,
                                    [[DSL, 128], [1, 128]]),
                        in_=o_tm[:, 0:128]).then_inc(h_st, 16)
                st += 16
                mk_otm_st = st

        @B.scalar
        def _(e, w=st):
            e.wait_ge(h_st, w)

        @B.sync
        def _(e, w=ws):
            e.wait_ge(h_ws, w)

    return nc



def _shard_inputs(inputs, cfg: Cfg, n_cores=8):
    x_full = np.asarray(inputs["L0_attn_x"], dtype=np.float32)
    alog = np.asarray(inputs["L0_attn_A_log"], dtype=np.float32)
    dvec = np.asarray(inputs["L0_attn_D"], dtype=np.float32)
    xpw = np.asarray(inputs["L0_attn_x_proj_w"], dtype=np.float32)
    dtw = np.asarray(inputs["L0_attn_dt_proj_w"], dtype=np.float32)
    dtb = np.asarray(inputs["L0_attn_dt_proj_b"], dtype=np.float32)
    bw = np.asarray(inputs["L0_ffn_base_w"], dtype=np.float32)
    bb = np.asarray(inputs["L0_ffn_base_b"], dtype=np.float32)
    amat = np.asarray(inputs["L0_ffn_A"], dtype=np.float32)
    bmat = np.asarray(inputs["L0_ffn_B"], dtype=np.float32)
    rw = np.asarray(inputs["L0_ffn_router_w"], dtype=np.float32)

    DM, DSL, N, RK, E, R = cfg.DM, cfg.DSL, cfg.N, cfg.RK, cfg.E, cfg.R
    ident = np.eye(128, dtype=np.float32)
    in_maps = []
    n_j = DM // DSL
    n_s = n_cores // n_j
    for c in range(n_cores):
        s = c // n_j
        j = c % n_j
        dsl = np.arange(j * DSL, (j + 1) * DSL)
        rest = np.concatenate([np.arange(0, j * DSL),
                               np.arange((j + 1) * DSL, DM)])
        perm = np.concatenate([dsl, rest])
        auxw = np.concatenate([rw, xpw[RK:RK + 2 * N, :], amat], axis=0)
        im = {
            "x": np.ascontiguousarray(x_full[s][:, perm]),
            "bw_t": np.ascontiguousarray(bw[dsl, :][:, perm].T),
            "bm_t": np.ascontiguousarray(
                bmat[:, dsl, :].transpose(0, 2, 1).reshape(E * R, DSL)),
            "xpw_dt_t": np.ascontiguousarray(xpw[:RK, :][:, perm].T),
            "auxw_t": np.ascontiguousarray(auxw[:, perm].T),
            "dtw_t": np.ascontiguousarray(dtw[dsl, :].T),
            "dtb": np.ascontiguousarray(dtb[dsl]),
            "alog": np.ascontiguousarray(alog[dsl, :]),
            "dvec": np.ascontiguousarray(dvec[dsl]),
            "bb": np.ascontiguousarray(bb[dsl]),
            "ident": ident,
            "consts": np.broadcast_to(
                np.array([1e-5, 1.0], dtype=np.float32), (128, 2)).copy(),
        }
        in_maps.append(im)
    return in_maps


def kernel(**inputs):
    cfg = Cfg()
    nc = build_program(cfg)
    in_maps = _shard_inputs(inputs, cfg, n_cores=8)
    res = run_bass_kernel_spmd(nc, in_maps, core_ids=list(range(8)))
    T, DM, DSL = cfg.T, cfg.DM, cfg.DSL
    n_j = DM // DSL
    out = np.empty((2, T, DM), dtype=np.float32)
    for c in range(8):
        s, j = c // n_j, c % n_j
        out[s][:, j * DSL:(j + 1) * DSL] = res.results[c]["out"]
    return out


# version 09
# speedup vs baseline: 1.1494x; 1.1452x over previous
import numpy as np
import concourse.bass as bass
import concourse.mybir as mybir
from concourse.bass_utils import run_bass_kernel_spmd
from concourse.alu_op_type import AluOpType as OP

F32 = mybir.dt.float32
BF16 = mybir.dt.bfloat16
AF = mybir.ActivationFunctionType
AX = mybir.AxisListType


class Cfg:
    def __init__(self, T=4096, DM=4096, DSL=1024, N=16, RK=128, E=16, R=64,
                 TB=256, eps=1e-5):
        self.T, self.DM, self.DSL, self.N, self.RK, self.E, self.R, self.TB = \
            T, DM, DSL, N, RK, E, R, TB
        self.eps = eps
        self.NCH = T // TB
        self.KCH = DM // 128
        self.DC = DSL // 128
        self.NTT = TB // 128
        self.AUX = 2 * N + R + E
        self.KH = (E * R) // 128


def build_program(cfg: Cfg):
    nc = bass.Bass("TRN2", target_bir_lowering=False)
    T, DM, DSL, N, RK, E, R, TB = (cfg.T, cfg.DM, cfg.DSL, cfg.N, cfg.RK,
                                   cfg.E, cfg.R, cfg.TB)
    NCH, KCH, DC, NTT, AUX, KH = (cfg.NCH, cfg.KCH, cfg.DC, cfg.NTT,
                                  cfg.AUX, cfg.KH)
    eps = cfg.eps
    TB1 = TB + 1
    NLH = 8 if DM >= 2048 else 2
    HDM = DM // NLH
    HKC = KCH // NLH
    NKQ = max(1, (DM // NLH) // 512)
    SBW = (DM // NLH) // NKQ
    NB = NLH * NKQ

    x_h = nc.dram_tensor("x", [T, DM], F32, kind="ExternalInput")
    bw_h = nc.dram_tensor("bw_t", [DM, DSL], F32, kind="ExternalInput")
    bm_h = nc.dram_tensor("bm_t", [E * R, DSL], F32, kind="ExternalInput")
    xpd_h = nc.dram_tensor("xpw_dt_t", [DM, RK], F32, kind="ExternalInput")
    aux_h = nc.dram_tensor("auxw_t", [DM, AUX], F32, kind="ExternalInput")
    dtw_h = nc.dram_tensor("dtw_t", [RK, DSL], F32, kind="ExternalInput")
    dtb_h = nc.dram_tensor("dtb", [DSL], F32, kind="ExternalInput")
    alog_h = nc.dram_tensor("alog", [DSL, N], F32, kind="ExternalInput")
    dvec_h = nc.dram_tensor("dvec", [DSL], F32, kind="ExternalInput")
    bb_h = nc.dram_tensor("bb", [DSL], F32, kind="ExternalInput")
    id_h = nc.dram_tensor("ident", [128, 128], F32, kind="ExternalInput")
    cst_h = nc.dram_tensor("consts", [128, 2], F32, kind="ExternalInput")
    out_h = nc.dram_tensor("out", [T, DSL], F32, kind="ExternalOutput")
    Q = (KCH + KH) * 128
    wscr_hi = nc.dram_tensor("wscr_hi", [DC, 128, Q], BF16)
    wscr_lo = nc.dram_tensor("wscr_lo", [DC, 128, Q], BF16)
    xscr_hi = nc.dram_tensor("xscr_hi", [KCH // 4, 128, 4 * RK], BF16)
    xscr_lo = nc.dram_tensor("xscr_lo", [KCH // 4, 128, 4 * RK], BF16)
    ascr_hi = nc.dram_tensor("ascr_hi", [KCH // 4, 128, 4 * AUX], BF16)
    ascr_lo = nc.dram_tensor("ascr_lo", [KCH // 4, 128, 4 * AUX], BF16)

    from contextlib import ExitStack
    with ExitStack() as _stk:
        bnbuf = _stk.enter_context(nc.sbuf_tensor("bnbuf", [128, 6 * NB], F32))
        mv = _stk.enter_context(nc.sbuf_tensor("mv", [128, 2], F32))
        lnv = _stk.enter_context(nc.sbuf_tensor("lnv", [128, 1], F32))
        mpak = _stk.enter_context(nc.sbuf_tensor("mpak", [128, 2], F32))
        sig_sv = _stk.enter_context(nc.sbuf_tensor("sig_sv", [128, 2], F32))
        xnh = _stk.enter_context(nc.sbuf_tensor("xnh", [128, HDM], F32))
        hih = _stk.enter_context(nc.sbuf_tensor("hih", [128, HDM], BF16))
        loh = _stk.enter_context(nc.sbuf_tensor("loh", [128, HDM], BF16))
        xnA_hi = _stk.enter_context(nc.sbuf_tensor("xnA_hi", [128, KCH * TB], BF16))
        xnA_lo = _stk.enter_context(nc.sbuf_tensor("xnA_lo", [128, KCH * TB], BF16))
        xnB_hi = _stk.enter_context(nc.sbuf_tensor("xnB_hi", [128, (KCH // 2) * TB], BF16))
        xnB_lo = _stk.enter_context(nc.sbuf_tensor("xnB_lo", [128, (KCH // 2) * TB], BF16))
        wbuf_hi = _stk.enter_context(nc.sbuf_tensor("wbuf_hi", [128, 2 * Q], BF16))
        wbuf_lo = _stk.enter_context(nc.sbuf_tensor("wbuf_lo", [128, 2 * Q], BF16))
        xbuf_hi = _stk.enter_context(nc.sbuf_tensor("xbuf_hi", [128, 4 * RK], BF16))
        xbuf_lo = _stk.enter_context(nc.sbuf_tensor("xbuf_lo", [128, 4 * RK], BF16))
        abuf_hi = _stk.enter_context(nc.sbuf_tensor("abuf_hi", [128, 4 * AUX], BF16))
        abuf_lo = _stk.enter_context(nc.sbuf_tensor("abuf_lo", [128, 4 * AUX], BF16))
        dtraw = _stk.enter_context(nc.sbuf_tensor("dtraw", [128, DC * TB], F32))
        dtw_sb = _stk.enter_context(nc.sbuf_tensor("dtw_sb", [128, DSL], F32))
        a_sb = _stk.enter_context(nc.sbuf_tensor("a_sb", [128, DC * N], F32))
        dtb_sb = _stk.enter_context(nc.sbuf_tensor("dtb_sb", [128, DC], F32))
        dvec_sb = _stk.enter_context(nc.sbuf_tensor("dvec_sb", [128, DC], F32))
        bb_sb = _stk.enter_context(nc.sbuf_tensor("bb_sb", [128, DC], F32))
        id_sb = _stk.enter_context(nc.sbuf_tensor("id_sb", [128, 128], F32))
        cst_sb = _stk.enter_context(nc.sbuf_tensor("cst_sb", [128, 2], F32))
        dtin_sb = _stk.enter_context(nc.sbuf_tensor("dtin_sb", [128, TB], F32))
        aux_sb = _stk.enter_context(nc.sbuf_tensor("aux_sb", [AUX, TB], F32))
        lt = _stk.enter_context(nc.sbuf_tensor("lt", [128, E], F32))
        r1 = _stk.enter_context(nc.sbuf_tensor("r1", [128, 1], F32))
        r2 = _stk.enter_context(nc.sbuf_tensor("r2", [128, 1], F32))
        r3 = _stk.enter_context(nc.sbuf_tensor("r3", [128, 1], F32))
        exb = _stk.enter_context(nc.sbuf_tensor("exb", [128, E], F32))
        pb = _stk.enter_context(nc.sbuf_tensor("pb", [128, E], F32))
        msk = _stk.enter_context(nc.sbuf_tensor("msk", [128, E], F32))
        cmb_tm = _stk.enter_context(nc.sbuf_tensor("cmb_tm", [128, E], F32))
        cmb_sb = _stk.enter_context(nc.sbuf_tensor("cmb_sb", [E, TB], F32))
        cmb_rep = _stk.enter_context(nc.sbuf_tensor("cmb_rep", [128, TB], F32))
        rh2 = _stk.enter_context(nc.sbuf_tensor("rh2", [128, TB], F32))
        h2_hi = _stk.enter_context(nc.sbuf_tensor("h2_hi", [128, KH * TB], BF16))
        h2_lo = _stk.enter_context(nc.sbuf_tensor("h2_lo", [128, KH * TB], BF16))
        g_sb = _stk.enter_context(nc.sbuf_tensor("g_sb", [128, DC * TB], F32))
        msf = _stk.enter_context(nc.sbuf_tensor("msf", [2, TB], F32))
        m_rep = _stk.enter_context(nc.sbuf_tensor("m_rep", [128, TB], F32))
        r_rep = _stk.enter_context(nc.sbuf_tensor("r_rep", [128, TB], F32))
        cor_t = _stk.enter_context(nc.sbuf_tensor("cor_t", [128, TB], F32))
        ws_b = _stk.enter_context(nc.sbuf_tensor("ws_b", [128, DC], F32))
        ws_d = _stk.enter_context(nc.sbuf_tensor("ws_d", [128, 1], F32))
        ws_a = _stk.enter_context(nc.sbuf_tensor("ws_a", [128, 1], F32))
        b_rep = _stk.enter_context(nc.sbuf_tensor("b_rep", [128, N * TB], F32))
        c_rep = _stk.enter_context(nc.sbuf_tensor("c_rep", [128, N * TB], F32))
        dt_dc = _stk.enter_context(nc.sbuf_tensor("dt_dc", [128, TB], F32))
        z_dc = _stk.enter_context(nc.sbuf_tensor("z_dc", [128, TB], F32))
        xnsl = _stk.enter_context(nc.sbuf_tensor("xnsl", [128, TB], F32))
        da = _stk.enter_context(nc.sbuf_tensor("da", [128, N * TB1], F32))
        dbx = _stk.enter_context(nc.sbuf_tensor("dbx", [128, N * TB1], F32))
        state = _stk.enter_context(nc.sbuf_tensor("state", [128, DC * N], F32))
        p_dtin = _stk.enter_context(nc.psum_tensor("p_dtin", [128, 512], F32))
        p_aux = _stk.enter_context(nc.psum_tensor("p_aux", [128, 512], F32))
        p_ffn0 = _stk.enter_context(nc.psum_tensor("p_ffn0", [128, 512], F32))
        p_ffn1 = _stk.enter_context(nc.psum_tensor("p_ffn1", [128, 512], F32))
        p_dt0 = _stk.enter_context(nc.psum_tensor("p_dt0", [128, 512], F32))
        p_dt1 = _stk.enter_context(nc.psum_tensor("p_dt1", [128, 512], F32))
        p_tr = _stk.enter_context(nc.psum_tensor("p_tr", [128, 512], F32))
        p_tr2 = _stk.enter_context(nc.psum_tensor("p_tr2", [128, 512], F32))
        h_ld = _stk.enter_context(nc.semaphore("s_ld"))
        h_tp = _stk.enter_context(nc.semaphore("s_tp"))
        h_rp = _stk.enter_context(nc.semaphore("s_rp"))
        h_st = _stk.enter_context(nc.semaphore("s_st"))
        h_pe = _stk.enter_context(nc.semaphore("s_pe"))
        h_ve = _stk.enter_context(nc.semaphore("s_ve"))
        h_ac = _stk.enter_context(nc.semaphore("s_ac"))
        h_gp = _stk.enter_context(nc.semaphore("s_gp"))
        h_ws = _stk.enter_context(nc.semaphore("s_ws"))
        h_x = _stk.enter_context(nc.semaphore("s_x"))
        h_w0 = _stk.enter_context(nc.semaphore("s_w0"))
        h_w1 = _stk.enter_context(nc.semaphore("s_w1"))
        h_rb = _stk.enter_context(nc.semaphore("s_rb"))
        h_rc = _stk.enter_context(nc.semaphore("s_rc"))
        h_rm = _stk.enter_context(nc.semaphore("s_rm"))
        h_rcb = _stk.enter_context(nc.semaphore("s_rcb"))
        B = _stk.enter_context(nc.Block())
        ld = ve = ac = pe = gp = tp = rp = st = ws = 0
        xc = w0 = w1 = rb = rc = rm = rcb = 0
        h_w = [h_w0, h_w1]
        p_ffn = [p_ffn0, p_ffn1]
        p_dt = [p_dt0, p_dt1]

        @B.sync
        def _(e):
            e.dma_start(out=dtw_sb[:, :], in_=dtw_h.ap()[:, :]).then_inc(h_ld, 16)
            for (sb, hb) in ((dtb_sb, dtb_h), (dvec_sb, dvec_h), (bb_sb, bb_h)):
                e.dma_start(out=bass.AP(sb, 0, [[DC, 128], [1, DC]]),
                            in_=bass.AP(hb, 0, [[1, 128], [128, DC]]),
                            allow_slow_non_contiguous=True
                            ).then_inc(h_ld, 16)
            e.dma_start(out=bass.AP(a_sb, 0, [[DC * N, 128], [N, DC], [1, N]]),
                        in_=bass.AP(alog_h, 0, [[N, 128], [128 * N, DC], [1, N]])
                        ).then_inc(h_ld, 16)
            e.dma_start(out=id_sb[:, :], in_=id_h.ap()[:, :]).then_inc(h_ld, 16)
            e.dma_start(out=cst_sb[:, :], in_=cst_h.ap()[:, :]).then_inc(h_ld, 16)
        ld += 7 * 16
        nc.const_aps.register(F32, eps, cst_sb[:, 0:1])
        nc.const_aps.register(F32, 1.0, cst_sb[:, 1:2])
        ld_consts = ld

        @B.scalar
        def _(e, w=ld):
            e.wait_ge(h_ld, w)
            e.activation(out=a_sb[:, :], in_=a_sb[:, :], func=AF.Exp
                         ).then_inc(h_ac, 1)
        ac += 1

        @B.vector
        def _(e, w=ac):
            e.wait_ge(h_ac, w)
            e.tensor_scalar(a_sb[:, :], a_sb[:, :], -1.0, None, op0=OP.mult)
            e.memset(state[:, :], 0.0)
            e.drain()
            e.nop().then_inc(h_ve, 1)
        ve += 1
        a_ready = ve

        prev_sinks = []
        prev_lo = 0

        o_tm = cmb_rep
        CW = DSL // 2
        wsplit, wsp_hi, wsp_lo = xnh, hih, loh

        ws_started = [False]

        def split_stream(src_h, n_tiles, width, sink, ws_map=None,
                         ws_last=False):
            nonlocal ld, ve, ac, pe, prev_sinks, prev_lo
            ncb = (width + CW - 1) // CW
            for kt in range(n_tiles):
                for cb in range(ncb):
                    cw = min(CW, width - cb * CW)

                    @B.sync
                    def _(e, kt=kt, cb=cb, cw=cw, g=list(prev_sinks), pl2=ld):
                        e.wait_ge(h_ld, pl2)
                        for (sh, vv) in g:
                            e.wait_ge(sh, vv)
                        e.dma_start(out=wsplit[:, :cw],
                                    in_=src_h.ap()[kt * 128:(kt + 1) * 128,
                                                   cb * CW:cb * CW + cw]
                                    ).then_inc(h_ld, 16)
                    ld += 16

                    @B.scalar
                    def _(e, w=ld, pl=prev_lo, cw=cw):
                        e.wait_ge(h_ld, w)
                        e.wait_ge(h_ve, pl)
                        e.drain()
                        e.copy(out=wsp_hi[:, :cw], in_=wsplit[:, :cw]
                               ).then_inc(h_ac, 1)
                    ac += 1
                    hi_mark = ac

                    @B.vector
                    def _(e, w=ld, hm=hi_mark, cw=cw):
                        e.wait_ge(h_ld, w)
                        e.wait_ge(h_ac, hm)
                        e.drain()
                        e.tensor_tensor(wsp_lo[:, :cw], wsplit[:, :cw],
                                        wsp_hi[:, :cw], op=OP.subtract
                                        ).then_inc(h_ve, 1)
                    ve += 1
                    lo_mark = ve
                    prev_sinks = sink(kt, cb, cw, hi_mark, lo_mark)
                    if ws_map is not None:
                        nsub = (cw + 127) // 128

                        @B.tensor
                        def _(e, kt=kt, cb=cb, cw=cw, nsub=nsub, w=ld,
                              ar=a_ready):
                            e.wait_ge(h_ld, w)
                            e.wait_ge(h_ve, ar)
                            for sub in range(nsub):
                                sw = 128
                                col = ws_map(cb, sub)
                                st_ = not ws_started[0]
                                ws_started[0] = True
                                sp_ = (ws_last and kt == n_tiles - 1
                                       and sub == nsub - 1)
                                mm = e.matmul(p_tr2[0:sw, col:col + 1],
                                              wsplit[:, sub * 128:sub * 128 + sw],
                                              cst_sb[:, 1:2],
                                              start=st_, stop=sp_)
                                if sub == nsub - 1:
                                    mm.then_inc(h_pe, 1)
                        pe += 1
                        prev_sinks = list(prev_sinks) + [(h_pe, pe)]
                    prev_lo = lo_mark

        def scr_sink(qoff):
            def sink(kt, cb, cw, hm, lm):
                nonlocal ws
                marks = []
                ndg = cw // 128
                for (sb, scr, sh, sv) in ((wsp_hi, wscr_hi, h_ac, hm),
                                          (wsp_lo, wscr_lo, h_ve, lm)):
                    @B.sync
                    def _(e, sb=sb, scr=scr, sh=sh, sv=sv, kt=kt, cb=cb,
                          ndg=ndg, pw=ws):
                        e.wait_ge(h_ws, pw)
                        e.wait_ge(sh, sv)
                        e.dma_start(
                            out=bass.AP(scr, (qoff + kt) * 128
                                        + cb * (CW // 128) * 128 * Q,
                                        [[Q, 128], [128 * Q, ndg], [1, 128]]),
                            in_=bass.AP(sb, 0, [[HDM, 128], [128, ndg], [1, 128]])
                        ).then_inc(h_ws, 16)
                    ws += 16
                    marks.append((h_ws, ws))
                return marks
            return sink

        def gscr_sink(scr_hi, scr_lo, W):
            def sink(kt, cb, cw, hm, lm):
                nonlocal ws
                marks = []
                g, s = kt // 4, kt % 4
                for (sb, scr, sh, sv) in ((wsp_hi, scr_hi, h_ac, hm),
                                          (wsp_lo, scr_lo, h_ve, lm)):
                    @B.sync
                    def _(e, sb=sb, scr=scr, sh=sh, sv=sv, g=g, s=s, pw=ws):
                        e.wait_ge(h_ws, pw)
                        e.wait_ge(sh, sv)
                        e.dma_start(
                            out=bass.AP(scr, g * 128 * 4 * W + s * W,
                                        [[4 * W, 128], [1, W]]),
                            in_=bass.AP(sb, 0, [[HDM, 128], [1, W]])
                        ).then_inc(h_ws, 16)
                    ws += 16
                    marks.append((h_ws, ws))
                return marks
            return sink

        def resident_sink(hi_t, lo_t, width):
            def sink(kt, cb, cw, hm, lm):
                nonlocal ve, ac
                marks = []
                off = kt * width + cb * CW

                @B.scalar
                def _(e, off=off, cw=cw):
                    e.drain()
                    e.copy(out=hi_t[:, off:off + cw],
                           in_=wsp_hi[:, :cw]).then_inc(h_ac, 1)
                ac += 1
                marks.append((h_ac, ac))

                @B.vector
                def _(e, off=off, cw=cw):
                    e.drain()
                    e.tensor_copy(lo_t[:, off:off + cw],
                                  wsp_lo[:, :cw]).then_inc(h_ve, 1)
                ve += 1
                marks.append((h_ve, ve))
                return marks
            return sink

        split_stream(bw_h, KCH, DSL, scr_sink(0),
                     ws_map=lambda cb, sub: cb * (CW // 128) + sub)
        split_stream(bm_h, KH, DSL, scr_sink(KCH))
        split_stream(xpd_h, KCH, RK, gscr_sink(xscr_hi, xscr_lo, RK),
                     ws_map=lambda cb, sub: DC)
        split_stream(aux_h, KCH, AUX, gscr_sink(ascr_hi, ascr_lo, AUX),
                     ws_map=lambda cb, sub: DC + 1, ws_last=True)

        @B.vector
        def _(e, w=pe):
            e.wait_ge(h_pe, w)
            e.drain()
            e.tensor_copy(ws_b[:, :], p_tr2[0:128, 0:DC])
            e.tensor_copy(ws_d[:, :], p_tr2[0:128, DC:DC + 1])
            e.tensor_copy(ws_a[:, :], p_tr2[0:128, DC + 1:DC + 2])
            e.drain()
            e.nop().then_inc(h_ve, 1)
        ve += 1
        ws_evac = ve

        @B.vector
        def _(e):
            e.drain()

        @B.scalar
        def _(e):
            e.drain()

        mk_xn_free_pe = 0
        prologue_ws = ws
        mk_wbuf_free = [0, 0]
        mk_ffn_evac = [0, 0]
        mk_dt_exp = [0, 0]
        mk_dtin_evac = 0
        mk_xnh_free = ve
        mk_half_free = 0
        mk_prev_hi = ac
        mk_rstd_prev = 0
        mk_mpak_pe = 0
        mk_msf_free = 0
        mk_brep_free = 0
        mk_crep_free = 0
        mk_rh2_free = 0
        mk_cmbsb_free = 0
        mk_aux_pe_free = 0
        mk_aux_rp_free = 0
        mk_aux_rb_free = 0
        mk_gsb_free = 0
        mk_mrep_free = 0
        mk_h2_free = 0
        mk_h2f_free = 0
        mk_odc_pe = 0
        mk_ptr2_evac = ws_evac
        mk_otm_st = 0
        mk_ptr_evac = 0
        mk_lt_free = 0
        mk_scan_prev = 0
        mk_hc_prev = 0
        mk_tree_prev = 0
        mk_z_free = 0
        mk_dt_z = 0
        mk_da_free = 0
        mk_aux_evac = 0
        dt_prev_ev = 0
        h2hi_prev = 0
        mk_sig_free = 0

        out_q = []
        mk_xnA_free = [0, 0]
        mk_xnA_scan = [0, 0]
        mk_xnB_free = 0
        mk_gout = [0] * DC
        mk_xabuf_free = 0
        dtev = [0] * DC
        dtsp = [0] * DC
        da_mark = [0] * DC
        z_mark = [0] * DC
        NG = KCH // 4
        HALF = KCH // 2

        def xn_hi_sl(kch, par):
            if kch < HALF:
                return xnA_hi[:, (par * HALF + kch) * TB:(par * HALF + kch + 1) * TB]
            return xnB_hi[:, (kch - HALF) * TB:(kch - HALF + 1) * TB]

        def xn_lo_sl(kch, par):
            if kch < HALF:
                return xnA_lo[:, (par * HALF + kch) * TB:(par * HALF + kch + 1) * TB]
            return xnB_lo[:, (kch - HALF) * TB:(kch - HALF + 1) * TB]

        def emit_ln(ci):
            nonlocal xc, ve, ac, pe, tp
            nonlocal mk_xnh_free, mk_prev_hi, mk_rstd_prev, mk_mpak_pe
            nonlocal mk_half_free, mk_ptr_evac
            t0 = ci * TB
            par = ci % 2
            c = ci
            msf_marks = []
            for tt in range(NTT):
                for q in range(NLH):
                    @B.sync
                    def _(e, tt=tt, q=q, w=mk_xnh_free, wh=mk_prev_hi,
                          px=xc, c_=c, pws=prologue_ws):
                        e.wait_ge(h_x, px)
                        e.wait_ge(h_ve, w)
                        e.wait_ge(h_ac, wh)
                        if c_ == 0 and tt == 0 and q == 0:
                            e.wait_ge(h_ws, pws)
                        e.dma_start(
                            out=xnh[:, :],
                            in_=x_h.ap()[t0 + tt * 128:t0 + tt * 128 + 128,
                                         q * HDM:(q + 1) * HDM]
                            ).then_inc(h_x, 16)
                    xc += 16
                    xld = xc

                    @B.vector
                    def _(e, q=q, w=xld, pr=mk_rstd_prev, pm=mk_mpak_pe):
                        e.wait_ge(h_x, w)
                        if q == 0:
                            e.wait_ge(h_ac, pr)
                            e.wait_ge(h_pe, pm)
                        e.drain()
                        for k in range(NKQ):
                            e.bn_stats(
                                bnbuf[:, 6 * (q * NKQ + k):
                                      6 * (q * NKQ + k) + 6],
                                xnh[:, SBW * k:SBW * (k + 1)])
                        e.drain()
                        e.nop().then_inc(h_ve, 1)
                    ve += 1

                    @B.scalar
                    def _(e, q=q, w=xld, fr=mk_half_free):
                        e.wait_ge(h_x, w)
                        e.wait_ge(h_tp, fr)
                        e.drain()
                        e.copy(out=hih[:, :], in_=xnh[:, :]).then_inc(h_ac, 1)
                    ac += 1
                    hi_done = ac
                    mk_prev_hi = hi_done

                    @B.vector
                    def _(e, w=hi_done, fr=mk_half_free):
                        e.wait_ge(h_ac, w)
                        e.wait_ge(h_tp, fr)
                        e.drain()
                        e.tensor_tensor(loh[:, :], xnh[:, :], hih[:, :],
                                        op=OP.subtract).then_inc(h_ve, 1)
                    ve += 1
                    lo_done = ve
                    mk_xnh_free = ve

                    to_A = q < NLH // 2
                    if to_A:
                        dsts = (xnA_hi, xnA_lo)
                        doff = (par * HALF + q * HKC) * TB + tt * 128
                        dpitch = KCH * TB
                        pgate = mk_xnA_free[par]
                        vgate = mk_xnA_scan[par]
                    else:
                        dsts = (xnB_hi, xnB_lo)
                        doff = ((q - NLH // 2) * HKC) * TB + tt * 128
                        dpitch = HALF * TB
                        pgate = mk_xnB_free
                        vgate = 0

                    @B.sync
                    def _(e, tt=tt, q=q, hd=hi_done, ld_=lo_done, pt=tp,
                          dsts=dsts, doff=doff, dpi=dpitch, pg=pgate,
                          vg=vgate):
                        e.wait_ge(h_tp, pt)
                        e.wait_ge(h_ac, hd)
                        e.wait_ge(h_ve, ld_)
                        e.wait_ge(h_pe, pg)
                        if vg:
                            e.wait_ge(h_ve, vg)
                        e.dma_start_transpose(
                            out=bass.AP(dsts[0], doff,
                                        [[dpi, 128], [TB, HKC], [1, 128]]),
                            in_=hih[:, :]).then_inc(h_tp, 16)
                        e.dma_start_transpose(
                            out=bass.AP(dsts[1], doff,
                                        [[dpi, 128], [TB, HKC], [1, 128]]),
                            in_=loh[:, :]).then_inc(h_tp, 16)
                    tp += 32
                    mk_half_free = tp

                @B.vector
                def _(e, pm=mk_mpak_pe):
                    e.wait_ge(h_pe, pm)
                    e.drain()
                    e.bn_aggr(mv[:, :], bnbuf[:, :])
                    e.drain()
                    e.tensor_copy(mpak[:, 0:1], mv[:, 0:1])
                    e.drain()
                    e.nop().then_inc(h_ve, 1)
                ve += 1
                bn_done = ve

                @B.scalar
                def _(e, tt=tt, w=bn_done, sg=mk_sig_free):
                    e.wait_ge(h_ve, w)
                    e.wait_ge(h_ve, sg)
                    e.wait_ge(h_ld, ld_consts)
                    e.drain()
                    e.activation(out=lnv[:, :], in_=mv[:, 1:2], func=AF.Ln,
                                 bias=eps)
                    e.drain()
                    e.activation(out=mpak[:, 1:2], in_=lnv[:, :], func=AF.Exp,
                                 scale=-0.5)
                    e.activation(out=sig_sv[:, tt:tt + 1], in_=lnv[:, :],
                                 func=AF.Exp, scale=0.5)
                    e.drain()
                    e.nop().then_inc(h_ac, 1)
                ac += 1
                rstd_done = ac
                mk_rstd_prev = rstd_done

                @B.tensor
                def _(e, tt=tt, w=rstd_done, pf=mk_ptr_evac):
                    e.wait_ge(h_ac, w)
                    e.wait_ge(h_ve, pf)
                    e.transpose(p_tr[0:2, 0:128], mpak[:, :], id_sb[:, :]
                                ).then_inc(h_pe, 1)
                pe += 1
                mk_mpak_pe = pe

                @B.vector
                def _(e, tt=tt, w=pe, mf=mk_msf_free):
                    e.wait_ge(h_pe, w)
                    e.wait_ge(h_rm, mf)
                    e.drain()
                    e.tensor_copy(msf[:, tt * 128:(tt + 1) * 128],
                                  p_tr[0:2, 0:128]).then_inc(h_ve, 1)
                ve += 1
                mk_ptr_evac = ve
                msf_marks.append(ve)

            return tp, msf_marks[-1]

        def emit_mrep(msf_last):
            nonlocal rm, mk_msf_free
            @B.scalar
            def _(e, w=msf_last, fr=mk_mrep_free, pm2=rm):
                e.wait_ge(h_rm, pm2)
                e.wait_ge(h_ve, w)
                e.wait_ge(h_ve, fr)
                e.dma_start(out=m_rep[:, :],
                            in_=bass.AP(msf, 0, [[TB, 1], [0, 128], [1, TB]])
                            ).then_inc(h_rm, 16)
                e.dma_start(out=r_rep[:, :],
                            in_=bass.AP(msf, TB, [[TB, 1], [0, 128], [1, TB]])
                            ).then_inc(h_rm, 16)
            rm += 32
            mrep_done = rm
            mk_msf_free = rm

            return mrep_done

        ln_tp, ln_msf = emit_ln(0)
        mrep_done = emit_mrep(ln_msf)

        for c in range(NCH):
            t0 = c * TB
            par = c % 2
            tp_done = ln_tp

            xp_first = True
            for gidx in range(NG):
                @B.sync
                def _(e, gidx=gidx, pl=ld, fr=mk_xabuf_free):
                    e.wait_ge(h_ld, pl)
                    e.wait_ge(h_pe, fr)
                    e.dma_start(out=xbuf_hi[:, :], in_=xscr_hi.ap()[gidx, :, :]
                                ).then_inc(h_ld, 16)
                    e.dma_start(out=xbuf_lo[:, :], in_=xscr_lo.ap()[gidx, :, :]
                                ).then_inc(h_ld, 16)
                    e.dma_start(out=abuf_hi[:, :], in_=ascr_hi.ap()[gidx, :, :]
                                ).then_inc(h_ld, 16)
                    e.dma_start(out=abuf_lo[:, :], in_=ascr_lo.ap()[gidx, :, :]
                                ).then_inc(h_ld, 16)
                ld += 64
                gld = ld

                @B.tensor
                def _(e, gidx=gidx, w=gld, tp_=tp_done, dtf=mk_dtin_evac,
                      axf=mk_aux_evac, par=par):
                    e.wait_ge(h_ld, w)
                    if gidx == 0:
                        e.wait_ge(h_tp, tp_)
                        e.wait_ge(h_ve, max(dtf, axf))
                    for i in range(4):
                        kch = gidx * 4 + i
                        first = kch == 0
                        last = kch == KCH - 1
                        rh_ = xn_hi_sl(kch, par)
                        rl_ = xn_lo_sl(kch, par)
                        wh = xbuf_hi[:, i * RK:(i + 1) * RK]
                        wl = xbuf_lo[:, i * RK:(i + 1) * RK]
                        e.matmul(p_dtin[0:RK, 0:TB], wh, rh_, start=first,
                                 stop=False)
                        e.matmul(p_dtin[0:RK, 0:TB], wh, rl_, start=False,
                                 stop=False)
                        e.matmul(p_dtin[0:RK, 0:TB], wl, rh_, start=False,
                                 stop=last)
                        wh = abuf_hi[:, i * AUX:(i + 1) * AUX]
                        wl = abuf_lo[:, i * AUX:(i + 1) * AUX]
                        e.matmul(p_aux[0:AUX, 0:TB], wh, rh_, start=first,
                                 stop=False)
                        e.matmul(p_aux[0:AUX, 0:TB], wh, rl_, start=False,
                                 stop=False)
                        mm = e.matmul(p_aux[0:AUX, 0:TB], wl, rh_, start=False,
                                      stop=last)
                    mm.then_inc(h_pe, 1)
                pe += 1
                mk_xabuf_free = pe
            xp1_done = pe
            aux_done = pe

            @B.vector
            def _(e, w=xp1_done, mr=mrep_done):
                e.wait_ge(h_pe, w)
                e.wait_ge(h_rm, mr)
                e.drain()
                e.tensor_scalar(cor_t[:, :], m_rep[:, :], ws_d[:, 0:1], None,
                                op0=OP.mult)
                e.drain()
                e.tensor_tensor(dtin_sb[:, :], p_dtin[0:RK, 0:TB],
                                cor_t[:, :], op=OP.subtract)
                e.drain()
                e.tensor_tensor(dtin_sb[:, :], dtin_sb[:, :], r_rep[:, :],
                                op=OP.mult).then_inc(h_ve, 1)
            ve += 1
            mk_dtin_evac = ve

            @B.vector
            def _(e, w=aux_done, f1=mk_aux_pe_free, f2=mk_aux_rp_free,
                  f3=mk_aux_rb_free):
                e.wait_ge(h_pe, w)
                e.wait_ge(h_pe, f1)
                e.wait_ge(h_rc, f2)
                e.wait_ge(h_rb, f3)
                e.wait_ge(h_rm, mrep_done)
                e.drain()
                e.tensor_scalar(cor_t[0:AUX, :], m_rep[0:AUX, :],
                                ws_a[0:AUX, 0:1], None, op0=OP.mult)
                e.drain()
                e.tensor_tensor(aux_sb[:, :], p_aux[0:AUX, 0:TB],
                                cor_t[0:AUX, :], op=OP.subtract)
                e.drain()
                e.tensor_tensor(aux_sb[:, :], aux_sb[:, :], r_rep[0:AUX, :],
                                op=OP.mult).then_inc(h_ve, 1)
            ve += 1
            mk_aux_evac = ve

            for dc in range(DC):
                dpar = dc % 2

                @B.tensor
                def _(e, dc=dc, dpar=dpar, w=mk_dtin_evac,
                      fe=dtev[dc - 2] if dc >= 2 else dt_prev_ev):
                    e.wait_ge(h_ve, max(w, fe))
                    e.matmul(p_dt[dpar][0:128, 0:TB],
                             dtw_sb[:, dc * 128:(dc + 1) * 128],
                             dtin_sb[:, :], start=True, stop=True
                             ).then_inc(h_pe, 1)
                pe += 1
                dtmm = pe

                @B.vector
                def _(e, dc=dc, dpar=dpar, w=dtmm, fa=da_mark[dc],
                      fz=z_mark[dc]):
                    e.wait_ge(h_pe, w)
                    e.wait_ge(h_ac, fa)
                    e.wait_ge(h_ve, fz)
                    e.drain()
                    e.tensor_copy(dtraw[:, dc * TB:(dc + 1) * TB],
                                  p_dt[dpar][0:128, 0:TB]).then_inc(h_ve, 1)
                ve += 1
                dtev[dc] = ve

                @B.scalar
                def _(e, dc=dc, w=dtev[dc]):
                    e.wait_ge(h_ve, w)
                    e.drain()
                    e.activation(out=dtraw[:, dc * TB:(dc + 1) * TB],
                                 in_=dtraw[:, dc * TB:(dc + 1) * TB],
                                 func=AF.Exp, bias=dtb_sb[:, dc:dc + 1])
                    e.drain()
                    e.activation(out=dtraw[:, dc * TB:(dc + 1) * TB],
                                 in_=dtraw[:, dc * TB:(dc + 1) * TB],
                                 func=AF.Ln, bias=1.0).then_inc(h_ac, 1)
                ac += 1
                dtsp[dc] = ac
            dt_prev_ev = dtev[DC - 2]

            @B.sync
            def _(e, w=mk_aux_evac, fb=mk_brep_free, fr_=mk_rh2_free, pb_=rb):
                e.wait_ge(h_rb, pb_)
                e.wait_ge(h_ve, w)
                e.wait_ge(h_gp, fb)
                e.wait_ge(h_ve, fr_)
                for n in range(N):
                    e.dma_start(out=b_rep[:, n * TB:(n + 1) * TB],
                                in_=bass.AP(aux_sb, (E + n) * TB,
                                            [[TB, 1], [0, 128], [1, TB]])
                                ).then_inc(h_rb, 16)
                for half in range(2):
                    e.dma_start(out=rh2[half * R:(half + 1) * R, :],
                                in_=bass.AP(aux_sb, (E + 2 * N) * TB,
                                            [[TB, R], [1, TB]])
                                ).then_inc(h_rb, 16)
            rb += 16 * (N + 2)
            rh2_done = rb

            @B.scalar
            def _(e, w=mk_aux_evac, fc=mk_crep_free, pc=rc):
                e.wait_ge(h_rc, pc)
                e.wait_ge(h_ve, max(w, fc))
                for n in range(N):
                    e.dma_start(out=c_rep[:, n * TB:(n + 1) * TB],
                                in_=bass.AP(aux_sb, (E + N + n) * TB,
                                            [[TB, 1], [0, 128], [1, TB]])
                                ).then_inc(h_rc, 16)
            rc += 16 * N
            crep_done = rc
            brep_done = rh2_done
            mk_aux_rp_free = rc
            mk_aux_rb_free = rb

            cmb_done_mark = 0
            for tt in range(NTT):
                @B.tensor
                def _(e, tt=tt, w=mk_aux_evac, pf=mk_ptr_evac):
                    e.wait_ge(h_ve, max(w, pf))
                    e.transpose(p_tr[0:128, 0:E],
                                aux_sb[0:E, tt * 128:(tt + 1) * 128],
                                id_sb[0:E, 0:E]).then_inc(h_pe, 1)
                pe += 1
                ltr = pe
                mk_aux_pe_free = pe

                @B.vector
                def _(e, w=ltr, lf=mk_lt_free):
                    e.wait_ge(h_pe, w)
                    e.wait_ge(h_ac, lf)
                    e.drain()
                    e.tensor_copy(lt[:, :], p_tr[0:128, 0:E]).then_inc(h_ve, 1)
                ve += 1
                mk_ptr_evac = ve

                @B.vector
                def _(e):
                    e.drain()
                    e.tensor_reduce(r1[:, :], lt[:, :], axis=AX.X, op=OP.max,
                                    negate=True)
                    e.drain()
                    e.nop().then_inc(h_ve, 1)
                ve += 1
                nmx_done = ve

                @B.scalar
                def _(e, w=nmx_done):
                    e.wait_ge(h_ve, w)
                    e.drain()
                    e.activation(out=exb[:, :], in_=lt[:, :], func=AF.Exp,
                                 bias=r1[:, 0:1]).then_inc(h_ac, 1)
                ac += 1
                exp_done = ac
                mk_lt_free = ac

                @B.vector
                def _(e, tt=tt, w=exp_done):
                    e.wait_ge(h_ac, w)
                    e.drain()
                    e.tensor_reduce(r2[:, :], exb[:, :], axis=AX.X, op=OP.add)
                    e.drain()
                    e.reciprocal(r2[:, :], r2[:, :])
                    e.drain()
                    e.tensor_scalar(pb[:, :], exb[:, :], r2[:, 0:1], None,
                                    op0=OP.mult)
                    e.drain()
                    e.tensor_reduce(r1[:, :], pb[:, :], axis=AX.X, op=OP.max)
                    e.drain()
                    e.tensor_scalar(msk[:, :], pb[:, :], r1[:, 0:1], None,
                                    op0=OP.is_ge)
                    e.drain()
                    e.scalar_tensor_tensor(exb[:, :], msk[:, :], -2.0, pb[:, :],
                                           op0=OP.mult, op1=OP.add)
                    e.drain()
                    e.tensor_reduce(r2[:, :], exb[:, :], axis=AX.X, op=OP.max)
                    e.drain()
                    e.tensor_scalar(msk[:, :], pb[:, :], r2[:, 0:1], None,
                                    op0=OP.is_ge)
                    e.tensor_tensor(r3[:, :], r1[:, :], r2[:, :], op=OP.add)
                    e.drain()
                    e.tensor_scalar(r3[:, :], r3[:, :], 2.0, 2e-6,
                                    op0=OP.mult, op1=OP.add)
                    e.drain()
                    e.reciprocal(r3[:, :], r3[:, :])
                    e.drain()
                    e.tensor_tensor(cmb_tm[:, :], pb[:, :], msk[:, :], op=OP.mult)
                    e.drain()
                    e.tensor_scalar(cmb_tm[:, :], cmb_tm[:, :], r3[:, 0:1],
                                    sig_sv[:, tt:tt + 1],
                                    op0=OP.mult, op1=OP.mult)
                    e.drain()
                    e.nop().then_inc(h_ve, 1)
                ve += 1
                cmbv = ve

                @B.tensor
                def _(e, w=cmbv, pf=mk_ptr_evac):
                    e.wait_ge(h_ve, max(w, pf))
                    e.transpose(p_tr[0:E, 0:128], cmb_tm[:, :], id_sb[:, :]
                                ).then_inc(h_pe, 1)
                pe += 1
                ctr = pe

                @B.vector
                def _(e, tt=tt, w=ctr, cf=mk_cmbsb_free):
                    e.wait_ge(h_pe, w)
                    e.wait_ge(h_rcb, cf)
                    e.drain()
                    e.tensor_copy(cmb_sb[:, tt * 128:(tt + 1) * 128],
                                  p_tr[0:E, 0:128]).then_inc(h_ve, 1)
                ve += 1
                mk_ptr_evac = ve
                cmb_done_mark = ve

            h2hi_last = h2lo_last = 0
            crp_last = 0
            for eg in range(KH):
                @B.scalar
                def _(e, eg=eg, w=cmb_done_mark, fr=mk_h2f_free, pcb=rcb,
                      so=mk_otm_st):
                    e.wait_ge(h_rcb, pcb)
                    e.wait_ge(h_st, so)
                    e.wait_ge(h_ve, max(w, fr))
                    e.dma_start(out=cmb_rep[:, :],
                                in_=bass.AP(cmb_sb, (2 * eg) * TB,
                                            [[TB, 2], [0, R], [1, TB]])
                                ).then_inc(h_rcb, 16)
                rcb += 16
                crp = rcb
                crp_last = rcb

                @B.vector
                def _(e, eg=eg, w=crp, r=rh2_done, hf=mk_h2f_free):
                    e.wait_ge(h_rcb, w)
                    e.wait_ge(h_rb, r)
                    e.wait_ge(h_ve, hf)
                    e.drain()
                    e.tensor_tensor(cmb_rep[:, :], cmb_rep[:, :], rh2[:, :],
                                    op=OP.mult).then_inc(h_ve, 1)
                ve += 1
                h2f_done = ve
                if eg == KH - 1:
                    mk_rh2_free = ve

                @B.scalar
                def _(e, eg=eg, w=h2f_done, pf=mk_h2_free):
                    e.wait_ge(h_ve, w)
                    e.wait_ge(h_pe, pf)
                    e.drain()
                    e.copy(out=h2_hi[:, eg * TB:(eg + 1) * TB],
                           in_=cmb_rep[:, :]).then_inc(h_ac, 1)
                ac += 1
                h2hi_last = ac

                @B.vector
                def _(e, eg=eg, w=h2hi_last, pf=mk_h2_free):
                    e.wait_ge(h_ac, w)
                    e.wait_ge(h_pe, pf)
                    e.drain()
                    e.tensor_tensor(h2_lo[:, eg * TB:(eg + 1) * TB],
                                    cmb_rep[:, :],
                                    h2_hi[:, eg * TB:(eg + 1) * TB],
                                    op=OP.subtract).then_inc(h_ve, 1)
                ve += 1
                h2lo_last = ve
                mk_h2f_free = ve
            mk_cmbsb_free = crp_last
            h2hi_prev = h2hi_last

            for (odc, ovm) in out_q:
                for th in range(NTT):
                    @B.tensor
                    def _(e, dc=odc, th=th, w=ovm, pf=mk_ptr2_evac):
                        e.wait_ge(h_ve, max(w, pf))
                        e.transpose(p_tr2[0:128, 0:128],
                                    g_sb[:, dc * TB + th * 128:
                                         dc * TB + (th + 1) * 128],
                                    id_sb[:, :]).then_inc(h_pe, 1)
                    pe += 1
                    otr = pe

                    @B.vector
                    def _(e, dc=odc, th=th, w=otr, sf=mk_otm_st,
                          hh=h2hi_prev):
                        e.wait_ge(h_pe, w)
                        e.wait_ge(h_st, sf)
                        e.wait_ge(h_ac, hh)
                        e.drain()
                        e.tensor_copy(o_tm[:, 0:128],
                                      p_tr2[0:128, 0:128]).then_inc(h_ve, 1)
                    ve += 1
                    mk_ptr2_evac = ve
                    otm_done = ve

                    @B.scalar
                    def _(e, dc=odc, th=th, w=otm_done, ps=st, tprev=t0 - TB):
                        e.wait_ge(h_st, ps)
                        e.wait_ge(h_ve, w)
                        e.dma_start(
                            out=bass.AP(out_h,
                                        (tprev + th * 128) * DSL + dc * 128,
                                        [[DSL, 128], [1, 128]]),
                            in_=o_tm[:, 0:128]).then_inc(h_st, 16)
                    st += 16
                    mk_otm_st = st
                mk_gout[odc] = pe
            out_q = []

            g_marks = []
            for dg in range(DC):
                fpar = dg % 2
                wcnt = [w0, w1]
                for (scr, wb) in ((wscr_hi, wbuf_hi), (wscr_lo, wbuf_lo)):
                    @B.sync
                    def _(e, scr=scr, wb=wb, dg=dg, wpar=fpar,
                          fr=mk_wbuf_free[fpar], pw=wcnt[fpar], c_=c, wsv=ws):
                        e.wait_ge(h_w[wpar], pw)
                        if c_ == 0:
                            e.wait_ge(h_ws, wsv)
                        e.wait_ge(h_pe, fr)
                        e.dma_start(
                            out=wb[:, wpar * Q:(wpar + 1) * Q],
                            in_=scr.ap()[dg, :, :]).then_inc(h_w[wpar], 16)
                    if fpar == 0:
                        w0 += 16
                    else:
                        w1 += 16
                wld = (fpar, w0 if fpar == 0 else w1)

                @B.tensor
                def _(e, dg=dg, fpar=fpar, wm=wld, par=par,
                      fe=mk_ffn_evac[fpar], hh=h2hi_last, hl=h2lo_last):
                    e.wait_ge(h_w[wm[0]], wm[1])
                    e.wait_ge(h_ac, hh)
                    e.wait_ge(h_ve, max(fe, hl))
                    pf = p_ffn[fpar]
                    wo = fpar * Q
                    for kch in range(KCH):
                        wh = wbuf_hi[:, wo + kch * 128:wo + (kch + 1) * 128]
                        wl = wbuf_lo[:, wo + kch * 128:wo + (kch + 1) * 128]
                        rh_ = xn_hi_sl(kch, par)
                        rl_ = xn_lo_sl(kch, par)
                        e.matmul(pf[0:128, 0:TB], wh, rh_, start=(kch == 0),
                                 stop=False)
                        e.matmul(pf[0:128, 0:TB], wh, rl_, start=False,
                                 stop=False)
                        e.matmul(pf[0:128, 0:TB], wl, rh_, start=False,
                                 stop=False)
                    for kh in range(KH):
                        wh = wbuf_hi[:, wo + (KCH + kh) * 128:wo + (KCH + kh + 1) * 128]
                        wl = wbuf_lo[:, wo + (KCH + kh) * 128:wo + (KCH + kh + 1) * 128]
                        rh_ = h2_hi[:, kh * TB:(kh + 1) * TB]
                        rl_ = h2_lo[:, kh * TB:(kh + 1) * TB]
                        e.matmul(pf[0:128, 0:TB], wh, rh_, start=False,
                                 stop=False)
                        e.matmul(pf[0:128, 0:TB], wh, rl_, start=False,
                                 stop=False)
                        mm = e.matmul(pf[0:128, 0:TB], wl, rh_, start=False,
                                      stop=(kh == KH - 1))
                        if kh == KH - 1:
                            mm.then_inc(h_pe, 1)
                pe += 1
                ffn_done = pe
                mk_wbuf_free[fpar] = ffn_done
                if dg == DC - 1:
                    mk_xnB_free = ffn_done
                    mk_xnA_free[par] = ffn_done
                    mk_h2_free = ffn_done

                @B.vector
                def _(e, dg=dg, fpar=fpar, w=ffn_done, gf=mk_gout[dg],
                      mr=mrep_done):
                    e.wait_ge(h_pe, w)
                    e.wait_ge(h_pe, gf)
                    e.wait_ge(h_rm, mr)
                    e.drain()
                    e.tensor_scalar(cor_t[:, :], m_rep[:, :],
                                    ws_b[:, dg:dg + 1], None, op0=OP.mult)
                    e.drain()
                    e.tensor_tensor(g_sb[:, dg * TB:(dg + 1) * TB],
                                    p_ffn[fpar][0:128, 0:TB], cor_t[:, :],
                                    op=OP.subtract)
                    e.drain()
                    e.tensor_tensor(g_sb[:, dg * TB:(dg + 1) * TB],
                                    g_sb[:, dg * TB:(dg + 1) * TB],
                                    r_rep[:, :], op=OP.mult).then_inc(h_ve, 1)
                ve += 1
                mk_ffn_evac[fpar] = ve

                @B.scalar
                def _(e, dg=dg, w=ve):
                    e.wait_ge(h_ve, w)
                    e.drain()
                    e.activation(out=g_sb[:, dg * TB:(dg + 1) * TB],
                                 in_=g_sb[:, dg * TB:(dg + 1) * TB],
                                 func=AF.Silu,
                                 bias=bb_sb[:, dg:dg + 1]).then_inc(h_ac, 1)
                ac += 1
                g_marks.append(ac)

            if c + 1 < NCH:
                mk_sig_free = cmb_done_mark
                ln_tp, ln_msf = emit_ln(c + 1)

            for dc in range(DC):
                @B.vector
                def _(e, dc=dc, w=dtsp[dc], tp_=tp_done, zf=mk_z_free,
                      op_=mk_gout[dc] if False else 0, mr=mrep_done, par=par):
                    e.wait_ge(h_ac, w)
                    e.wait_ge(h_tp, tp_)
                    e.wait_ge(h_gp, zf)
                    e.wait_ge(h_rm, mr)
                    e.drain()
                    e.tensor_tensor(cor_t[:, :], xn_hi_sl(dc, par),
                                    xn_lo_sl(dc, par), op=OP.add)
                    e.drain()
                    e.tensor_tensor(xnsl[:, :], cor_t[:, :], m_rep[:, :],
                                    op=OP.subtract)
                    e.drain()
                    e.tensor_tensor(xnsl[:, :], xnsl[:, :], r_rep[:, :],
                                    op=OP.mult)
                    e.drain()
                    e.tensor_tensor(z_dc[:, :], dtraw[:, dc * TB:(dc + 1) * TB],
                                    xnsl[:, :], op=OP.mult)
                    e.drain()
                    e.nop().then_inc(h_ve, 1)
                ve += 1
                z_done = ve
                z_mark[dc] = ve
                if dc == DC - 1:
                    mk_xnA_scan[par] = ve

                @B.vector
                def _(e, dc=dc):
                    e.drain()
                    e.memset(bass.AP(da, 0, [[N * TB1, 128], [TB1, N], [1, 1]]),
                             0.0)
                    e.drain()
                    e.nop().then_inc(h_ve, 1)
                ve += 1
                slots_done = ve

                @B.scalar
                def _(e, dc=dc, sd=slots_done, ar=a_ready):
                    e.wait_ge(h_ve, max(sd, ar))
                    e.drain()
                    for n in range(N):
                        e.activation(out=da[:, n * TB1 + 1:(n + 1) * TB1],
                                     in_=dtraw[:, dc * TB:(dc + 1) * TB],
                                     func=AF.Exp,
                                     scale=a_sb[:, dc * N + n:dc * N + n + 1])
                    e.drain()
                    e.nop().then_inc(h_ac, 1)
                ac += 1
                da_done = ac
                da_mark[dc] = ac

                @B.gpsimd
                def _(e, dc=dc, zm=z_done, bm_=brep_done, sm=mk_scan_prev,
                      tv=mk_tree_prev):
                    e.wait_ge(h_ve, max(zm, sm, tv))
                    e.wait_ge(h_rb, bm_)
                    e.drain()
                    for n in range(N):
                        e.tensor_tensor(dbx[:, n * TB1 + 1:(n + 1) * TB1],
                                        z_dc[:, :], b_rep[:, n * TB:(n + 1) * TB],
                                        op=OP.mult)
                    e.tensor_copy(
                        bass.AP(dbx, 0, [[N * TB1, 128], [TB1, N], [1, 1]]),
                        bass.AP(state, dc * N, [[DC * N, 128], [1, N], [1, 1]]))
                    e.drain()
                    e.nop().then_inc(h_gp, 1)
                gp += 1
                dbx_done = gp
                mk_z_free = gp

                @B.vector
                def _(e, dc=dc, dm=da_done, bm_=dbx_done):
                    e.wait_ge(h_ac, dm)
                    e.wait_ge(h_gp, bm_)
                    e.drain()
                    e.tensor_tensor_scan(dbx[:, :], da[:, :], dbx[:, :], 0.0,
                                         op0=OP.mult, op1=OP.add)
                    e.drain()
                    e.tensor_copy(
                        bass.AP(state, dc * N, [[DC * N, 128], [1, N], [1, 1]]),
                        bass.AP(dbx, TB, [[N * TB1, 128], [TB1, N], [1, 1]]))
                    e.drain()
                    e.nop().then_inc(h_ve, 1)
                ve += 1
                scan_done = ve
                mk_scan_prev = ve

                @B.vector
                def _(e, dc=dc, w=scan_done, cm=crep_done, gm=g_marks[dc]):
                    e.wait_ge(h_rc, cm)
                    e.wait_ge(h_ac, gm)
                    e.drain()
                    for n in range(N):
                        e.tensor_tensor(da[:, n * TB:(n + 1) * TB],
                                        dbx[:, n * TB1 + 1:(n + 1) * TB1],
                                        c_rep[:, n * TB:(n + 1) * TB],
                                        op=OP.mult)
                    e.drain()
                    e.tensor_tensor(dbx[:, 0:(N // 2) * TB],
                                    da[:, 0:(N // 2) * TB],
                                    da[:, (N // 2) * TB:N * TB], op=OP.add)
                    e.drain()
                    e.nop().then_inc(h_ve, 1)
                ve += 1
                hc_done = ve
                mk_crep_free = ve
                mk_hc_prev = ve

                @B.vector
                def _(e, dc=dc):
                    e.drain()
                    w4 = (N // 4) * TB
                    e.tensor_tensor(dbx[:, 0:w4], dbx[:, 0:w4],
                                    dbx[:, w4:2 * w4], op=OP.add)
                    e.drain()
                    w8 = (N // 8) * TB
                    e.tensor_tensor(dbx[:, 0:w8], dbx[:, 0:w8],
                                    dbx[:, w8:2 * w8], op=OP.add)
                    e.drain()
                    w16 = (N // 16) * TB
                    e.tensor_tensor(z_dc[:, 0:w16], dbx[:, 0:w16],
                                    dbx[:, w16:2 * w16], op=OP.add)
                    e.drain()
                    e.scalar_tensor_tensor(dt_dc[:, :], xnsl[:, :],
                                           dvec_sb[:, dc:dc + 1], z_dc[:, :],
                                           op0=OP.mult, op1=OP.add)
                    e.drain()
                    e.tensor_tensor(g_sb[:, dc * TB:(dc + 1) * TB],
                                    dt_dc[:, :],
                                    g_sb[:, dc * TB:(dc + 1) * TB], op=OP.mult)
                    e.drain()
                    e.tensor_tensor(g_sb[:, dc * TB:(dc + 1) * TB],
                                    g_sb[:, dc * TB:(dc + 1) * TB],
                                    cor_t[:, :], op=OP.add)
                    e.drain()
                    e.nop().then_inc(h_ve, 1)
                ve += 1
                mk_tree_prev = ve
                out_q.append((dc, ve))

            mk_mrep_free = z_mark[DC - 1]
            if c + 1 < NCH:
                mrep_done = emit_mrep(ln_msf)

        t0 = (NCH - 1) * TB
        for (odc, ovm) in out_q:
            for th in range(NTT):
                @B.tensor
                def _(e, dc=odc, th=th, w=ovm, pf=mk_ptr2_evac):
                    e.wait_ge(h_ve, max(w, pf))
                    e.transpose(p_tr2[0:128, 0:128],
                                g_sb[:, dc * TB + th * 128:
                                     dc * TB + (th + 1) * 128],
                                id_sb[:, :]).then_inc(h_pe, 1)
                pe += 1
                otr = pe

                @B.vector
                def _(e, dc=odc, th=th, w=otr, sf=mk_otm_st, hh=h2hi_prev):
                    e.wait_ge(h_pe, w)
                    e.wait_ge(h_st, sf)
                    e.wait_ge(h_ac, hh)
                    e.drain()
                    e.tensor_copy(o_tm[:, 0:128],
                                  p_tr2[0:128, 0:128]).then_inc(h_ve, 1)
                ve += 1
                mk_ptr2_evac = ve
                otm_done = ve

                @B.scalar
                def _(e, dc=odc, th=th, w=otm_done, ps=st):
                    e.wait_ge(h_st, ps)
                    e.wait_ge(h_ve, w)
                    e.dma_start(
                        out=bass.AP(out_h, (t0 + th * 128) * DSL + dc * 128,
                                    [[DSL, 128], [1, 128]]),
                        in_=o_tm[:, 0:128]).then_inc(h_st, 16)
                st += 16
                mk_otm_st = st

        @B.scalar
        def _(e, w=st):
            e.wait_ge(h_st, w)

        @B.sync
        def _(e, w=ws):
            e.wait_ge(h_ws, w)

    return nc



def _shard_inputs(inputs, cfg: Cfg, n_cores=8):
    x_full = np.asarray(inputs["L0_attn_x"], dtype=np.float32)
    alog = np.asarray(inputs["L0_attn_A_log"], dtype=np.float32)
    dvec = np.asarray(inputs["L0_attn_D"], dtype=np.float32)
    xpw = np.asarray(inputs["L0_attn_x_proj_w"], dtype=np.float32)
    dtw = np.asarray(inputs["L0_attn_dt_proj_w"], dtype=np.float32)
    dtb = np.asarray(inputs["L0_attn_dt_proj_b"], dtype=np.float32)
    bw = np.asarray(inputs["L0_ffn_base_w"], dtype=np.float32)
    bb = np.asarray(inputs["L0_ffn_base_b"], dtype=np.float32)
    amat = np.asarray(inputs["L0_ffn_A"], dtype=np.float32)
    bmat = np.asarray(inputs["L0_ffn_B"], dtype=np.float32)
    rw = np.asarray(inputs["L0_ffn_router_w"], dtype=np.float32)

    DM, DSL, N, RK, E, R = cfg.DM, cfg.DSL, cfg.N, cfg.RK, cfg.E, cfg.R
    ident = np.eye(128, dtype=np.float32)
    in_maps = []
    n_j = DM // DSL
    n_s = n_cores // n_j
    for c in range(n_cores):
        s = c // n_j
        j = c % n_j
        dsl = np.arange(j * DSL, (j + 1) * DSL)
        rest = np.concatenate([np.arange(0, j * DSL),
                               np.arange((j + 1) * DSL, DM)])
        perm = np.concatenate([dsl, rest])
        auxw = np.concatenate([rw, xpw[RK:RK + 2 * N, :], amat], axis=0)
        im = {
            "x": np.ascontiguousarray(x_full[s][:, perm]),
            "bw_t": np.ascontiguousarray(bw[dsl, :][:, perm].T),
            "bm_t": np.ascontiguousarray(
                bmat[:, dsl, :].transpose(0, 2, 1).reshape(E * R, DSL)),
            "xpw_dt_t": np.ascontiguousarray(xpw[:RK, :][:, perm].T),
            "auxw_t": np.ascontiguousarray(auxw[:, perm].T),
            "dtw_t": np.ascontiguousarray(dtw[dsl, :].T),
            "dtb": np.ascontiguousarray(dtb[dsl]),
            "alog": np.ascontiguousarray(alog[dsl, :]),
            "dvec": np.ascontiguousarray(dvec[dsl]),
            "bb": np.ascontiguousarray(bb[dsl]),
            "ident": ident,
            "consts": np.broadcast_to(
                np.array([1e-5, 1.0], dtype=np.float32), (128, 2)).copy(),
        }
        in_maps.append(im)
    return in_maps


def kernel(**inputs):
    cfg = Cfg()
    nc = build_program(cfg)
    in_maps = _shard_inputs(inputs, cfg, n_cores=8)
    res = run_bass_kernel_spmd(nc, in_maps, core_ids=list(range(8)))
    T, DM, DSL = cfg.T, cfg.DM, cfg.DSL
    n_j = DM // DSL
    out = np.empty((2, T, DM), dtype=np.float32)
    for c in range(8):
        s, j = c // n_j, c % n_j
        out[s][:, j * DSL:(j + 1) * DSL] = res.results[c]["out"]
    return out


# version 10 (final)
# speedup vs baseline: 1.1532x; 1.0033x over previous
import numpy as np
import concourse.bass as bass
import concourse.mybir as mybir
from concourse.bass_utils import run_bass_kernel_spmd
from concourse.alu_op_type import AluOpType as OP

F32 = mybir.dt.float32
BF16 = mybir.dt.bfloat16
AF = mybir.ActivationFunctionType
AX = mybir.AxisListType


class Cfg:
    def __init__(self, T=4096, DM=4096, DSL=1024, N=16, RK=128, E=16, R=64,
                 TB=256, eps=1e-5):
        self.T, self.DM, self.DSL, self.N, self.RK, self.E, self.R, self.TB = \
            T, DM, DSL, N, RK, E, R, TB
        self.eps = eps
        self.NCH = T // TB
        self.KCH = DM // 128
        self.DC = DSL // 128
        self.NTT = TB // 128
        self.AUX = 2 * N + R + E
        self.KH = (E * R) // 128


def build_program(cfg: Cfg):
    nc = bass.Bass("TRN2", target_bir_lowering=False)
    T, DM, DSL, N, RK, E, R, TB = (cfg.T, cfg.DM, cfg.DSL, cfg.N, cfg.RK,
                                   cfg.E, cfg.R, cfg.TB)
    NCH, KCH, DC, NTT, AUX, KH = (cfg.NCH, cfg.KCH, cfg.DC, cfg.NTT,
                                  cfg.AUX, cfg.KH)
    eps = cfg.eps
    TB1 = TB + 1
    NLH = 8 if DM >= 2048 else 2
    HDM = DM // NLH
    HKC = KCH // NLH
    NKQ = max(1, (DM // NLH) // 512)
    SBW = (DM // NLH) // NKQ
    NB = NLH * NKQ

    x_h = nc.dram_tensor("x", [T, DM], F32, kind="ExternalInput")
    bw_h = nc.dram_tensor("bw_t", [DM, DSL], F32, kind="ExternalInput")
    bm_h = nc.dram_tensor("bm_t", [E * R, DSL], F32, kind="ExternalInput")
    xpd_h = nc.dram_tensor("xpw_dt_t", [DM, RK], F32, kind="ExternalInput")
    aux_h = nc.dram_tensor("auxw_t", [DM, AUX], F32, kind="ExternalInput")
    dtw_h = nc.dram_tensor("dtw_t", [RK, DSL], F32, kind="ExternalInput")
    dtb_h = nc.dram_tensor("dtb", [DSL], F32, kind="ExternalInput")
    alog_h = nc.dram_tensor("alog", [DSL, N], F32, kind="ExternalInput")
    dvec_h = nc.dram_tensor("dvec", [DSL], F32, kind="ExternalInput")
    bb_h = nc.dram_tensor("bb", [DSL], F32, kind="ExternalInput")
    id_h = nc.dram_tensor("ident", [128, 128], F32, kind="ExternalInput")
    cst_h = nc.dram_tensor("consts", [128, 2], F32, kind="ExternalInput")
    out_h = nc.dram_tensor("out", [T, DSL], F32, kind="ExternalOutput")
    Q = (KCH + KH) * 128
    wscr_hi = nc.dram_tensor("wscr_hi", [DC, 128, Q], BF16)
    wscr_lo = nc.dram_tensor("wscr_lo", [DC, 128, Q], BF16)
    xscr_hi = nc.dram_tensor("xscr_hi", [KCH // 4, 128, 4 * RK], BF16)
    xscr_lo = nc.dram_tensor("xscr_lo", [KCH // 4, 128, 4 * RK], BF16)
    ascr_hi = nc.dram_tensor("ascr_hi", [KCH // 4, 128, 4 * AUX], BF16)
    ascr_lo = nc.dram_tensor("ascr_lo", [KCH // 4, 128, 4 * AUX], BF16)

    from contextlib import ExitStack
    with ExitStack() as _stk:
        bnbuf = _stk.enter_context(nc.sbuf_tensor("bnbuf", [128, 6 * NB * NTT], F32))
        mv = _stk.enter_context(nc.sbuf_tensor("mv", [128, 2], F32))
        lnv = _stk.enter_context(nc.sbuf_tensor("lnv", [128, 1], F32))
        mpak = _stk.enter_context(nc.sbuf_tensor("mpak", [128, 2], F32))
        sig_sv = _stk.enter_context(nc.sbuf_tensor("sig_sv", [128, 2], F32))
        xnh = _stk.enter_context(nc.sbuf_tensor("xnh", [128, HDM], F32))
        hih = _stk.enter_context(nc.sbuf_tensor("hih", [128, HDM], BF16))
        loh = _stk.enter_context(nc.sbuf_tensor("loh", [128, HDM], BF16))
        xnA_hi = _stk.enter_context(nc.sbuf_tensor("xnA_hi", [128, KCH * TB], BF16))
        xnA_lo = _stk.enter_context(nc.sbuf_tensor("xnA_lo", [128, KCH * TB], BF16))
        xnB_hi = _stk.enter_context(nc.sbuf_tensor("xnB_hi", [128, (KCH // 2) * TB], BF16))
        xnB_lo = _stk.enter_context(nc.sbuf_tensor("xnB_lo", [128, (KCH // 2) * TB], BF16))
        wbuf_hi = _stk.enter_context(nc.sbuf_tensor("wbuf_hi", [128, 2 * Q], BF16))
        wbuf_lo = _stk.enter_context(nc.sbuf_tensor("wbuf_lo", [128, 2 * Q], BF16))
        xbuf_hi = _stk.enter_context(nc.sbuf_tensor("xbuf_hi", [128, 4 * RK], BF16))
        xbuf_lo = _stk.enter_context(nc.sbuf_tensor("xbuf_lo", [128, 4 * RK], BF16))
        abuf_hi = _stk.enter_context(nc.sbuf_tensor("abuf_hi", [128, 4 * AUX], BF16))
        abuf_lo = _stk.enter_context(nc.sbuf_tensor("abuf_lo", [128, 4 * AUX], BF16))
        dtraw = _stk.enter_context(nc.sbuf_tensor("dtraw", [128, DC * TB], F32))
        dtw_sb = _stk.enter_context(nc.sbuf_tensor("dtw_sb", [128, DSL], F32))
        a_sb = _stk.enter_context(nc.sbuf_tensor("a_sb", [128, DC * N], F32))
        dtb_sb = _stk.enter_context(nc.sbuf_tensor("dtb_sb", [128, DC], F32))
        dvec_sb = _stk.enter_context(nc.sbuf_tensor("dvec_sb", [128, DC], F32))
        bb_sb = _stk.enter_context(nc.sbuf_tensor("bb_sb", [128, DC], F32))
        id_sb = _stk.enter_context(nc.sbuf_tensor("id_sb", [128, 128], F32))
        cst_sb = _stk.enter_context(nc.sbuf_tensor("cst_sb", [128, 2], F32))
        dtin_sb = _stk.enter_context(nc.sbuf_tensor("dtin_sb", [128, TB], F32))
        aux_sb = _stk.enter_context(nc.sbuf_tensor("aux_sb", [AUX, TB], F32))
        lt = _stk.enter_context(nc.sbuf_tensor("lt", [128, E], F32))
        r1 = _stk.enter_context(nc.sbuf_tensor("r1", [128, 1], F32))
        r2 = _stk.enter_context(nc.sbuf_tensor("r2", [128, 1], F32))
        r3 = _stk.enter_context(nc.sbuf_tensor("r3", [128, 1], F32))
        exb = _stk.enter_context(nc.sbuf_tensor("exb", [128, E], F32))
        pb = _stk.enter_context(nc.sbuf_tensor("pb", [128, E], F32))
        msk = _stk.enter_context(nc.sbuf_tensor("msk", [128, E], F32))
        cmb_tm = _stk.enter_context(nc.sbuf_tensor("cmb_tm", [128, E], F32))
        cmb_sb = _stk.enter_context(nc.sbuf_tensor("cmb_sb", [E, TB], F32))
        cmb_rep = _stk.enter_context(nc.sbuf_tensor("cmb_rep", [128, TB], F32))
        rh2 = _stk.enter_context(nc.sbuf_tensor("rh2", [128, TB], F32))
        h2_hi = _stk.enter_context(nc.sbuf_tensor("h2_hi", [128, KH * TB], BF16))
        h2_lo = _stk.enter_context(nc.sbuf_tensor("h2_lo", [128, KH * TB], BF16))
        g_sb = _stk.enter_context(nc.sbuf_tensor("g_sb", [128, DC * TB], F32))
        msf = _stk.enter_context(nc.sbuf_tensor("msf", [2, TB], F32))
        m_rep = _stk.enter_context(nc.sbuf_tensor("m_rep", [128, TB], F32))
        r_rep = _stk.enter_context(nc.sbuf_tensor("r_rep", [128, TB], F32))
        cor_t = _stk.enter_context(nc.sbuf_tensor("cor_t", [128, TB], F32))
        ws_b = _stk.enter_context(nc.sbuf_tensor("ws_b", [128, DC], F32))
        ws_d = _stk.enter_context(nc.sbuf_tensor("ws_d", [128, 1], F32))
        ws_a = _stk.enter_context(nc.sbuf_tensor("ws_a", [128, 1], F32))
        b_rep = _stk.enter_context(nc.sbuf_tensor("b_rep", [128, N * TB], F32))
        c_rep = _stk.enter_context(nc.sbuf_tensor("c_rep", [128, N * TB], F32))
        dt_dc = _stk.enter_context(nc.sbuf_tensor("dt_dc", [128, TB], F32))
        z_dc = _stk.enter_context(nc.sbuf_tensor("z_dc", [128, TB], F32))
        xnsl = _stk.enter_context(nc.sbuf_tensor("xnsl", [128, TB], F32))
        da = _stk.enter_context(nc.sbuf_tensor("da", [128, N * TB1], F32))
        dbx = _stk.enter_context(nc.sbuf_tensor("dbx", [128, N * TB1], F32))
        state = _stk.enter_context(nc.sbuf_tensor("state", [128, DC * N], F32))
        p_dtin = _stk.enter_context(nc.psum_tensor("p_dtin", [128, 512], F32))
        p_aux = _stk.enter_context(nc.psum_tensor("p_aux", [128, 512], F32))
        p_ffn0 = _stk.enter_context(nc.psum_tensor("p_ffn0", [128, 512], F32))
        p_ffn1 = _stk.enter_context(nc.psum_tensor("p_ffn1", [128, 512], F32))
        p_dt0 = _stk.enter_context(nc.psum_tensor("p_dt0", [128, 512], F32))
        p_dt1 = _stk.enter_context(nc.psum_tensor("p_dt1", [128, 512], F32))
        p_tr = _stk.enter_context(nc.psum_tensor("p_tr", [128, 512], F32))
        p_tr2 = _stk.enter_context(nc.psum_tensor("p_tr2", [128, 512], F32))
        h_ld = _stk.enter_context(nc.semaphore("s_ld"))
        h_tp = _stk.enter_context(nc.semaphore("s_tp"))
        h_rp = _stk.enter_context(nc.semaphore("s_rp"))
        h_st = _stk.enter_context(nc.semaphore("s_st"))
        h_pe = _stk.enter_context(nc.semaphore("s_pe"))
        h_ve = _stk.enter_context(nc.semaphore("s_ve"))
        h_ac = _stk.enter_context(nc.semaphore("s_ac"))
        h_gp = _stk.enter_context(nc.semaphore("s_gp"))
        h_ws = _stk.enter_context(nc.semaphore("s_ws"))
        h_x = _stk.enter_context(nc.semaphore("s_x"))
        h_w0 = _stk.enter_context(nc.semaphore("s_w0"))
        h_w1 = _stk.enter_context(nc.semaphore("s_w1"))
        h_rb = _stk.enter_context(nc.semaphore("s_rb"))
        h_rc = _stk.enter_context(nc.semaphore("s_rc"))
        h_rm = _stk.enter_context(nc.semaphore("s_rm"))
        h_rcb = _stk.enter_context(nc.semaphore("s_rcb"))
        B = _stk.enter_context(nc.Block())
        ld = ve = ac = pe = gp = tp = rp = st = ws = 0
        xc = w0 = w1 = rb = rc = rm = rcb = 0
        h_w = [h_w0, h_w1]
        p_ffn = [p_ffn0, p_ffn1]
        p_dt = [p_dt0, p_dt1]

        @B.sync
        def _(e):
            e.dma_start(out=dtw_sb[:, :], in_=dtw_h.ap()[:, :]).then_inc(h_ld, 16)
            for (sb, hb) in ((dtb_sb, dtb_h), (dvec_sb, dvec_h), (bb_sb, bb_h)):
                e.dma_start(out=bass.AP(sb, 0, [[DC, 128], [1, DC]]),
                            in_=bass.AP(hb, 0, [[1, 128], [128, DC]]),
                            allow_slow_non_contiguous=True
                            ).then_inc(h_ld, 16)
            e.dma_start(out=bass.AP(a_sb, 0, [[DC * N, 128], [N, DC], [1, N]]),
                        in_=bass.AP(alog_h, 0, [[N, 128], [128 * N, DC], [1, N]])
                        ).then_inc(h_ld, 16)
            e.dma_start(out=id_sb[:, :], in_=id_h.ap()[:, :]).then_inc(h_ld, 16)
            e.dma_start(out=cst_sb[:, :], in_=cst_h.ap()[:, :]).then_inc(h_ld, 16)
        ld += 7 * 16
        nc.const_aps.register(F32, eps, cst_sb[:, 0:1])
        nc.const_aps.register(F32, 1.0, cst_sb[:, 1:2])
        ld_consts = ld

        @B.scalar
        def _(e, w=ld):
            e.wait_ge(h_ld, w)
            e.activation(out=a_sb[:, :], in_=a_sb[:, :], func=AF.Exp
                         ).then_inc(h_ac, 1)
        ac += 1

        @B.vector
        def _(e, w=ac):
            e.wait_ge(h_ac, w)
            e.tensor_scalar(a_sb[:, :], a_sb[:, :], -1.0, None, op0=OP.mult)
            e.memset(state[:, :], 0.0)
            e.drain()
            e.nop().then_inc(h_ve, 1)
        ve += 1
        a_ready = ve

        prev_sinks = []
        prev_lo = 0

        o_tm = cmb_rep
        CW = DSL // 2
        wsplit, wsp_hi, wsp_lo = xnh, hih, loh

        ws_started = [False]

        def split_stream(src_h, n_tiles, width, sink, ws_map=None,
                         ws_last=False):
            nonlocal ld, ve, ac, pe, prev_sinks, prev_lo
            ncb = (width + CW - 1) // CW
            for kt in range(n_tiles):
                for cb in range(ncb):
                    cw = min(CW, width - cb * CW)

                    @B.sync
                    def _(e, kt=kt, cb=cb, cw=cw, g=list(prev_sinks), pl2=ld):
                        e.wait_ge(h_ld, pl2)
                        for (sh, vv) in g:
                            e.wait_ge(sh, vv)
                        e.dma_start(out=wsplit[:, :cw],
                                    in_=src_h.ap()[kt * 128:(kt + 1) * 128,
                                                   cb * CW:cb * CW + cw]
                                    ).then_inc(h_ld, 16)
                    ld += 16

                    @B.scalar
                    def _(e, w=ld, pl=prev_lo, cw=cw):
                        e.wait_ge(h_ld, w)
                        e.wait_ge(h_ve, pl)
                        e.drain()
                        e.copy(out=wsp_hi[:, :cw], in_=wsplit[:, :cw]
                               ).then_inc(h_ac, 1)
                    ac += 1
                    hi_mark = ac

                    @B.vector
                    def _(e, w=ld, hm=hi_mark, cw=cw):
                        e.wait_ge(h_ld, w)
                        e.wait_ge(h_ac, hm)
                        e.drain()
                        e.tensor_tensor(wsp_lo[:, :cw], wsplit[:, :cw],
                                        wsp_hi[:, :cw], op=OP.subtract
                                        ).then_inc(h_ve, 1)
                    ve += 1
                    lo_mark = ve
                    prev_sinks = sink(kt, cb, cw, hi_mark, lo_mark)
                    if ws_map is not None:
                        nsub = (cw + 127) // 128

                        @B.tensor
                        def _(e, kt=kt, cb=cb, cw=cw, nsub=nsub, w=ld,
                              ar=a_ready):
                            e.wait_ge(h_ld, w)
                            e.wait_ge(h_ve, ar)
                            for sub in range(nsub):
                                sw = 128
                                col = ws_map(cb, sub)
                                st_ = not ws_started[0]
                                ws_started[0] = True
                                sp_ = (ws_last and kt == n_tiles - 1
                                       and sub == nsub - 1)
                                mm = e.matmul(p_tr2[0:sw, col:col + 1],
                                              wsplit[:, sub * 128:sub * 128 + sw],
                                              cst_sb[:, 1:2],
                                              start=st_, stop=sp_)
                                if sub == nsub - 1:
                                    mm.then_inc(h_pe, 1)
                        pe += 1
                        prev_sinks = list(prev_sinks) + [(h_pe, pe)]
                    prev_lo = lo_mark

        def scr_sink(qoff):
            def sink(kt, cb, cw, hm, lm):
                nonlocal ws
                marks = []
                ndg = cw // 128
                for (sb, scr, sh, sv) in ((wsp_hi, wscr_hi, h_ac, hm),
                                          (wsp_lo, wscr_lo, h_ve, lm)):
                    @B.sync
                    def _(e, sb=sb, scr=scr, sh=sh, sv=sv, kt=kt, cb=cb,
                          ndg=ndg, pw=ws):
                        e.wait_ge(h_ws, pw)
                        e.wait_ge(sh, sv)
                        e.dma_start(
                            out=bass.AP(scr, (qoff + kt) * 128
                                        + cb * (CW // 128) * 128 * Q,
                                        [[Q, 128], [128 * Q, ndg], [1, 128]]),
                            in_=bass.AP(sb, 0, [[HDM, 128], [128, ndg], [1, 128]])
                        ).then_inc(h_ws, 16)
                    ws += 16
                    marks.append((h_ws, ws))
                return marks
            return sink

        def gscr_sink(scr_hi, scr_lo, W):
            def sink(kt, cb, cw, hm, lm):
                nonlocal ws
                marks = []
                g, s = kt // 4, kt % 4
                for (sb, scr, sh, sv) in ((wsp_hi, scr_hi, h_ac, hm),
                                          (wsp_lo, scr_lo, h_ve, lm)):
                    @B.sync
                    def _(e, sb=sb, scr=scr, sh=sh, sv=sv, g=g, s=s, pw=ws):
                        e.wait_ge(h_ws, pw)
                        e.wait_ge(sh, sv)
                        e.dma_start(
                            out=bass.AP(scr, g * 128 * 4 * W + s * W,
                                        [[4 * W, 128], [1, W]]),
                            in_=bass.AP(sb, 0, [[HDM, 128], [1, W]])
                        ).then_inc(h_ws, 16)
                    ws += 16
                    marks.append((h_ws, ws))
                return marks
            return sink

        def resident_sink(hi_t, lo_t, width):
            def sink(kt, cb, cw, hm, lm):
                nonlocal ve, ac
                marks = []
                off = kt * width + cb * CW

                @B.scalar
                def _(e, off=off, cw=cw):
                    e.drain()
                    e.copy(out=hi_t[:, off:off + cw],
                           in_=wsp_hi[:, :cw]).then_inc(h_ac, 1)
                ac += 1
                marks.append((h_ac, ac))

                @B.vector
                def _(e, off=off, cw=cw):
                    e.drain()
                    e.tensor_copy(lo_t[:, off:off + cw],
                                  wsp_lo[:, :cw]).then_inc(h_ve, 1)
                ve += 1
                marks.append((h_ve, ve))
                return marks
            return sink

        split_stream(bw_h, KCH, DSL, scr_sink(0),
                     ws_map=lambda cb, sub: cb * (CW // 128) + sub)
        split_stream(bm_h, KH, DSL, scr_sink(KCH))
        split_stream(xpd_h, KCH, RK, gscr_sink(xscr_hi, xscr_lo, RK),
                     ws_map=lambda cb, sub: DC)
        split_stream(aux_h, KCH, AUX, gscr_sink(ascr_hi, ascr_lo, AUX),
                     ws_map=lambda cb, sub: DC + 1, ws_last=True)

        @B.vector
        def _(e, w=pe):
            e.wait_ge(h_pe, w)
            e.drain()
            e.tensor_copy(ws_b[:, :], p_tr2[0:128, 0:DC])
            e.tensor_copy(ws_d[:, :], p_tr2[0:128, DC:DC + 1])
            e.tensor_copy(ws_a[:, :], p_tr2[0:128, DC + 1:DC + 2])
            e.drain()
            e.nop().then_inc(h_ve, 1)
        ve += 1
        ws_evac = ve

        @B.vector
        def _(e):
            e.drain()

        @B.scalar
        def _(e):
            e.drain()

        mk_xn_free_pe = 0
        prologue_ws = ws
        mk_wbuf_free = [0, 0]
        mk_ffn_evac = [0, 0]
        mk_dt_exp = [0, 0]
        mk_dtin_evac = 0
        mk_xnh_free = ve
        mk_half_free = 0
        mk_prev_hi = ac
        mk_rstd_prev = 0
        mk_mpak_pe = 0
        mk_msf_free = 0
        mk_brep_free = 0
        mk_crep_free = 0
        mk_rh2_free = 0
        mk_cmbsb_free = 0
        mk_aux_pe_free = 0
        mk_aux_rp_free = 0
        mk_aux_rb_free = 0
        mk_gsb_free = 0
        mk_mrep_free = 0
        mk_h2_free = 0
        mk_h2f_free = 0
        mk_odc_pe = 0
        mk_ptr2_evac = ws_evac
        mk_otm_st = 0
        mk_ptr_evac = 0
        mk_lt_free = 0
        mk_scan_prev = 0
        mk_hc_prev = 0
        mk_tree_prev = 0
        mk_z_free = 0
        mk_dt_z = 0
        mk_da_free = 0
        mk_aux_evac = 0
        dt_prev_ev = 0
        h2hi_prev = 0
        mk_sig_free = 0

        out_q = []
        mk_xnA_free = [0, 0]
        mk_xnA_scan = [0, 0]
        mk_xnB_free = 0
        mk_gout = [0] * DC
        mk_xabuf_free = 0
        dtev = [0] * DC
        dtsp = [0] * DC
        da_mark = [0] * DC
        z_mark = [0] * DC
        NG = KCH // 4
        HALF = KCH // 2

        def xn_hi_sl(kch, par):
            if kch < HALF:
                return xnA_hi[:, (par * HALF + kch) * TB:(par * HALF + kch + 1) * TB]
            return xnB_hi[:, (kch - HALF) * TB:(kch - HALF + 1) * TB]

        def xn_lo_sl(kch, par):
            if kch < HALF:
                return xnA_lo[:, (par * HALF + kch) * TB:(par * HALF + kch + 1) * TB]
            return xnB_lo[:, (kch - HALF) * TB:(kch - HALF + 1) * TB]

        def emit_ln(ci, half):
            nonlocal xc, ve, ac, pe, tp
            nonlocal mk_xnh_free, mk_prev_hi, mk_rstd_prev, mk_mpak_pe
            nonlocal mk_half_free, mk_ptr_evac
            t0 = ci * TB
            par = ci % 2
            c = ci
            qr = range(0, NLH // 2) if half == 0 else range(NLH // 2, NLH)
            msf_marks = []
            for tt in range(NTT):
                for q in qr:
                    @B.sync
                    def _(e, tt=tt, q=q, w=mk_xnh_free, wh=mk_prev_hi,
                          px=xc, c_=c, pws=prologue_ws):
                        e.wait_ge(h_x, px)
                        e.wait_ge(h_ve, w)
                        e.wait_ge(h_ac, wh)
                        if c_ == 0 and tt == 0 and q == 0:
                            e.wait_ge(h_ws, pws)
                        e.dma_start(
                            out=xnh[:, :],
                            in_=x_h.ap()[t0 + tt * 128:t0 + tt * 128 + 128,
                                         q * HDM:(q + 1) * HDM]
                            ).then_inc(h_x, 16)
                    xc += 16
                    xld = xc

                    @B.vector
                    def _(e, tt=tt, q=q, w=xld, pr=mk_rstd_prev, pm=mk_mpak_pe):
                        e.wait_ge(h_x, w)
                        if q == 0:
                            e.wait_ge(h_ac, pr)
                            e.wait_ge(h_pe, pm)
                        e.drain()
                        for k in range(NKQ):
                            e.bn_stats(
                                bnbuf[:, 6 * (tt * NB + q * NKQ + k):
                                      6 * (tt * NB + q * NKQ + k) + 6],
                                xnh[:, SBW * k:SBW * (k + 1)])
                        e.drain()
                        e.nop().then_inc(h_ve, 1)
                    ve += 1

                    @B.scalar
                    def _(e, q=q, w=xld, fr=mk_half_free):
                        e.wait_ge(h_x, w)
                        e.wait_ge(h_tp, fr)
                        e.drain()
                        e.copy(out=hih[:, :], in_=xnh[:, :]).then_inc(h_ac, 1)
                    ac += 1
                    hi_done = ac
                    mk_prev_hi = hi_done

                    @B.vector
                    def _(e, w=hi_done, fr=mk_half_free):
                        e.wait_ge(h_ac, w)
                        e.wait_ge(h_tp, fr)
                        e.drain()
                        e.tensor_tensor(loh[:, :], xnh[:, :], hih[:, :],
                                        op=OP.subtract).then_inc(h_ve, 1)
                    ve += 1
                    lo_done = ve
                    mk_xnh_free = ve

                    to_A = q < NLH // 2
                    if to_A:
                        dsts = (xnA_hi, xnA_lo)
                        doff = (par * HALF + q * HKC) * TB + tt * 128
                        dpitch = KCH * TB
                        pgate = mk_xnA_free[par]
                        vgate = mk_xnA_scan[par]
                    else:
                        dsts = (xnB_hi, xnB_lo)
                        doff = ((q - NLH // 2) * HKC) * TB + tt * 128
                        dpitch = HALF * TB
                        pgate = mk_xnB_free
                        vgate = 0

                    @B.sync
                    def _(e, tt=tt, q=q, hd=hi_done, ld_=lo_done, pt=tp,
                          dsts=dsts, doff=doff, dpi=dpitch, pg=pgate,
                          vg=vgate):
                        e.wait_ge(h_tp, pt)
                        e.wait_ge(h_ac, hd)
                        e.wait_ge(h_ve, ld_)
                        e.wait_ge(h_pe, pg)
                        if vg:
                            e.wait_ge(h_ve, vg)
                        e.dma_start_transpose(
                            out=bass.AP(dsts[0], doff,
                                        [[dpi, 128], [TB, HKC], [1, 128]]),
                            in_=hih[:, :]).then_inc(h_tp, 16)
                        e.dma_start_transpose(
                            out=bass.AP(dsts[1], doff,
                                        [[dpi, 128], [TB, HKC], [1, 128]]),
                            in_=loh[:, :]).then_inc(h_tp, 16)
                    tp += 32
                    mk_half_free = tp

                if half == 1:
                    @B.vector
                    def _(e, tt=tt, pm=mk_mpak_pe):
                        e.wait_ge(h_pe, pm)
                        e.drain()
                        e.bn_aggr(mv[:, :],
                                  bnbuf[:, 6 * tt * NB:6 * (tt + 1) * NB])
                        e.drain()
                        e.tensor_copy(mpak[:, 0:1], mv[:, 0:1])
                        e.drain()
                        e.nop().then_inc(h_ve, 1)
                    ve += 1
                    bn_done = ve

                    @B.scalar
                    def _(e, tt=tt, w=bn_done, sg=mk_sig_free):
                        e.wait_ge(h_ve, w)
                        e.wait_ge(h_ve, sg)
                        e.wait_ge(h_ld, ld_consts)
                        e.drain()
                        e.activation(out=lnv[:, :], in_=mv[:, 1:2], func=AF.Ln,
                                     bias=eps)
                        e.drain()
                        e.activation(out=mpak[:, 1:2], in_=lnv[:, :], func=AF.Exp,
                                     scale=-0.5)
                        e.activation(out=sig_sv[:, tt:tt + 1], in_=lnv[:, :],
                                     func=AF.Exp, scale=0.5)
                        e.drain()
                        e.nop().then_inc(h_ac, 1)
                    ac += 1
                    rstd_done = ac
                    mk_rstd_prev = rstd_done

                    @B.tensor
                    def _(e, tt=tt, w=rstd_done, pf=mk_ptr_evac):
                        e.wait_ge(h_ac, w)
                        e.wait_ge(h_ve, pf)
                        e.transpose(p_tr[0:2, 0:128], mpak[:, :], id_sb[:, :]
                                    ).then_inc(h_pe, 1)
                    pe += 1
                    mk_mpak_pe = pe

                    @B.vector
                    def _(e, tt=tt, w=pe, mf=mk_msf_free):
                        e.wait_ge(h_pe, w)
                        e.wait_ge(h_rm, mf)
                        e.drain()
                        e.tensor_copy(msf[:, tt * 128:(tt + 1) * 128],
                                      p_tr[0:2, 0:128]).then_inc(h_ve, 1)
                    ve += 1
                    mk_ptr_evac = ve
                    msf_marks.append(ve)

            if half == 0:
                return None, None
            return tp, msf_marks[-1]

        def emit_mrep(msf_last):
            nonlocal rm, mk_msf_free
            @B.scalar
            def _(e, w=msf_last, fr=mk_mrep_free, pm2=rm):
                e.wait_ge(h_rm, pm2)
                e.wait_ge(h_ve, w)
                e.wait_ge(h_ve, fr)
                e.dma_start(out=m_rep[:, :],
                            in_=bass.AP(msf, 0, [[TB, 1], [0, 128], [1, TB]])
                            ).then_inc(h_rm, 16)
                e.dma_start(out=r_rep[:, :],
                            in_=bass.AP(msf, TB, [[TB, 1], [0, 128], [1, TB]])
                            ).then_inc(h_rm, 16)
            rm += 32
            mrep_done = rm
            mk_msf_free = rm

            return mrep_done

        emit_ln(0, 0)
        ln_tp, ln_msf = emit_ln(0, 1)
        mrep_done = emit_mrep(ln_msf)

        for c in range(NCH):
            t0 = c * TB
            par = c % 2
            tp_done = ln_tp

            for (odc, ovm) in out_q:
                for th in range(NTT):
                    @B.tensor
                    def _(e, dc=odc, th=th, w=ovm, pf=mk_ptr2_evac):
                        e.wait_ge(h_ve, max(w, pf))
                        e.transpose(p_tr2[0:128, 0:128],
                                    g_sb[:, dc * TB + th * 128:
                                         dc * TB + (th + 1) * 128],
                                    id_sb[:, :]).then_inc(h_pe, 1)
                    pe += 1
                    otr = pe

                    @B.vector
                    def _(e, dc=odc, th=th, w=otr, sf=mk_otm_st,
                          hh=h2hi_prev):
                        e.wait_ge(h_pe, w)
                        e.wait_ge(h_st, sf)
                        e.wait_ge(h_ac, hh)
                        e.drain()
                        e.tensor_copy(o_tm[:, 0:128],
                                      p_tr2[0:128, 0:128]).then_inc(h_ve, 1)
                    ve += 1
                    mk_ptr2_evac = ve
                    otm_done = ve

                    @B.scalar
                    def _(e, dc=odc, th=th, w=otm_done, ps=st, tprev=t0 - TB):
                        e.wait_ge(h_st, ps)
                        e.wait_ge(h_ve, w)
                        e.dma_start(
                            out=bass.AP(out_h,
                                        (tprev + th * 128) * DSL + dc * 128,
                                        [[DSL, 128], [1, 128]]),
                            in_=o_tm[:, 0:128]).then_inc(h_st, 16)
                    st += 16
                    mk_otm_st = st
                mk_gout[odc] = pe
            out_q = []

            xp_first = True
            for gidx in range(NG):
                @B.sync
                def _(e, gidx=gidx, pl=ld, fr=mk_xabuf_free):
                    e.wait_ge(h_ld, pl)
                    e.wait_ge(h_pe, fr)
                    e.dma_start(out=xbuf_hi[:, :], in_=xscr_hi.ap()[gidx, :, :]
                                ).then_inc(h_ld, 16)
                    e.dma_start(out=xbuf_lo[:, :], in_=xscr_lo.ap()[gidx, :, :]
                                ).then_inc(h_ld, 16)
                    e.dma_start(out=abuf_hi[:, :], in_=ascr_hi.ap()[gidx, :, :]
                                ).then_inc(h_ld, 16)
                    e.dma_start(out=abuf_lo[:, :], in_=ascr_lo.ap()[gidx, :, :]
                                ).then_inc(h_ld, 16)
                ld += 64
                gld = ld

                @B.tensor
                def _(e, gidx=gidx, w=gld, tp_=tp_done, dtf=mk_dtin_evac,
                      axf=mk_aux_evac, par=par):
                    e.wait_ge(h_ld, w)
                    if gidx == 0:
                        e.wait_ge(h_tp, tp_)
                        e.wait_ge(h_ve, max(dtf, axf))
                    for i in range(4):
                        kch = gidx * 4 + i
                        first = kch == 0
                        last = kch == KCH - 1
                        rh_ = xn_hi_sl(kch, par)
                        rl_ = xn_lo_sl(kch, par)
                        wh = xbuf_hi[:, i * RK:(i + 1) * RK]
                        wl = xbuf_lo[:, i * RK:(i + 1) * RK]
                        e.matmul(p_dtin[0:RK, 0:TB], wh, rh_, start=first,
                                 stop=False)
                        e.matmul(p_dtin[0:RK, 0:TB], wh, rl_, start=False,
                                 stop=False)
                        e.matmul(p_dtin[0:RK, 0:TB], wl, rh_, start=False,
                                 stop=last)
                        wh = abuf_hi[:, i * AUX:(i + 1) * AUX]
                        wl = abuf_lo[:, i * AUX:(i + 1) * AUX]
                        e.matmul(p_aux[0:AUX, 0:TB], wh, rh_, start=first,
                                 stop=False)
                        e.matmul(p_aux[0:AUX, 0:TB], wh, rl_, start=False,
                                 stop=False)
                        mm = e.matmul(p_aux[0:AUX, 0:TB], wl, rh_, start=False,
                                      stop=last)
                    mm.then_inc(h_pe, 1)
                pe += 1
                mk_xabuf_free = pe
            xp1_done = pe
            aux_done = pe

            @B.vector
            def _(e, w=xp1_done, mr=mrep_done):
                e.wait_ge(h_pe, w)
                e.wait_ge(h_rm, mr)
                e.drain()
                e.tensor_scalar(cor_t[:, :], m_rep[:, :], ws_d[:, 0:1], None,
                                op0=OP.mult)
                e.drain()
                e.tensor_tensor(dtin_sb[:, :], p_dtin[0:RK, 0:TB],
                                cor_t[:, :], op=OP.subtract)
                e.drain()
                e.tensor_tensor(dtin_sb[:, :], dtin_sb[:, :], r_rep[:, :],
                                op=OP.mult).then_inc(h_ve, 1)
            ve += 1
            mk_dtin_evac = ve

            @B.vector
            def _(e, w=aux_done, f1=mk_aux_pe_free, f2=mk_aux_rp_free,
                  f3=mk_aux_rb_free):
                e.wait_ge(h_pe, w)
                e.wait_ge(h_pe, f1)
                e.wait_ge(h_rc, f2)
                e.wait_ge(h_rb, f3)
                e.wait_ge(h_rm, mrep_done)
                e.drain()
                e.tensor_scalar(cor_t[0:AUX, :], m_rep[0:AUX, :],
                                ws_a[0:AUX, 0:1], None, op0=OP.mult)
                e.drain()
                e.tensor_tensor(aux_sb[:, :], p_aux[0:AUX, 0:TB],
                                cor_t[0:AUX, :], op=OP.subtract)
                e.drain()
                e.tensor_tensor(aux_sb[:, :], aux_sb[:, :], r_rep[0:AUX, :],
                                op=OP.mult).then_inc(h_ve, 1)
            ve += 1
            mk_aux_evac = ve

            for dc in range(DC):
                dpar = dc % 2

                @B.tensor
                def _(e, dc=dc, dpar=dpar, w=mk_dtin_evac,
                      fe=dtev[dc - 2] if dc >= 2 else dt_prev_ev):
                    e.wait_ge(h_ve, max(w, fe))
                    e.matmul(p_dt[dpar][0:128, 0:TB],
                             dtw_sb[:, dc * 128:(dc + 1) * 128],
                             dtin_sb[:, :], start=True, stop=True
                             ).then_inc(h_pe, 1)
                pe += 1
                dtmm = pe

                @B.vector
                def _(e, dc=dc, dpar=dpar, w=dtmm, fa=da_mark[dc],
                      fz=z_mark[dc]):
                    e.wait_ge(h_pe, w)
                    e.wait_ge(h_ac, fa)
                    e.wait_ge(h_ve, fz)
                    e.drain()
                    e.tensor_copy(dtraw[:, dc * TB:(dc + 1) * TB],
                                  p_dt[dpar][0:128, 0:TB]).then_inc(h_ve, 1)
                ve += 1
                dtev[dc] = ve

                @B.scalar
                def _(e, dc=dc, w=dtev[dc]):
                    e.wait_ge(h_ve, w)
                    e.drain()
                    e.activation(out=dtraw[:, dc * TB:(dc + 1) * TB],
                                 in_=dtraw[:, dc * TB:(dc + 1) * TB],
                                 func=AF.Exp, bias=dtb_sb[:, dc:dc + 1])
                    e.drain()
                    e.activation(out=dtraw[:, dc * TB:(dc + 1) * TB],
                                 in_=dtraw[:, dc * TB:(dc + 1) * TB],
                                 func=AF.Ln, bias=1.0).then_inc(h_ac, 1)
                ac += 1
                dtsp[dc] = ac
            dt_prev_ev = dtev[DC - 2]

            @B.sync
            def _(e, w=mk_aux_evac, fb=mk_brep_free, fr_=mk_rh2_free, pb_=rb):
                e.wait_ge(h_rb, pb_)
                e.wait_ge(h_ve, w)
                e.wait_ge(h_gp, fb)
                e.wait_ge(h_ve, fr_)
                for n in range(N):
                    e.dma_start(out=b_rep[:, n * TB:(n + 1) * TB],
                                in_=bass.AP(aux_sb, (E + n) * TB,
                                            [[TB, 1], [0, 128], [1, TB]])
                                ).then_inc(h_rb, 16)
                for half in range(2):
                    e.dma_start(out=rh2[half * R:(half + 1) * R, :],
                                in_=bass.AP(aux_sb, (E + 2 * N) * TB,
                                            [[TB, R], [1, TB]])
                                ).then_inc(h_rb, 16)
            rb += 16 * (N + 2)
            rh2_done = rb

            @B.scalar
            def _(e, w=mk_aux_evac, fc=mk_crep_free, pc=rc):
                e.wait_ge(h_rc, pc)
                e.wait_ge(h_ve, max(w, fc))
                for n in range(N):
                    e.dma_start(out=c_rep[:, n * TB:(n + 1) * TB],
                                in_=bass.AP(aux_sb, (E + N + n) * TB,
                                            [[TB, 1], [0, 128], [1, TB]])
                                ).then_inc(h_rc, 16)
            rc += 16 * N
            crep_done = rc
            brep_done = rh2_done
            mk_aux_rp_free = rc
            mk_aux_rb_free = rb

            cmb_done_mark = 0
            for tt in range(NTT):
                @B.tensor
                def _(e, tt=tt, w=mk_aux_evac, pf=mk_ptr_evac):
                    e.wait_ge(h_ve, max(w, pf))
                    e.transpose(p_tr[0:128, 0:E],
                                aux_sb[0:E, tt * 128:(tt + 1) * 128],
                                id_sb[0:E, 0:E]).then_inc(h_pe, 1)
                pe += 1
                ltr = pe
                mk_aux_pe_free = pe

                @B.vector
                def _(e, w=ltr, lf=mk_lt_free):
                    e.wait_ge(h_pe, w)
                    e.wait_ge(h_ac, lf)
                    e.drain()
                    e.tensor_copy(lt[:, :], p_tr[0:128, 0:E]).then_inc(h_ve, 1)
                ve += 1
                mk_ptr_evac = ve

                @B.vector
                def _(e):
                    e.drain()
                    e.tensor_reduce(r1[:, :], lt[:, :], axis=AX.X, op=OP.max,
                                    negate=True)
                    e.drain()
                    e.nop().then_inc(h_ve, 1)
                ve += 1
                nmx_done = ve

                @B.scalar
                def _(e, w=nmx_done):
                    e.wait_ge(h_ve, w)
                    e.drain()
                    e.activation(out=exb[:, :], in_=lt[:, :], func=AF.Exp,
                                 bias=r1[:, 0:1]).then_inc(h_ac, 1)
                ac += 1
                exp_done = ac
                mk_lt_free = ac

                @B.vector
                def _(e, tt=tt, w=exp_done):
                    e.wait_ge(h_ac, w)
                    e.drain()
                    e.tensor_reduce(r2[:, :], exb[:, :], axis=AX.X, op=OP.add)
                    e.drain()
                    e.reciprocal(r2[:, :], r2[:, :])
                    e.drain()
                    e.tensor_scalar(pb[:, :], exb[:, :], r2[:, 0:1], None,
                                    op0=OP.mult)
                    e.drain()
                    e.tensor_reduce(r1[:, :], pb[:, :], axis=AX.X, op=OP.max)
                    e.drain()
                    e.tensor_scalar(msk[:, :], pb[:, :], r1[:, 0:1], None,
                                    op0=OP.is_ge)
                    e.drain()
                    e.scalar_tensor_tensor(exb[:, :], msk[:, :], -2.0, pb[:, :],
                                           op0=OP.mult, op1=OP.add)
                    e.drain()
                    e.tensor_reduce(r2[:, :], exb[:, :], axis=AX.X, op=OP.max)
                    e.drain()
                    e.tensor_scalar(msk[:, :], pb[:, :], r2[:, 0:1], None,
                                    op0=OP.is_ge)
                    e.tensor_tensor(r3[:, :], r1[:, :], r2[:, :], op=OP.add)
                    e.drain()
                    e.tensor_scalar(r3[:, :], r3[:, :], 2.0, 2e-6,
                                    op0=OP.mult, op1=OP.add)
                    e.drain()
                    e.reciprocal(r3[:, :], r3[:, :])
                    e.drain()
                    e.tensor_tensor(cmb_tm[:, :], pb[:, :], msk[:, :], op=OP.mult)
                    e.drain()
                    e.tensor_scalar(cmb_tm[:, :], cmb_tm[:, :], r3[:, 0:1],
                                    sig_sv[:, tt:tt + 1],
                                    op0=OP.mult, op1=OP.mult)
                    e.drain()
                    e.nop().then_inc(h_ve, 1)
                ve += 1
                cmbv = ve

                @B.tensor
                def _(e, w=cmbv, pf=mk_ptr_evac):
                    e.wait_ge(h_ve, max(w, pf))
                    e.transpose(p_tr[0:E, 0:128], cmb_tm[:, :], id_sb[:, :]
                                ).then_inc(h_pe, 1)
                pe += 1
                ctr = pe

                @B.vector
                def _(e, tt=tt, w=ctr, cf=mk_cmbsb_free):
                    e.wait_ge(h_pe, w)
                    e.wait_ge(h_rcb, cf)
                    e.drain()
                    e.tensor_copy(cmb_sb[:, tt * 128:(tt + 1) * 128],
                                  p_tr[0:E, 0:128]).then_inc(h_ve, 1)
                ve += 1
                mk_ptr_evac = ve
                cmb_done_mark = ve

            h2hi_last = h2lo_last = 0
            crp_last = 0
            for eg in range(KH):
                @B.scalar
                def _(e, eg=eg, w=cmb_done_mark, fr=mk_h2f_free, pcb=rcb,
                      so=mk_otm_st):
                    e.wait_ge(h_rcb, pcb)
                    e.wait_ge(h_st, so)
                    e.wait_ge(h_ve, max(w, fr))
                    e.dma_start(out=cmb_rep[:, :],
                                in_=bass.AP(cmb_sb, (2 * eg) * TB,
                                            [[TB, 2], [0, R], [1, TB]])
                                ).then_inc(h_rcb, 16)
                rcb += 16
                crp = rcb
                crp_last = rcb

                @B.vector
                def _(e, eg=eg, w=crp, r=rh2_done, hf=mk_h2f_free):
                    e.wait_ge(h_rcb, w)
                    e.wait_ge(h_rb, r)
                    e.wait_ge(h_ve, hf)
                    e.drain()
                    e.tensor_tensor(cmb_rep[:, :], cmb_rep[:, :], rh2[:, :],
                                    op=OP.mult).then_inc(h_ve, 1)
                ve += 1
                h2f_done = ve
                if eg == KH - 1:
                    mk_rh2_free = ve

                @B.scalar
                def _(e, eg=eg, w=h2f_done, pf=mk_h2_free):
                    e.wait_ge(h_ve, w)
                    e.wait_ge(h_pe, pf)
                    e.drain()
                    e.copy(out=h2_hi[:, eg * TB:(eg + 1) * TB],
                           in_=cmb_rep[:, :]).then_inc(h_ac, 1)
                ac += 1
                h2hi_last = ac

                @B.vector
                def _(e, eg=eg, w=h2hi_last, pf=mk_h2_free):
                    e.wait_ge(h_ac, w)
                    e.wait_ge(h_pe, pf)
                    e.drain()
                    e.tensor_tensor(h2_lo[:, eg * TB:(eg + 1) * TB],
                                    cmb_rep[:, :],
                                    h2_hi[:, eg * TB:(eg + 1) * TB],
                                    op=OP.subtract).then_inc(h_ve, 1)
                ve += 1
                h2lo_last = ve
                mk_h2f_free = ve
            mk_cmbsb_free = crp_last
            h2hi_prev = h2hi_last

            if c + 1 < NCH:
                emit_ln(c + 1, 0)

            g_marks = []
            for dg in range(DC):
                fpar = dg % 2
                wcnt = [w0, w1]
                for (scr, wb) in ((wscr_hi, wbuf_hi), (wscr_lo, wbuf_lo)):
                    @B.sync
                    def _(e, scr=scr, wb=wb, dg=dg, wpar=fpar,
                          fr=mk_wbuf_free[fpar], pw=wcnt[fpar], c_=c, wsv=ws):
                        e.wait_ge(h_w[wpar], pw)
                        if c_ == 0:
                            e.wait_ge(h_ws, wsv)
                        e.wait_ge(h_pe, fr)
                        e.dma_start(
                            out=wb[:, wpar * Q:(wpar + 1) * Q],
                            in_=scr.ap()[dg, :, :]).then_inc(h_w[wpar], 16)
                    if fpar == 0:
                        w0 += 16
                    else:
                        w1 += 16
                wld = (fpar, w0 if fpar == 0 else w1)

                @B.tensor
                def _(e, dg=dg, fpar=fpar, wm=wld, par=par,
                      fe=mk_ffn_evac[fpar], hh=h2hi_last, hl=h2lo_last):
                    e.wait_ge(h_w[wm[0]], wm[1])
                    e.wait_ge(h_ac, hh)
                    e.wait_ge(h_ve, max(fe, hl))
                    pf = p_ffn[fpar]
                    wo = fpar * Q
                    for kch in range(KCH):
                        wh = wbuf_hi[:, wo + kch * 128:wo + (kch + 1) * 128]
                        wl = wbuf_lo[:, wo + kch * 128:wo + (kch + 1) * 128]
                        rh_ = xn_hi_sl(kch, par)
                        rl_ = xn_lo_sl(kch, par)
                        e.matmul(pf[0:128, 0:TB], wh, rh_, start=(kch == 0),
                                 stop=False)
                        e.matmul(pf[0:128, 0:TB], wh, rl_, start=False,
                                 stop=False)
                        e.matmul(pf[0:128, 0:TB], wl, rh_, start=False,
                                 stop=False)
                    for kh in range(KH):
                        wh = wbuf_hi[:, wo + (KCH + kh) * 128:wo + (KCH + kh + 1) * 128]
                        wl = wbuf_lo[:, wo + (KCH + kh) * 128:wo + (KCH + kh + 1) * 128]
                        rh_ = h2_hi[:, kh * TB:(kh + 1) * TB]
                        rl_ = h2_lo[:, kh * TB:(kh + 1) * TB]
                        e.matmul(pf[0:128, 0:TB], wh, rh_, start=False,
                                 stop=False)
                        e.matmul(pf[0:128, 0:TB], wh, rl_, start=False,
                                 stop=False)
                        mm = e.matmul(pf[0:128, 0:TB], wl, rh_, start=False,
                                      stop=(kh == KH - 1))
                        if kh == KH - 1:
                            mm.then_inc(h_pe, 1)
                pe += 1
                ffn_done = pe
                mk_wbuf_free[fpar] = ffn_done
                if dg == DC - 1:
                    mk_xnB_free = ffn_done
                    mk_xnA_free[par] = ffn_done
                    mk_h2_free = ffn_done

                @B.vector
                def _(e, dg=dg, fpar=fpar, w=ffn_done, gf=mk_gout[dg],
                      mr=mrep_done):
                    e.wait_ge(h_pe, w)
                    e.wait_ge(h_pe, gf)
                    e.wait_ge(h_rm, mr)
                    e.drain()
                    e.tensor_scalar(cor_t[:, :], m_rep[:, :],
                                    ws_b[:, dg:dg + 1], None, op0=OP.mult)
                    e.drain()
                    e.tensor_tensor(g_sb[:, dg * TB:(dg + 1) * TB],
                                    p_ffn[fpar][0:128, 0:TB], cor_t[:, :],
                                    op=OP.subtract)
                    e.drain()
                    e.tensor_tensor(g_sb[:, dg * TB:(dg + 1) * TB],
                                    g_sb[:, dg * TB:(dg + 1) * TB],
                                    r_rep[:, :], op=OP.mult).then_inc(h_ve, 1)
                ve += 1
                mk_ffn_evac[fpar] = ve

                @B.scalar
                def _(e, dg=dg, w=ve):
                    e.wait_ge(h_ve, w)
                    e.drain()
                    e.activation(out=g_sb[:, dg * TB:(dg + 1) * TB],
                                 in_=g_sb[:, dg * TB:(dg + 1) * TB],
                                 func=AF.Silu,
                                 bias=bb_sb[:, dg:dg + 1]).then_inc(h_ac, 1)
                ac += 1
                g_marks.append(ac)

                dc = dg
                @B.vector
                def _(e, dc=dc, w=dtsp[dc], tp_=tp_done, zf=mk_z_free,
                      op_=mk_gout[dc] if False else 0, mr=mrep_done, par=par):
                    e.wait_ge(h_ac, w)
                    e.wait_ge(h_tp, tp_)
                    e.wait_ge(h_gp, zf)
                    e.wait_ge(h_rm, mr)
                    e.drain()
                    e.tensor_tensor(cor_t[:, :], xn_hi_sl(dc, par),
                                    xn_lo_sl(dc, par), op=OP.add)
                    e.drain()
                    e.tensor_tensor(xnsl[:, :], cor_t[:, :], m_rep[:, :],
                                    op=OP.subtract)
                    e.drain()
                    e.tensor_tensor(xnsl[:, :], xnsl[:, :], r_rep[:, :],
                                    op=OP.mult)
                    e.drain()
                    e.tensor_tensor(z_dc[:, :], dtraw[:, dc * TB:(dc + 1) * TB],
                                    xnsl[:, :], op=OP.mult)
                    e.drain()
                    e.nop().then_inc(h_ve, 1)
                ve += 1
                z_done = ve
                z_mark[dc] = ve
                if dc == DC - 1:
                    mk_xnA_scan[par] = ve

                @B.vector
                def _(e, dc=dc):
                    e.drain()
                    e.memset(bass.AP(da, 0, [[N * TB1, 128], [TB1, N], [1, 1]]),
                             0.0)
                    e.drain()
                    e.nop().then_inc(h_ve, 1)
                ve += 1
                slots_done = ve

                @B.scalar
                def _(e, dc=dc, sd=slots_done, ar=a_ready):
                    e.wait_ge(h_ve, max(sd, ar))
                    e.drain()
                    for n in range(N):
                        e.activation(out=da[:, n * TB1 + 1:(n + 1) * TB1],
                                     in_=dtraw[:, dc * TB:(dc + 1) * TB],
                                     func=AF.Exp,
                                     scale=a_sb[:, dc * N + n:dc * N + n + 1])
                    e.drain()
                    e.nop().then_inc(h_ac, 1)
                ac += 1
                da_done = ac
                da_mark[dc] = ac

                @B.gpsimd
                def _(e, dc=dc, zm=z_done, bm_=brep_done, sm=mk_scan_prev,
                      tv=mk_tree_prev):
                    e.wait_ge(h_ve, max(zm, sm, tv))
                    e.wait_ge(h_rb, bm_)
                    e.drain()
                    for n in range(N):
                        e.tensor_tensor(dbx[:, n * TB1 + 1:(n + 1) * TB1],
                                        z_dc[:, :], b_rep[:, n * TB:(n + 1) * TB],
                                        op=OP.mult)
                    e.tensor_copy(
                        bass.AP(dbx, 0, [[N * TB1, 128], [TB1, N], [1, 1]]),
                        bass.AP(state, dc * N, [[DC * N, 128], [1, N], [1, 1]]))
                    e.drain()
                    e.nop().then_inc(h_gp, 1)
                gp += 1
                dbx_done = gp
                mk_z_free = gp

                @B.vector
                def _(e, dc=dc, dm=da_done, bm_=dbx_done):
                    e.wait_ge(h_ac, dm)
                    e.wait_ge(h_gp, bm_)
                    e.drain()
                    e.tensor_tensor_scan(dbx[:, :], da[:, :], dbx[:, :], 0.0,
                                         op0=OP.mult, op1=OP.add)
                    e.drain()
                    e.tensor_copy(
                        bass.AP(state, dc * N, [[DC * N, 128], [1, N], [1, 1]]),
                        bass.AP(dbx, TB, [[N * TB1, 128], [TB1, N], [1, 1]]))
                    e.drain()
                    e.nop().then_inc(h_ve, 1)
                ve += 1
                scan_done = ve
                mk_scan_prev = ve

                @B.vector
                def _(e, dc=dc, w=scan_done, cm=crep_done, gm=g_marks[dc]):
                    e.wait_ge(h_rc, cm)
                    e.wait_ge(h_ac, gm)
                    e.drain()
                    for n in range(N):
                        e.tensor_tensor(da[:, n * TB:(n + 1) * TB],
                                        dbx[:, n * TB1 + 1:(n + 1) * TB1],
                                        c_rep[:, n * TB:(n + 1) * TB],
                                        op=OP.mult)
                    e.drain()
                    e.tensor_tensor(dbx[:, 0:(N // 2) * TB],
                                    da[:, 0:(N // 2) * TB],
                                    da[:, (N // 2) * TB:N * TB], op=OP.add)
                    e.drain()
                    e.nop().then_inc(h_ve, 1)
                ve += 1
                hc_done = ve
                mk_crep_free = ve
                mk_hc_prev = ve

                @B.vector
                def _(e, dc=dc):
                    e.drain()
                    w4 = (N // 4) * TB
                    e.tensor_tensor(dbx[:, 0:w4], dbx[:, 0:w4],
                                    dbx[:, w4:2 * w4], op=OP.add)
                    e.drain()
                    w8 = (N // 8) * TB
                    e.tensor_tensor(dbx[:, 0:w8], dbx[:, 0:w8],
                                    dbx[:, w8:2 * w8], op=OP.add)
                    e.drain()
                    w16 = (N // 16) * TB
                    e.tensor_tensor(z_dc[:, 0:w16], dbx[:, 0:w16],
                                    dbx[:, w16:2 * w16], op=OP.add)
                    e.drain()
                    e.scalar_tensor_tensor(dt_dc[:, :], xnsl[:, :],
                                           dvec_sb[:, dc:dc + 1], z_dc[:, :],
                                           op0=OP.mult, op1=OP.add)
                    e.drain()
                    e.tensor_tensor(g_sb[:, dc * TB:(dc + 1) * TB],
                                    dt_dc[:, :],
                                    g_sb[:, dc * TB:(dc + 1) * TB], op=OP.mult)
                    e.drain()
                    e.tensor_tensor(g_sb[:, dc * TB:(dc + 1) * TB],
                                    g_sb[:, dc * TB:(dc + 1) * TB],
                                    cor_t[:, :], op=OP.add)
                    e.drain()
                    e.nop().then_inc(h_ve, 1)
                ve += 1
                mk_tree_prev = ve
                out_q.append((dc, ve))

            mk_mrep_free = z_mark[DC - 1]
            if c + 1 < NCH:
                mk_sig_free = cmb_done_mark
                ln_tp, ln_msf = emit_ln(c + 1, 1)
                mrep_done = emit_mrep(ln_msf)

        t0 = (NCH - 1) * TB
        for (odc, ovm) in out_q:
            for th in range(NTT):
                @B.tensor
                def _(e, dc=odc, th=th, w=ovm, pf=mk_ptr2_evac):
                    e.wait_ge(h_ve, max(w, pf))
                    e.transpose(p_tr2[0:128, 0:128],
                                g_sb[:, dc * TB + th * 128:
                                     dc * TB + (th + 1) * 128],
                                id_sb[:, :]).then_inc(h_pe, 1)
                pe += 1
                otr = pe

                @B.vector
                def _(e, dc=odc, th=th, w=otr, sf=mk_otm_st, hh=h2hi_prev):
                    e.wait_ge(h_pe, w)
                    e.wait_ge(h_st, sf)
                    e.wait_ge(h_ac, hh)
                    e.drain()
                    e.tensor_copy(o_tm[:, 0:128],
                                  p_tr2[0:128, 0:128]).then_inc(h_ve, 1)
                ve += 1
                mk_ptr2_evac = ve
                otm_done = ve

                @B.scalar
                def _(e, dc=odc, th=th, w=otm_done, ps=st):
                    e.wait_ge(h_st, ps)
                    e.wait_ge(h_ve, w)
                    e.dma_start(
                        out=bass.AP(out_h, (t0 + th * 128) * DSL + dc * 128,
                                    [[DSL, 128], [1, 128]]),
                        in_=o_tm[:, 0:128]).then_inc(h_st, 16)
                st += 16
                mk_otm_st = st

        @B.scalar
        def _(e, w=st):
            e.wait_ge(h_st, w)

        @B.sync
        def _(e, w=ws):
            e.wait_ge(h_ws, w)

    return nc



def _shard_inputs(inputs, cfg: Cfg, n_cores=8):
    x_full = np.asarray(inputs["L0_attn_x"], dtype=np.float32)
    alog = np.asarray(inputs["L0_attn_A_log"], dtype=np.float32)
    dvec = np.asarray(inputs["L0_attn_D"], dtype=np.float32)
    xpw = np.asarray(inputs["L0_attn_x_proj_w"], dtype=np.float32)
    dtw = np.asarray(inputs["L0_attn_dt_proj_w"], dtype=np.float32)
    dtb = np.asarray(inputs["L0_attn_dt_proj_b"], dtype=np.float32)
    bw = np.asarray(inputs["L0_ffn_base_w"], dtype=np.float32)
    bb = np.asarray(inputs["L0_ffn_base_b"], dtype=np.float32)
    amat = np.asarray(inputs["L0_ffn_A"], dtype=np.float32)
    bmat = np.asarray(inputs["L0_ffn_B"], dtype=np.float32)
    rw = np.asarray(inputs["L0_ffn_router_w"], dtype=np.float32)

    DM, DSL, N, RK, E, R = cfg.DM, cfg.DSL, cfg.N, cfg.RK, cfg.E, cfg.R
    ident = np.eye(128, dtype=np.float32)
    in_maps = []
    n_j = DM // DSL
    n_s = n_cores // n_j
    for c in range(n_cores):
        s = c // n_j
        j = c % n_j
        dsl = np.arange(j * DSL, (j + 1) * DSL)
        rest = np.concatenate([np.arange(0, j * DSL),
                               np.arange((j + 1) * DSL, DM)])
        perm = np.concatenate([dsl, rest])
        auxw = np.concatenate([rw, xpw[RK:RK + 2 * N, :], amat], axis=0)
        im = {
            "x": np.ascontiguousarray(x_full[s][:, perm]),
            "bw_t": np.ascontiguousarray(bw[dsl, :][:, perm].T),
            "bm_t": np.ascontiguousarray(
                bmat[:, dsl, :].transpose(0, 2, 1).reshape(E * R, DSL)),
            "xpw_dt_t": np.ascontiguousarray(xpw[:RK, :][:, perm].T),
            "auxw_t": np.ascontiguousarray(auxw[:, perm].T),
            "dtw_t": np.ascontiguousarray(dtw[dsl, :].T),
            "dtb": np.ascontiguousarray(dtb[dsl]),
            "alog": np.ascontiguousarray(alog[dsl, :]),
            "dvec": np.ascontiguousarray(dvec[dsl]),
            "bb": np.ascontiguousarray(bb[dsl]),
            "ident": ident,
            "consts": np.broadcast_to(
                np.array([1e-5, 1.0], dtype=np.float32), (128, 2)).copy(),
        }
        in_maps.append(im)
    return in_maps


def kernel(**inputs):
    cfg = Cfg()
    nc = build_program(cfg)
    in_maps = _shard_inputs(inputs, cfg, n_cores=8)
    res = run_bass_kernel_spmd(nc, in_maps, core_ids=list(range(8)))
    T, DM, DSL = cfg.T, cfg.DM, cfg.DSL
    n_j = DM // DSL
    out = np.empty((2, T, DM), dtype=np.float32)
    for c in range(8):
        s, j = c // n_j, c % n_j
        out[s][:, j * DSL:(j + 1) * DSL] = res.results[c]["out"]
    return out
